# version 08
# speedup vs baseline: 1.5183x; 1.0607x over previous
import numpy as np
import ml_dtypes
import concourse.bass as bass
import concourse.mybir as mybir
from concourse.alu_op_type import AluOpType as ALU
from concourse.bass_utils import run_bass_kernel_spmd
from contextlib import ExitStack

F32 = mybir.dt.float32
BF16 = mybir.dt.bfloat16
F16 = mybir.dt.float16
F8 = mybir.dt.float8e4
U16 = mybir.dt.uint16
AF = mybir.ActivationFunctionType
AX = mybir.AxisListType

NB = ml_dtypes.bfloat16
NF8 = ml_dtypes.float8_e4m3

S = 1024
D = 2048
H = 128
FF = 8192
EPS = 1e-5
NEG = -1.0e30


def _preload_one(nc, s_):
    try:
        from concourse.hw_specs import get_activation_tables
        need = (AF.Exp, AF.Ln, AF.Copy, AF.Identity, AF.Square, AF.Relu)
        set_id = None
        for idx, funcs in enumerate(get_activation_tables("TRN2").values()):
            if all(f in funcs for f in need):
                set_id = idx
                break
    except Exception:
        set_id = 6
    if set_id is not None:
        s_.add_instruction(mybir.InstLoadActFuncSet(
            name=nc.get_next_instruction_name(),
            act_func_set_id=set_id, ins=[], outs=[]))


def _preload_tables(nc, blk):
    try:
        from concourse.hw_specs import get_activation_tables
        need = (AF.Exp, AF.Ln, AF.Copy, AF.Identity, AF.Square, AF.Relu)
        set_id = None
        for idx, funcs in enumerate(get_activation_tables("TRN2").values()):
            if all(f in funcs for f in need):
                set_id = idx
                break
    except Exception:
        set_id = 6
    if set_id is None:
        return

    @blk.scalar
    def _(s_):
        s_.add_instruction(mybir.InstLoadActFuncSet(
            name=nc.get_next_instruction_name(),
            act_func_set_id=set_id, ins=[], outs=[]))


class _Prog:

    def __init__(self, nc, st, chans):
        self.nc = nc
        self.sems = {k: st.enter_context(nc.semaphore(k)) for k in chans}
        self.cnt = {k: 0 for k in chans}
        self.M = {}

    def inc(self, instr, ch, n, mark=None):
        instr.then_inc(self.sems[ch], n)
        self.cnt[ch] += n
        if mark:
            self.M[mark] = (ch, self.cnt[ch])
        return self.cnt[ch]

    def w(self, eng, mark):
        ch, val = self.M[mark]
        eng.wait_ge(self.sems[ch], val)

    def ss(self, eng, instr, ch):
        instr.then_inc(self.sems[ch], 1)
        self.cnt[ch] += 1
        eng.wait_ge(self.sems[ch], self.cnt[ch])



def build_L1():
    nc = bass.Bass("TRN2", target_bir_lowering=False)
    x = nc.dram_tensor("x", [S, D], F32, kind="ExternalInput").ap()
    wqT = nc.dram_tensor("wqT", [D, H], F32, kind="ExternalInput").ap()
    wkT = nc.dram_tensor("wkT", [D, H], F32, kind="ExternalInput").ap()
    wvT = nc.dram_tensor("wvT", [D, D], F32, kind="ExternalInput").ap()
    wfc_sl = nc.dram_tensor("wfc_sl", [D, S], F32, kind="ExternalInput").ap()
    wproj_sl = nc.dram_tensor("wproj_sl", [FF, 256], F32, kind="ExternalInput").ap()

    o_q = nc.dram_tensor("q", [H, S], F32, kind="ExternalOutput").ap()
    o_cand = nc.dram_tensor("cand", [H, 64], F32, kind="ExternalOutput").ap()
    o_payV = nc.dram_tensor("payV", [128, 1024], BF16, kind="ExternalOutput").ap()
    o_wamax = nc.dram_tensor("wamax", [1, 2], F32, kind="ExternalOutput").ap()

    idx_scr = nc.dram_tensor("idx_scr", [H, 64], F32)

    with ExitStack() as st:
        sb = lambda n, s, d=F32: st.enter_context(nc.sbuf_tensor(n, s, d))
        psum = lambda n, s: st.enter_context(nc.psum_tensor(n, s, F32))

        x_ch = [sb(f"x{i}", [128, D]) for i in range(2)]
        lnx_ch = [sb(f"lnxc{i}", [128, D]) for i in range(2)]
        stg = [sb(f"stg{i}", [128, 512]) for i in range(2)]
        lnb = sb("lnb", [128, 16 * S], BF16)
        wq_t = sb("wq_t", [128, 16 * H])
        wk_t = sb("wk_t", [128, 16 * H])
        wv_ch = [sb(f"wv{i}", [128, 512], BF16) for i in range(2)]
        v_tm = sb("v_tm", [128, 8 * D], BF16)
        q_sb = sb("q_sbuf", [H, S])
        k_sb = sb("k_sb", [H, S])
        negk = sb("negk", [H, S])
        work = [sb(f"wk_w{i}", [H, S]) for i in range(2)]
        cand = sb("cand_sb", [H, 64])
        idx16 = sb("idx16", [H, 64], U16)
        idxf = sb("idxf", [H, 64])
        idx_bc = sb("idx_bc", [128, 8192], F16)
        gt = [sb(f"gt{i}", [128, 8192], BF16) for i in range(2)]
        payV = sb("payV_sb", [128, 1024], BF16)
        iota_tok = sb("iota_tok", [128, 8])
        ident32 = sb("ident32", [128, 128])
        stat = sb("stat", [128, 64])
        wsl = [sb(f"wsl{i}", [128, 2048]) for i in range(2)]
        wparts = sb("wparts", [128, 64])
        wpool = sb("wpool", [1, 64])
        wamax_sb = sb("wamax_sb", [1, 2])

        psA = psum("psA", [128, 2048])
        psB = psum("psB", [128, 1024])
        psC = psum("psC", [128, 1024])

        P = _Prog(nc, st, ("sq", "gq", "aq", "pe", "sc", "ve", "po",
                   "x0", "x1", "wv0", "wv1", "ws0", "ws1", "stq"))
        inc, w, ss = P.inc, P.w, P.ss

        blk = st.enter_context(nc.Block())

        @blk.gpsimd
        def _(g):
            inc(g.memset(ident32[:], 0.0), "po", 1, "id0")
            g.iota(iota_tok[:], [[128, 8]], base=0, channel_multiplier=1,
                   allow_small_or_imprecise_dtypes=True)
            w(g, "id0")
            inc(g.affine_select(out=ident32[:], in_=ident32[:], compare_op=ALU.not_equal,
                                fill=1.0, base=0, pattern=[[-1, 128]],
                                channel_multiplier=1), "po", 1, "const")

        @blk.vector
        def _(v):
            inc(v.memset(payV[:], 0.0), "ve", 1, "pay0")

        @blk.sync
        def _(sync):
            for c in range(16):
                inc(sync.dma_start(out=wq_t[:, c * H:(c + 1) * H],
                                   in_=wqT[c * 128:(c + 1) * 128, :]),
                    "sq", 16, "wq" if c == 15 else None)
            for c in range(16):
                inc(sync.dma_start(out=wk_t[:, c * H:(c + 1) * H],
                                   in_=wkT[c * 128:(c + 1) * 128, :]),
                    "sq", 16, "wk" if c == 15 else None)

        _sc = nc.enter_named_scope("l1_stats", False)[0]
        for i in range(8):
            @blk.sync
            def _(sync, i=i):
                if i >= 2:
                    w(sync, f"stat{i - 2}")
                inc(sync.dma_start(out=x_ch[i % 2][:], in_=x[i * 128:(i + 1) * 128, :]),
                    f"x{i % 2}", 16, f"x{i}")

            @blk.scalar
            def _(s, i=i):
                w(s, f"x{i}")
                ss(s, s.activation(lnx_ch[i % 2][:], x_ch[i % 2][:], AF.Copy,
                                   accum_out=stat[:, i:i + 1]), "sc")
                inc(s.activation(lnx_ch[i % 2][:], x_ch[i % 2][:], AF.Square,
                                 accum_out=stat[:, 8 + i:9 + i]), "sc", 1, f"stat{i}")

        @blk.vector
        def _(v):
            w(v, "stat7")
            inv = 1.0 / D
            ss(v, v.tensor_scalar(stat[:, 16:24], stat[:, 0:8], inv, None, ALU.mult, ALU.bypass), "ve")
            ss(v, v.tensor_scalar(stat[:, 24:32], stat[:, 8:16], inv, None, ALU.mult, ALU.bypass), "ve")
            ss(v, v.tensor_tensor(stat[:, 48:56], stat[:, 16:24], stat[:, 16:24], ALU.mult), "ve")
            ss(v, v.tensor_tensor(stat[:, 24:32], stat[:, 24:32], stat[:, 48:56], ALU.subtract), "ve")
            inc(v.tensor_scalar(stat[:, 24:32], stat[:, 24:32], EPS, None, ALU.add,
                                ALU.bypass), "ve", 1, "var")

        @blk.scalar
        def _(s):
            w(s, "var")
            ss(s, s.activation(stat[:, 48:56], stat[:, 24:32], AF.Ln), "sc")
            inc(s.activation(stat[:, 32:40], stat[:, 48:56], AF.Exp, scale=-0.5),
                "sc", 1, "rstd")

        @blk.vector
        def _(v):
            w(v, "rstd")
            ss(v, v.tensor_tensor(stat[:, 56:64], stat[:, 16:24], stat[:, 32:40], ALU.mult), "ve")
            inc(v.tensor_scalar(stat[:, 40:48], stat[:, 56:64], -1.0, None, ALU.mult,
                                ALU.bypass), "ve", 1, "bias")

        nc.leave_named_scope("l1_stats", _sc, False)
        _sc = nc.enter_named_scope("l1_qk", False)[0]
        slots2 = [psB[:, 0:512], psC[:, 0:512]]

        def emit_qk_mms(t, g):
            i, g2 = divmod(g, 4)
            w(t, f"evf{g}")
            first0 = (i % 4 == 0 and g2 == 0)
            mm = None
            for j in range(4):
                c = g2 * 4 + j
                first = first0 and j == 0
                t.matmul(psA[:, i * 128:(i + 1) * 128], wq_t[:, c * H:(c + 1) * H],
                         stg[g % 2][:, j * 128:(j + 1) * 128],
                         start=first, stop=False, skip_group_check=True)
                mm = t.matmul(psA[:, 1024 + i * 128:1024 + (i + 1) * 128],
                              wk_t[:, c * H:(c + 1) * H],
                              stg[g % 2][:, j * 128:(j + 1) * 128],
                              start=first, stop=False, skip_group_check=True)
            inc(mm, "pe", 1, f"qkm{g}")

        for i in range(8):
            @blk.sync
            def _(sync, i=i):
                if i == 0:
                    w(sync, "stat7")
                else:
                    w(sync, f"lnxn{i - 1}")
                inc(sync.dma_start(out=x_ch[i % 2][:], in_=x[i * 128:(i + 1) * 128, :]),
                    f"x{i % 2}", 16, f"x2_{i}")

            @blk.scalar
            def _(s, i=i):
                w(s, f"x2_{i}")
                if i == 0:
                    w(s, "stat7")
                    w(s, "bias")
                if i >= 2:
                    w(s, f"trg{(i - 2) * 4 + 3}")
                inc(s.activation(lnx_ch[i % 2][:], x_ch[i % 2][:], AF.Identity,
                                 scale=stat[:, 32 + i:33 + i],
                                 bias=stat[:, 40 + i:41 + i]), "sc", 1, f"lnxn{i}")

            for g2 in range(4):
                g = i * 4 + g2

                @blk.tensor
                def _(t, i=i, g2=g2, g=g):
                    if g == 0:
                        w(t, "const")
                        w(t, "wk")
                    if g2 == 0:
                        w(t, f"lnxn{i}")
                    if g >= 2:
                        w(t, f"evf{g - 2}")
                        w(t, f"evb{g - 2}")
                    mm = None
                    for j in range(4):
                        c = g2 * 4 + j
                        mm = t.transpose(slots2[g % 2][:, j * 128:(j + 1) * 128],
                                         lnx_ch[i % 2][:, c * 128:(c + 1) * 128],
                                         ident32[:])
                    inc(mm, "pe", 1, f"trg{g}")

                @blk.vector
                def _(v, i=i, g2=g2, g=g):
                    w(v, f"trg{g}")
                    if g >= 2:
                        w(v, f"qkm{g - 2}")
                    ss(v, v.tensor_copy(stg[g % 2][:], slots2[g % 2]), "ve")
                    P.M[f"evf{g}"] = ("ve", P.cnt["ve"])
                    lnb_v = lnb[:].rearrange("p (c t) -> p c t", t=S)[
                        :, g2 * 4:g2 * 4 + 4, i * 128:(i + 1) * 128]
                    ps_v = slots2[g % 2].rearrange("p (c t) -> p c t", t=128)
                    inc(v.tensor_copy(lnb_v, ps_v), "ve", 1, f"evb{g}")

                @blk.tensor
                def _(t, g=g):
                    if g >= 1:
                        emit_qk_mms(t, g - 1)

        @blk.tensor
        def _(t):
            emit_qk_mms(t, 31)

        @blk.scalar
        def _(s):
            w(s, "qkm31")
            s.activation(q_sb[:], psA[:, 0:1024], AF.Copy)
            inc(s.activation(k_sb[:], psA[:, 1024:2048], AF.Copy),
                "sc", 1, "qk_evac")

        nc.leave_named_scope("l1_qk", _sc, False)
        _sc = nc.enter_named_scope("l1_v", False)[0]
        def emit_wam(kk):
            @blk.sync
            def _(s, kk=kk):
                if kk >= 2:
                    w(s, f"wred{kk - 2}")
                if kk < 8:
                    for j_ in range(2):
                        a = kk * 2 + j_
                        inc(s.dma_start(out=wsl[kk % 2][:, j_ * 1024:(j_ + 1) * 1024],
                                        in_=wfc_sl[a * 128:(a + 1) * 128, :]),
                            f"ws{kk % 2}", 16, f"wsl{kk}" if j_ == 1 else None)
                else:
                    for j_ in range(8):
                        a = (kk - 8) * 8 + j_
                        inc(s.dma_start(out=wsl[kk % 2][:, j_ * 256:(j_ + 1) * 256],
                                        in_=wproj_sl[a * 128:(a + 1) * 128, :]),
                            f"ws{kk % 2}", 16, f"wsl{kk}" if j_ == 7 else None)

            @blk.vector
            def _(v, kk=kk):
                w(v, f"wsl{kk}")
                if kk >= 1:
                    w(v, f"wred{kk - 1}")
                base = 0 if kk < 8 else 32
                j0 = kk if kk < 8 else kk - 8
                ss(v, v.tensor_scalar(negk[:, 0:S], wsl[kk % 2][:, 0:S], 0.0,
                                      None, ALU.max, ALU.max,
                                      accum_out=wparts[:, base + j0:base + j0 + 1]), "ve")
                ss(v, v.tensor_scalar(negk[:, 0:S], wsl[kk % 2][:, S:2 * S], 0.0,
                                      None, ALU.max, ALU.max,
                                      accum_out=wparts[:, base + 8 + j0:base + 9 + j0]), "ve")
                ss(v, v.tensor_scalar(negk[:, 0:S], wsl[kk % 2][:, 0:S], -1.0,
                                      None, ALU.mult, ALU.max,
                                      accum_out=wparts[:, base + 16 + j0:base + 17 + j0]), "ve")
                inc(v.tensor_scalar(negk[:, 0:S], wsl[kk % 2][:, S:2 * S], -1.0,
                                    None, ALU.mult, ALU.max,
                                    accum_out=wparts[:, base + 24 + j0:base + 25 + j0]),
                    "ve", 1, f"wred{kk}")


        @blk.vector
        def _(v):
            w(v, "qk_evac")
            ss(v, v.tensor_scalar(negk[:], k_sb[:], -1.0, None, ALU.mult, ALU.bypass), "ve")
            for sel, srcT in ((0, k_sb), (1, negk)):
                cur = srcT
                for p in range(4):
                    col = sel * 32 + p * 8
                    ss(v, v.max(cand[:, col:col + 8], cur[:]), "ve")
                    ss(v, v.max_index(idx16[:, col:col + 8], cand[:, col:col + 8], cur[:]), "ve")
                    if p < 3:
                        nxt = work[p % 2]
                        ss(v, v.match_replace(nxt[:], in_to_replace=cand[:, col:col + 8],
                                              in_values=cur[:], imm_value=NEG), "ve")
                        cur = nxt
            inc(v.tensor_scalar(idxf[:], idx16[:], 0.0, None, ALU.add, ALU.bypass),
                "ve", 1, "sel_done")


        vtiles = [psA[:, 0:512], psA[:, 512:1024], psA[:, 1024:1536], psA[:, 1536:2048],
                  psB[:, 0:512], psB[:, 512:1024], psC[:, 0:512], psC[:, 512:1024]]
        for t_ in range(4):
            for c in range(16):
                k_ = t_ * 16 + c

                @blk.gpsimd
                def _(g, t_=t_, c=c, k_=k_):
                    if k_ >= 2:
                        w(g, f"vmm{k_ - 2}")
                    inc(g.dma_start(out=wv_ch[k_ % 2][:],
                                    in_=wvT[c * 128:(c + 1) * 128,
                                            t_ * 512:(t_ + 1) * 512]),
                        f"wv{k_ % 2}", 16, f"wv{k_}")

                @blk.tensor
                def _(t, t_=t_, c=c, k_=k_):
                    w(t, f"wv{k_}")
                    if k_ == 0:
                        w(t, "qk_evac")
                        w(t, "evb31")
                    if c == 0 and t_ > 0:
                        w(t, f"vev{t_ - 1}")
                    for i in range(7):
                        t.matmul(vtiles[i],
                                 lnb[:, c * S + i * 128:c * S + (i + 1) * 128],
                                 wv_ch[k_ % 2][:], start=(c == 0), stop=(c == 15))
                    inc(t.matmul(vtiles[7],
                                 lnb[:, c * S + 7 * 128:c * S + 8 * 128],
                                 wv_ch[k_ % 2][:], start=(c == 0), stop=(c == 15)),
                        "pe", 1, f"vmm{k_}")

            @blk.vector
            def _(v, t_=t_):
                w(v, f"vmm{t_ * 16 + 15}")
                for i in range(8):
                    v.tensor_copy(v_tm[:, i * D + t_ * 512:i * D + (t_ + 1) * 512],
                                  vtiles[i])
                inc(v.engine_nop(), "ve", 1, f"vev{t_}")

            for _wk in range(4 * t_, 4 * t_ + 4):
                emit_wam(_wk)

        nc.leave_named_scope("l1_v", _sc, False)
        _sc = nc.enter_named_scope("l1_sel", False)[0]
        nc.leave_named_scope("l1_sel", _sc, False)
        _sc = nc.enter_named_scope("l1_pay", False)[0]
        @blk.gpsimd
        def _(g):
            w(g, "sel_done")
            inc(g.dma_start(out=idx_scr[:, :], in_=idxf[:]), "gq", 16, "idxw")
            w(g, "idxw")
            flat = idx_scr.rearrange("a b -> (a b)")
            inc(g.dma_start(out=idx_bc[:, :],
                            in_=flat[None, :].broadcast_to([128, 8192])),
                "gq", 16, "idxbc")

        for i in range(8):
            @blk.vector
            def _(v, i=i):
                if i == 0:
                    w(v, "idxbc")
                if i >= 2:
                    w(v, f"cmm{i - 2}")
                inc(v.tensor_scalar(gt[i % 2][:], idx_bc[:], iota_tok[:, i:i + 1], 0.0,
                                    ALU.subtract, ALU.is_equal), "ve", 1, f"gt{i}")

            @blk.tensor
            def _(t, i=i):
                w(t, f"gt{i}")
                if i == 0:
                    w(t, "vev3")
                for pair in range(64):
                    mm = t.matmul(psA[:, pair * 32:(pair + 1) * 32],
                                  gt[i % 2][:, pair * 128:(pair + 1) * 128],
                                  v_tm[:, i * D + pair * 32:i * D + (pair + 1) * 32],
                                  start=(i == 0 and pair % 16 == 0), stop=False,
                                  skip_group_check=True)
                inc(mm, "pe", 1, f"cmm{i}")

        @blk.vector
        def _(v):
            w(v, "cmm7")
            w(v, "pay0")
            for pair in range(64):
                v.tensor_copy(payV[0:64, pair * 16:(pair + 1) * 16],
                              psA[0:64, pair * 32:pair * 32 + 16])
                v.tensor_copy(payV[64:128, pair * 16:(pair + 1) * 16],
                              psA[64:128, pair * 32 + 16:pair * 32 + 32])
            inc(v.engine_nop(), "ve", 1, "pay_done")

        nc.leave_named_scope("l1_pay", _sc, False)
        _sc = nc.enter_named_scope("l1_wam", False)[0]
        @blk.gpsimd
        def _(g):
            w(g, "wred15")
            inc(g.tensor_reduce(wpool[:, :], wparts[:], axis=AX.C, op=ALU.max),
                "po", 1, "wpool")

        @blk.vector
        def _(v):
            w(v, "wpool")
            v.reduce_max(wamax_sb[:, 0:1], wpool[:, 0:32], axis=AX.X)
            inc(v.reduce_max(wamax_sb[:, 1:2], wpool[:, 32:64], axis=AX.X),
                "ve", 1, "wamax_done")

        nc.leave_named_scope("l1_wam", _sc, False)
        _sc = nc.enter_named_scope("l1_st", False)[0]
        @blk.sync
        def _(sync):
            w(sync, "qk_evac")
            inc(sync.dma_start(out=o_q[:, :], in_=q_sb[:]), "stq", 16)
            w(sync, "sel_done")
            inc(sync.dma_start(out=o_cand[:, :], in_=cand[:]), "stq", 16)
            w(sync, "pay_done")
            inc(sync.dma_start(out=o_payV[:, :], in_=payV[:]), "stq", 16)
            w(sync, "wamax_done")
            inc(sync.dma_start(out=o_wamax[:, :], in_=wamax_sb[:]), "stq", 16)
            sync.wait_ge(P.sems["stq"], P.cnt["stq"])

        nc.leave_named_scope("l1_st", _sc, False)

    return nc




def build_L2(stage=99):
    nc = bass.Bass("TRN2", target_bir_lowering=False)
    x = nc.dram_tensor("x", [S, D], F32, kind="ExternalInput").ap()
    qin = nc.dram_tensor("q", [H, S], F32, kind="ExternalInput").ap()
    ktin = nc.dram_tensor("kt", [H, 128], F32, kind="ExternalInput").ap()
    kbin = nc.dram_tensor("kb", [H, 128], F32, kind="ExternalInput").ap()
    cvtin = nc.dram_tensor("cvt", [128, D], BF16, kind="ExternalInput").ap()
    cvbin = nc.dram_tensor("cvb", [128, D], BF16, kind="ExternalInput").ap()
    woT = nc.dram_tensor("woT", [D, D], F32, kind="ExternalInput").ap()
    wfc_sl = nc.dram_tensor("wfc_sl", [D, S], F32, kind="ExternalInput").ap()
    wproj_sl = nc.dram_tensor("wproj_sl", [FF, 256], F32, kind="ExternalInput").ap()
    wam = nc.dram_tensor("wam", [1, 16], F32, kind="ExternalInput").ap()

    o_y = nc.dram_tensor("y", [S, D], F32, kind="ExternalOutput").ap()
    o_xam = nc.dram_tensor("xamax", [1, 1], F32, kind="ExternalOutput").ap()
    o_w8fc = nc.dram_tensor("w8fc_sl", [D, S], F8, kind="ExternalOutput").ap()
    o_w8pj = nc.dram_tensor("w8proj_sl", [FF, 256], F8, kind="ExternalOutput").ap()

    idx2_scr = nc.dram_tensor("idx2_scr", [H, 64], F32)
    kmq_scr = nc.dram_tensor("kmq_scr", [4, 64], F32)

    with ExitStack() as st:
        sb = lambda n, s, d=F32: st.enter_context(nc.sbuf_tensor(n, s, d))
        psum = lambda n, s, d=F32: st.enter_context(nc.psum_tensor(n, s, d))

        q_sb = sb("q_sb2", [H, S])
        qbf = sb("qbf", [H, S], BF16)
        qpos = sb("qpos", [128, 8 * H], mybir.dt.uint8)
        kt = sb("kt_sb", [H, 128])
        kb = sb("kb_sb", [H, 128])
        cvt = sb("cvt_sb", [128, D], BF16)
        cvb = sb("cvb_sb", [128, D], BF16)
        work = [sb(f"w2_{i}", [H, 128]) for i in range(2)]
        m82 = sb("m82", [H, 64])
        idx2 = sb("idx2", [H, 64], U16)
        idx2f = sb("idx2f", [H, 64])
        m82T = sb("m82T", [32, 2 * H])
        scol = sb("scol", [128, 64])
        kmc = sb("kmc", [128, 64])
        scolM = sb("scolM", [128, 64])
        RHS = sb("RHS", [128, 64 * 68], BF16)
        selw = sb("selw", [128, 8192], BF16)
        att = sb("att", [128, 8 * D], BF16)
        e_sb = sb("e_sb", [128, 14 * S], BF16)
        attnb = sb("attnb", [128, 4 * 448], BF16)
        maskx = sb("maskx", [128, 224], mybir.dt.uint8)
        recd = sb("recd", [128, 112])
        recs = sb("recs", [128, 112])
        iota_p = sb("iota_p2", [128, 1])
        ident32 = sb("ident2", [128, 128])
        identb = sb("identb", [128, 128], BF16)
        stat = sb("stat2", [128, 128])
        wamb = sb("wamb", [128, 16])
        scl = sb("scl2", [128, 4])

        psA = psum("psA2", [128, 2048])
        psB = psum("psB2", [128, 1024])
        psC = psum("psC2", [128, 1024], BF16)
        psT = psum("psT2", [128, 1024], BF16)

        P = _Prog(nc, st, ("sq", "gq", "aq", "pe", "sc", "ve", "po",
                           "x0", "x1", "wo0", "wo1", "wo2", "ws0", "ws1", "stq",
                           "stw0", "stw1", "sy0", "sy1"))
        inc, w, ss = P.inc, P.w, P.ss
        blk = st.enter_context(nc.Block())
        _preload_tables(nc, blk)

        def _stop():
            @blk.sync
            def _(sync):
                for ch in P.cnt:
                    if P.cnt[ch]:
                        sync.wait_ge(P.sems[ch], P.cnt[ch])

        @blk.gpsimd
        def _(g):
            inc(g.memset(ident32[:], 0.0), "po", 1, "id0")
            g.iota(iota_p[:], [[0, 1]], base=0, channel_multiplier=1,
                   allow_small_or_imprecise_dtypes=True)
            w(g, "id0")
            inc(g.affine_select(out=ident32[:], in_=ident32[:], compare_op=ALU.not_equal,
                                fill=1.0, base=0, pattern=[[-1, 128]],
                                channel_multiplier=1), "po", 1, "const")
            inc(g.iota(selw[:], [[2, 64], [0, 2], [1, 2], [0, 32]], base=0,
                       channel_multiplier=0,
                       allow_small_or_imprecise_dtypes=True), "po", 1, "selw_i")

        @blk.vector
        def _(v):
            w(v, "const")
            inc(v.tensor_copy(identb[:], ident32[:]), "ve", 1, "identb")
            w(v, "selw_i")
            inc(v.tensor_scalar(selw[:], selw[:], iota_p[:], 0.0,
                                ALU.subtract, ALU.is_equal), "ve", 1, "selw")
            ss(v, v.memset(RHS[:], 0.0), "ve")
            for j_ in range(2):
                den_t = RHS[j_ * 32:(j_ + 1) * 32, :].rearrange(
                    "p (g c) -> p g c", c=68)[:, :, j_ * 34 + 32:j_ * 34 + 33]
                ss(v, v.memset(den_t, 1.0), "ve")
                den_b = RHS[64 + j_ * 32:64 + (j_ + 1) * 32, :].rearrange(
                    "p (g c) -> p g c", c=68)[:, :, j_ * 34 + 33:j_ * 34 + 34]
                ss(v, v.memset(den_b, 1.0), "ve")
            inc(v.engine_nop(), "ve", 1, "rhs_ones")

        @blk.sync
        def _(sync):
            inc(sync.dma_start(out=q_sb[:], in_=qin[:, :]), "sq", 16)
            inc(sync.dma_start(out=kt[:], in_=ktin[:, :]), "sq", 16)
            inc(sync.dma_start(out=kb[:], in_=kbin[:, :]), "sq", 16)
            inc(sync.dma_start(out=cvt[:], in_=cvtin[:, :]), "sq", 16)
            inc(sync.dma_start(out=cvb[:], in_=cvbin[:, :]), "sq", 16)
            inc(sync.dma_start(out=wamb[:], in_=wam[0:1, :].broadcast_to([128, 16])),
                "sq", 16, "loads")

        _sc = nc.enter_named_scope("l2_prep", False)[0]
        @blk.vector
        def _(v):
            w(v, "loads")
            for sel, srcT in ((0, kt), (1, kb)):
                cur = srcT
                for p in range(4):
                    col = sel * 32 + p * 8
                    ss(v, v.max(m82[:, col:col + 8], cur[:]), "ve")
                    ss(v, v.max_index(idx2[:, col:col + 8], m82[:, col:col + 8], cur[:]), "ve")
                    if p < 3:
                        nxt = work[p % 2]
                        ss(v, v.match_replace(nxt[:], in_to_replace=m82[:, col:col + 8],
                                              in_values=cur[:], imm_value=NEG), "ve")
                        cur = nxt
            inc(v.tensor_scalar(idx2f[:], idx2[:], 0.0, None, ALU.add, ALU.bypass),
                "ve", 1, "sel2")
            inc(v.tensor_copy(qbf[:], q_sb[:]), "ve", 1, "qbf")
            ss(v, v.reduce_max(scl[:, 0:1], wamb[:, 0:8], axis=AX.X), "ve")
            ss(v, v.reduce_max(scl[:, 1:2], wamb[:, 8:16], axis=AX.X), "ve")
            ss(v, v.tensor_scalar(scl[:, 0:2], scl[:, 0:2], 1.0 / 448.0, 1e-6,
                                  ALU.mult, ALU.max), "ve")
            ss(v, v.reciprocal(scl[:, 2:4], scl[:, 0:2]), "ve")
            inc(v.tensor_scalar(scl[:, 2:4], scl[:, 2:4], 0.5, None, ALU.mult,
                                ALU.bypass), "ve", 1, "scl")

        @blk.tensor
        def _(t):
            w(t, "sel2")
            w(t, "const")
            inc(t.transpose(psB[0:32, 0:128], m82[:, 0:32], ident32[:]), "pe", 1, "m8t")

        @blk.vector
        def _(v):
            w(v, "m8t")
            inc(v.tensor_copy(m82T[:, 0:128], psB[0:32, 0:128]), "ve", 1, "m8tc")

        @blk.tensor
        def _(t):
            w(t, "m8tc")
            inc(t.transpose(psB[0:32, 0:128], m82[:, 32:64], ident32[:]), "pe", 1, "m8b")

        @blk.vector
        def _(v):
            w(v, "m8b")
            ss(v, v.tensor_scalar(m82T[:, 128:256], psB[0:32, 0:128], -1.0, None,
                                  ALU.mult, ALU.bypass), "ve")
            for sel in range(2):
                for j in range(2):
                    ss(v, v.tensor_copy(
                        scol[sel * 64 + j * 32:sel * 64 + (j + 1) * 32, :],
                        m82T[0:32, sel * 128 + j:sel * 128 + 128:2]), "ve")
            inc(v.engine_nop(), "ve", 1, "scol")

        @blk.sync
        def _(sync):
            w(sync, "scol")
            with nc.allow_non_contiguous_dma(reason="tiny 64-elem shift rows"):
                inc(sync.dma_start(out=kmq_scr[0:1, :], in_=m82T[0:1, 0:128:2]), "sq", 16)
                inc(sync.dma_start(out=kmq_scr[1:2, :], in_=m82T[0:1, 1:128:2]), "sq", 16)
                inc(sync.dma_start(out=kmq_scr[2:3, :], in_=m82T[0:1, 128:256:2]), "sq", 16)
                inc(sync.dma_start(out=kmq_scr[3:4, :], in_=m82T[0:1, 129:256:2]),
                    "sq", 16, "kmw")
            w(sync, "kmw")
            for rr in range(4):
                mk = "kmr" if rr == 3 else None
                inc(sync.dma_start(out=kmc[rr * 32:(rr + 1) * 32, :],
                                   in_=kmq_scr[rr:rr + 1, :].broadcast_to([32, 64])),
                    "sq", 16, mk)

        @blk.vector
        def _(v):
            w(v, "kmr")
            inc(v.tensor_tensor(scolM[:], scol[:], kmc[:], ALU.subtract),
                "ve", 1, "scolM")

        for i in range(8):
            @blk.tensor
            def _(t, i=i):
                if i == 0:
                    w(t, "scol")
                    w(t, "loads")
                else:
                    w(t, f"qpos{i - 1}")
                inc(t.transpose(psB[:, 128:256], q_sb[:, i * 128:(i + 1) * 128],
                                ident32[:]), "pe", 1, f"qtr{i}")

            @blk.vector
            def _(v, i=i):
                w(v, f"qtr{i}")
                inc(v.tensor_scalar(qpos[:, i * H:(i + 1) * H], psB[:, 128:256],
                                    0.0, None, ALU.is_gt, ALU.bypass),
                    "ve", 1, f"qpos{i}")

        if stage <= 1:
            _stop()
            return nc

        nc.leave_named_scope("l2_prep", _sc, False)
        _sc = nc.enter_named_scope("l2_cand", False)[0]
        with ExitStack() as st2:
            idx2_bc = st2.enter_context(nc.sbuf_tensor("idx2_bc", [128, 8192], F16))
            g2t = st2.enter_context(nc.sbuf_tensor("g2t", [128, 8192], BF16))
            g2b = st2.enter_context(nc.sbuf_tensor("g2b", [128, 8192], BF16))

            @blk.vector
            def _(v):
                inc(v.memset(g2t[:], 0.0), "ve", 1, "g2t0")
                inc(v.memset(g2b[:], 0.0), "ve", 1, "g2b0")

            @blk.gpsimd
            def _(g):
                w(g, "sel2")
                inc(g.dma_start(out=idx2_scr[:, :], in_=idx2f[:]), "gq", 16, "i2w")
                w(g, "i2w")
                flat = idx2_scr.rearrange("a b -> (a b)")
                inc(g.dma_start(out=idx2_bc[:, :],
                                in_=flat[None, :].broadcast_to([128, 8192])),
                    "gq", 16, "i2bc")

            @blk.vector
            def _(v):
                w(v, "i2bc")
                w(v, "g2t0")
                w(v, "g2b0")
                g2t_out = g2t[:].rearrange("p (g s j r) -> p g s j r",
                                           s=2, j=2, r=32)[:, :, 0]
                i2t_in = idx2_bc[:].rearrange("p (h s r) -> p h s r",
                                              s=2, r=32)[:, :, 0]
                ss(v, v.tensor_scalar(g2t_out, i2t_in, iota_p[:], 0.0,
                                      ALU.subtract, ALU.is_equal), "ve")
                g2b_out = g2b[:].rearrange("p (g s j r) -> p g s j r",
                                           s=2, j=2, r=32)[:, :, 1]
                i2b_in = idx2_bc[:].rearrange("p (h s r) -> p h s r",
                                              s=2, r=32)[:, :, 1]
                inc(v.tensor_scalar(g2b_out, i2b_in, iota_p[:], 0.0,
                                    ALU.subtract, ALU.is_equal), "ve", 1, "g2")

            for wv_ in range(4):
                for gg in range(16):
                    g_ = wv_ * 16 + gg
                    scol0 = (gg % 4) * 512 + (gg // 4) * 32

                    @blk.tensor
                    def _(t, g_=g_, gg=gg, wv_=wv_, scol0=scol0):
                        if g_ == 0:
                            w(t, "g2")
                        if g_ >= 4:
                            w(t, f"rhscp{g_ - 4}")
                        first = (gg < 4)
                        t.matmul(psA[:, scol0:scol0 + 32],
                                 g2t[:, g_ * 128:(g_ + 1) * 128],
                                 cvt[:, g_ * 32:(g_ + 1) * 32],
                                 start=first, stop=False, skip_group_check=True)
                        inc(t.matmul(psA[:, scol0:scol0 + 32],
                                     g2b[:, g_ * 128:(g_ + 1) * 128],
                                     cvb[:, g_ * 32:(g_ + 1) * 32],
                                     start=False, stop=True, skip_group_check=True),
                            "pe", 1, f"cv{g_}")

                    @blk.vector
                    def _(v, g_=g_, scol0=scol0):
                        w(v, f"cv{g_}")
                        if g_ == 0:
                            w(v, "rhs_ones")
                        col = g_ * 68
                        for j in range(2):
                            v.tensor_copy(RHS[j * 32:(j + 1) * 32,
                                              col + j * 34:col + j * 34 + 16],
                                          psA[j * 32:(j + 1) * 32,
                                              scol0 + j * 16:scol0 + (j + 1) * 16])
                            v.tensor_copy(RHS[64 + j * 32:64 + (j + 1) * 32,
                                              col + j * 34 + 16:col + j * 34 + 32],
                                          psA[64 + j * 32:64 + (j + 1) * 32,
                                              scol0 + j * 16:scol0 + (j + 1) * 16])
                        inc(v.engine_nop(), "ve", 1, f"rhscp{g_}")

        if stage <= 2:
            _stop()
            return nc

        nc.leave_named_scope("l2_cand", _sc, False)
        _sc = nc.enter_named_scope("l2_attn", False)[0]
        def emit_w8_load(a):
            @blk.gpsimd
            def _(g, a=a):
                if a < 2:
                    w(g, "rhscp63")
                if a >= 2:
                    w(g, f"w8q{a - 2}")
                if a < 16:
                    inc(g.dma_start(out=wq_buf[a % 2][:],
                                    in_=wfc_sl[a * 128:(a + 1) * 128, :]),
                        f"ws{a % 2}", 16, f"wfl{a}")
                else:
                    kk = a - 16
                    for j_ in range(4):
                        r = kk * 4 + j_
                        inc(g.dma_start(out=wq_buf[a % 2][:, j_ * 256:(j_ + 1) * 256],
                                        in_=wproj_sl[r * 128:(r + 1) * 128, :]),
                            f"ws{a % 2}", 16, f"wfl{a}" if j_ == 3 else None)

        def emit_w8_quant(a):
            @blk.scalar
            def _(s_, a=a):
                w(s_, f"wfl{a}")
                if a == 0:
                    w(s_, "scl")
                if a >= 2:
                    w(s_, f"w8st{a - 2}")
                inc(s_.activation(w8_buf[a % 2][:], wq_buf[a % 2][:], AF.Copy,
                                  scale=scl[:, 2:3] if a < 16 else scl[:, 3:4]),
                    "sc", 1, f"w8q{a}")

            @blk.sync
            def _(sync, a=a):
                w(sync, f"w8q{a}")
                if a < 16:
                    inc(sync.dma_start(out=o_w8fc[a * 128:(a + 1) * 128, :],
                                       in_=w8_buf[a % 2][:]),
                        f"stw{a % 2}", 16, f"w8st{a}")
                else:
                    kk = a - 16
                    for j_ in range(4):
                        r = kk * 4 + j_
                        inc(sync.dma_start(out=o_w8pj[r * 128:(r + 1) * 128, :],
                                           in_=w8_buf[a % 2][:, j_ * 256:(j_ + 1) * 256]),
                            f"stw{a % 2}", 16, f"w8st{a}" if j_ == 3 else None)

            if a + 2 < 32:
                emit_w8_load(a + 2)

        GBLK = [list(range(a, min(a + 7, 64))) for a in range(0, 64, 7)]
        psE = [psB[:, 0:512], psB[:, 512:1024]]
        psO = [psA[:, b * 512:(b + 1) * 512] for b in range(4)]

        afm = sb("afm", [128, 16 * S], BF16)
        x_ch = [sb(f"x2c{i}", [128, D]) for i in range(2)]
        wo_cache = sb("wo_cache", [128, 16 * 512], BF16)
        y_blk = [sb(f"yblk{i}", [128, D]) for i in range(2)]
        wq_buf = [sb(f"wqb{i}", [128, 1024]) for i in range(2)]
        w8_buf = [sb(f"w8b{i}", [128, 1024], F8) for i in range(2)]
        junk2 = sb("junk2", [128, 512])
        stat3 = sb("stat3b", [128, 64])
        wof32 = [sb(f"wof32_{i}", [128, 512]) for i in range(2)]

        emit_w8_load(0)
        emit_w8_load(1)

        trslot = [psT[:, 0:512], psC[:, 0:512]]
        _ntr = [0]

        def emit_E(bi, blkg):
            base = (bi % 2) * 7 * S
            for sl_, g_ in enumerate(blkg):
                for hf in range(2):
                    k2 = 2 * g_ + hf
                    eslot = k2 % 2

                    @blk.tensor
                    def _(t, bi=bi, blkg=blkg, g_=g_, hf=hf, k2=k2, eslot=eslot):
                        if bi == 0 and g_ == blkg[0] and hf == 0:
                            w(t, "selw")
                            w(t, "qbf")
                            w(t, "rhscp63")
                            w(t, "qpos7")
                        if k2 >= 2:
                            w(t, f"exp{k2 - 2}")
                        inc(t.matmul(psE[eslot], selw[:, g_ * 128:(g_ + 1) * 128],
                                     qbf[:, hf * 512:(hf + 1) * 512],
                                     start=True, stop=True), "pe", 1, f"eb{k2}")

                    @blk.scalar
                    def _(s_, bi=bi, sl_=sl_, g_=g_, hf=hf, k2=k2, eslot=eslot,
                          base=base):
                        w(s_, f"eb{k2}")
                        if k2 == 0:
                            w(s_, "scolM")
                        if bi >= 2 and sl_ == 0 and hf == 0:
                            w(s_, f"combdone{bi - 2}")
                        inc(s_.activation(e_sb[:, base + sl_ * S + hf * 512:
                                               base + sl_ * S + (hf + 1) * 512],
                                          psE[eslot], AF.Exp,
                                          scale=scolM[:, g_:g_ + 1]),
                            "sc", 1, f"exp{k2}")

        def emit_comb(bi, blkg, nb, hx):
            base = (bi % 2) * 7 * S

            @blk.tensor
            def _(t, bi=bi, blkg=blkg, nb=nb, hx=hx, base=base):
                w(t, f"exp{2 * blkg[-1] + 1}")
                if hx == 0:
                    if bi >= 1:
                        w(t, f"nselB{bi - 1}")
                else:
                    w(t, f"nselA{bi}")
                mm = None
                for b in range(4):
                    i = hx * 4 + b
                    for sl_, g_ in enumerate(blkg):
                        mm = t.matmul(
                            psO[b][:, sl_ * 68:(sl_ + 1) * 68],
                            e_sb[:, base + sl_ * S + i * 128:
                                 base + sl_ * S + (i + 1) * 128],
                            RHS[:, g_ * 68:(g_ + 1) * 68],
                            start=(sl_ == 0), stop=(sl_ == nb - 1),
                            skip_group_check=True)
                inc(mm, "pe", 1, f"comb{bi}_{hx}")
                if hx == 1:
                    P.M[f"combdone{bi}"] = ("pe", P.cnt["pe"])

        def emit_norm(bi, blkg, nb, hx):
            g0 = blkg[0]

            @blk.vector
            def _(v, bi=bi, blkg=blkg, nb=nb, hx=hx, g0=g0):
                w(v, f"comb{bi}_{hx}")
                dens = psA[:, 0:2048].rearrange("p (b x) -> p b x", b=4)[
                    :, :, 0:nb * 68].rearrange("p b (u j c) -> p b u j c",
                                               j=2, c=34)[:, :, :, :, 32:34]
                recs_v = recs[:, 0:4 * nb * 4].rearrange(
                    "p (b u j s) -> p b u j s", b=4, j=2, s=2)
                ss(v, v.tensor_copy(recs_v, dens), "ve")
                ss(v, v.reciprocal(recd[:, 0:4 * nb * 4], recs[:, 0:4 * nb * 4]),
                   "ve")
                for b in range(4):
                    po = psO[b]
                    num_ap = po[:, 0:nb * 68].rearrange(
                        "p (u j c) -> p u j c", j=2, c=34)[:, :, :, 0:32].rearrange(
                        "p u j (s c) -> p u j s c", c=16)
                    rec_ex = recd[:, b * nb * 4:(b + 1) * nb * 4].rearrange(
                        "p (u j s) -> p u j s", j=2, s=2)[
                        :, :, :, :, None].broadcast_to([128, nb, 2, 2, 16])
                    tt_out5 = attnb[:, b * 448:b * 448 + 448].rearrange(
                        "p (s uj c) -> p uj s c", s=2, c=16).rearrange(
                        "p (u j) s c -> p u j s c", j=2)[:, 0:nb]
                    ss(v, v.tensor_tensor(tt_out5, num_ap, rec_ex, ALU.mult), "ve")
                for b in range(4):
                    i = hx * 4 + b
                    mask3 = qpos[:, i * H + 2 * g0:i * H + 2 * (g0 + nb)][
                        :, :, None].broadcast_to([128, nb * 2, 16])
                    mx3 = maskx[:, 0:nb * 32].rearrange("p (u c) -> p u c", c=16)
                    ss(v, v.tensor_copy(mx3, mask3), "ve")
                    out2 = att[:, i * D + g0 * 32:i * D + (g0 + nb) * 32]
                    ss(v, v.tensor_copy(
                        out2, attnb[:, b * 448 + 224:b * 448 + 224 + nb * 32]), "ve")
                    cp = v.copy_predicated(out2, maskx[:, 0:nb * 32],
                                           attnb[:, b * 448:b * 448 + nb * 32])
                    if b == 3:
                        inc(cp, "ve", 1, f"nsel{'AB'[hx]}{bi}")
                    else:
                        ss(v, cp, "ve")

        def emit_tr(bi, i_list, cs):
            ihalf = i_list[0] // 4
            for c in cs:
                n = _ntr[0]
                _ntr[0] += 1

                @blk.tensor
                def _(t, n=n, c=c, bi=bi, ihalf=ihalf):
                    if n == 0:
                        w(t, "identb")
                    w(t, f"nsel{'AB'[ihalf]}{bi}")
                    if n >= 2:
                        w(t, f"aev{n - 2}")
                    mm = None
                    for k in range(4):
                        i = ihalf * 4 + k
                        mm = t.transpose(
                            trslot[n % 2][:, k * 128:(k + 1) * 128],
                            att[:, i * D + c * 128:i * D + (c + 1) * 128],
                            identb[:])
                    inc(mm, "pe", 1, f"atr{n}")

                @blk.scalar
                def _(s_, n=n, c=c, ihalf=ihalf):
                    w(s_, f"atr{n}")
                    inc(s_.activation(afm[:, c * S + ihalf * 512:
                                           c * S + (ihalf + 1) * 512],
                                      trslot[n % 2], AF.Copy), "sc", 1, f"aev{n}")

        cs_done = {bi: [c for c in range(16) if (4 * c + 3) // 7 == bi]
                   for bi in range(10)}
        for bi, blkg in enumerate(GBLK):
            nb = len(blkg)
            if bi >= 1:
                pb = GBLK[bi - 1]
                emit_comb(bi - 1, pb, len(pb), 1)
                emit_norm(bi - 1, pb, len(pb), 1)
                emit_tr(bi - 1, [0, 1, 2, 3], cs_done[bi - 1])
            emit_E(bi, blkg)
            for _qa in range(bi * 4, min(bi * 4 + 4, 32)):
                emit_w8_quant(_qa)
            emit_comb(bi, blkg, nb, 0)
            emit_norm(bi, blkg, nb, 0)
            if bi >= 1:
                emit_tr(bi - 1, [4, 5, 6, 7], cs_done[bi - 1])
        emit_comb(9, GBLK[9], len(GBLK[9]), 1)
        emit_norm(9, GBLK[9], len(GBLK[9]), 1)
        emit_tr(9, [0, 1, 2, 3], cs_done[9])
        emit_tr(9, [4, 5, 6, 7], cs_done[9])
        assert _ntr[0] == 32

        if stage <= 3:
            _stop()
            return nc

        nc.leave_named_scope("l2_attn", _sc, False)
        _sc = nc.enter_named_scope("l2_wo", False)[0]
        for e in range(4):
            @blk.gpsimd
            def _(g, e=e):
                if e == 0:
                    w(g, "rhscp63")
                else:
                    w(g, f"womh{e - 1}_1")
                for c in range(8):
                    inc(g.dma_start(out=wo_cache[:, c * 512:(c + 1) * 512],
                                    in_=woT[c * 128:(c + 1) * 128,
                                            e * 512:(e + 1) * 512]),
                        "wo0", 16, f"wog{e}" if c == 7 else None)

            for c in range(8, 16):
                kc = e * 8 + (c - 8)

                @blk.sync
                def _(sync, e=e, c=c, kc=kc):
                    if kc < 2:
                        w(sync, "rhscp63")
                    else:
                        w(sync, f"wocst{kc - 2}")
                    inc(sync.dma_start(out=wof32[kc % 2][:],
                                       in_=woT[c * 128:(c + 1) * 128,
                                               e * 512:(e + 1) * 512]),
                        f"wo{1 + kc % 2}", 16, f"wol{kc}")

                @blk.vector
                def _(v, e=e, c=c, kc=kc):
                    w(v, f"wol{kc}")
                    if c == 8 and e >= 1:
                        w(v, f"womm{e - 1}_1")
                    inc(v.tensor_copy(wo_cache[:, c * 512:(c + 1) * 512],
                                      wof32[kc % 2][:]),
                        "ve", 1,
                        f"wocst{kc}" if c < 15 else f"wos{e}")
                    if c == 15:
                        P.M[f"wocst{kc}"] = P.M[f"wos{e}"]

            for hh in range(2):
                gw = e * 2 + hh

                @blk.sync
                def _(sync, e=e, hh=hh, gw=gw):
                    if gw < 2:
                        w(sync, "rhscp63")
                    else:
                        w(sync, f"yev{gw - 2}")
                    for ii in range(4):
                        i = hh * 4 + ii
                        inc(sync.dma_start(
                            out=x_ch[hh][:, ii * 512:(ii + 1) * 512],
                            in_=x[i * 128:(i + 1) * 128, e * 512:(e + 1) * 512]),
                            f"x{hh}", 16, f"xs{gw}" if ii == 3 else None)

                @blk.tensor
                def _(t, e=e, hh=hh, gw=gw):
                    w(t, f"wog{e}")
                    w(t, f"wos{e}")
                    if gw == 0:
                        w(t, "aev31")
                    if gw >= 1:
                        w(t, f"yev{gw - 1}")
                    mm = None
                    for c in range(16):
                        for ii in range(4):
                            i = hh * 4 + ii
                            mm = t.matmul(
                                psO[ii][:, 0:512],
                                afm[:, c * S + i * 128:c * S + (i + 1) * 128],
                                wo_cache[:, c * 512:(c + 1) * 512],
                                start=(c == 0), stop=(c == 15),
                                skip_group_check=True)
                        if c == 7:
                            inc(mm, "pe", 1, f"womh{e}_{hh}")
                    inc(mm, "pe", 1, f"womm{e}_{hh}")

                @blk.vector
                def _(v, e=e, hh=hh, gw=gw):
                    w(v, f"womm{e}_{hh}")
                    w(v, f"xs{gw}")
                    if gw >= 2:
                        w(v, f"ysw{gw - 2}")
                        w(v, f"yst{gw - 2}")
                        w(v, f"yrd{gw - 2}")
                    yb = y_blk[gw % 2]
                    mm = None
                    for ii in range(4):
                        ss(v, v.tensor_tensor(yb[:, ii * 512:(ii + 1) * 512],
                                              psO[ii][:, 0:512],
                                              x_ch[hh][:, ii * 512:(ii + 1) * 512],
                                              ALU.add), "ve")
                    inc(v.engine_nop(), "ve", 1, f"yev{gw}")

                @blk.vector
                def _(v, e=e, hh=hh, gw=gw):
                    yb = y_blk[gw % 2]
                    for ii in range(4):
                        ig = hh * 4 + ii
                        ss(v, v.tensor_reduce(
                            stat[:, 64 + e * 8 + ig:65 + e * 8 + ig],
                            yb[:, ii * 512:(ii + 1) * 512],
                            axis=AX.X, op=ALU.max), "ve")
                        ss(v, v.tensor_reduce(
                            stat[:, 96 + e * 8 + ig:97 + e * 8 + ig],
                            yb[:, ii * 512:(ii + 1) * 512],
                            axis=AX.X, op=ALU.min), "ve")
                    inc(v.engine_nop(), "ve", 1, f"yrd{gw}")

                @blk.scalar
                def _(s_, e=e, hh=hh, gw=gw):
                    w(s_, f"yev{gw}")
                    if gw >= 1:
                        w(s_, f"yst{gw - 1}")
                    yb = y_blk[gw % 2]
                    act = None
                    for ii in range(4):
                        ig = hh * 4 + ii
                        ss(s_, s_.activation(junk2[:],
                                             yb[:, ii * 512:(ii + 1) * 512],
                                             AF.Copy,
                                             accum_out=stat[:, e * 8 + ig:
                                                            e * 8 + ig + 1]), "sc")
                        act = s_.activation(junk2[:],
                                            yb[:, ii * 512:(ii + 1) * 512],
                                            AF.Square,
                                            accum_out=stat[:, 32 + e * 8 + ig:
                                                           33 + e * 8 + ig])
                        if ii < 3:
                            ss(s_, act, "sc")
                    inc(act, "sc", 1, f"yst{gw}")

                @blk.sync
                def _(sync, e=e, hh=hh, gw=gw):
                    w(sync, f"yev{gw}")
                    for ii in range(4):
                        i = hh * 4 + ii
                        inc(sync.dma_start(
                            out=o_y[i * 128:(i + 1) * 128, e * 512:(e + 1) * 512],
                            in_=y_blk[gw % 2][:, ii * 512:(ii + 1) * 512]),
                            f"sy{gw % 2}", 16, f"ysw{gw}" if ii == 3 else None)

        if stage <= 5:
            _stop()
            return nc

        @blk.vector
        def _(v):
            w(v, "yst7")
            w(v, "yrd7")
            for base, op, dst in ((0, ALU.add, 0), (32, ALU.add, 8),
                                  (64, ALU.max, 16), (96, ALU.min, 24)):
                ss(v, v.tensor_tensor(stat3[:, dst:dst + 8],
                                      stat[:, base:base + 8],
                                      stat[:, base + 8:base + 16], op), "ve")
                ss(v, v.tensor_tensor(stat3[:, 32 + dst:40 + dst],
                                      stat[:, base + 16:base + 24],
                                      stat[:, base + 24:base + 32], op), "ve")
                ss(v, v.tensor_tensor(stat3[:, dst:dst + 8],
                                      stat3[:, dst:dst + 8],
                                      stat3[:, 32 + dst:40 + dst], op), "ve")
            ss(v, v.tensor_scalar(stat3[:, 32:40], stat3[:, 0:8], 1.0 / D, None,
                                  ALU.mult, ALU.bypass), "ve")
            ss(v, v.tensor_scalar(stat3[:, 40:48], stat3[:, 8:16], 1.0 / D, None,
                                  ALU.mult, ALU.bypass), "ve")
            ss(v, v.tensor_tensor(stat3[:, 48:56], stat3[:, 32:40],
                                  stat3[:, 32:40], ALU.mult), "ve")
            ss(v, v.tensor_tensor(stat3[:, 40:48], stat3[:, 40:48],
                                  stat3[:, 48:56], ALU.subtract), "ve")
            inc(v.tensor_scalar(stat3[:, 40:48], stat3[:, 40:48], EPS, None,
                                ALU.add, ALU.bypass), "ve", 1, "yvar")

        @blk.scalar
        def _(s_):
            w(s_, "yvar")
            ss(s_, s_.activation(stat3[:, 48:56], stat3[:, 40:48], AF.Ln), "sc")
            inc(s_.activation(stat3[:, 48:56], stat3[:, 48:56], AF.Exp, scale=-0.5),
                "sc", 1, "yrstd")

        @blk.vector
        def _(v):
            w(v, "yrstd")
            ss(v, v.tensor_tensor(stat3[:, 56:64], stat3[:, 16:24],
                                  stat3[:, 32:40], ALU.subtract), "ve")
            ss(v, v.tensor_tensor(stat3[:, 16:24], stat3[:, 32:40],
                                  stat3[:, 24:32], ALU.subtract), "ve")
            ss(v, v.tensor_tensor(stat3[:, 56:64], stat3[:, 56:64],
                                  stat3[:, 16:24], ALU.max), "ve")
            ss(v, v.tensor_tensor(stat3[:, 56:64], stat3[:, 56:64],
                                  stat3[:, 48:56], ALU.mult), "ve")
            inc(v.reduce_max(stat3[:, 62:63], stat3[:, 56:64], axis=AX.X),
                "ve", 1, "xam_col")

        @blk.gpsimd
        def _(g):
            w(g, "xam_col")
            inc(g.tensor_reduce(stat3[0:1, 63:64], stat3[:, 62:63], axis=AX.C,
                                op=ALU.max), "po", 1, "xam")

        @blk.sync
        def _(sync):
            w(sync, "xam")
            inc(sync.dma_start(out=o_xam[0:1, :], in_=stat3[0:1, 63:64]),
                "stq", 16, "xam_out")

        nc.leave_named_scope("l2_wo", _sc, False)
        _sc = nc.enter_named_scope("l2_w8", False)[0]
        @blk.sync
        def _(sync):
            sync.wait_ge(P.sems["stq"], P.cnt["stq"])
            sync.wait_ge(P.sems["stw0"], P.cnt["stw0"])
            sync.wait_ge(P.sems["stw1"], P.cnt["stw1"])

        nc.leave_named_scope("l2_w8", _sc, False)

    return nc



def build_L3():
    nc = bass.Bass("TRN2", target_bir_lowering=False)
    y_in = nc.dram_tensor("y", [S, D], F32, kind="ExternalInput").ap()
    w8st = nc.dram_tensor("w8st", [64 * 16 * 128, 128], F8, kind="ExternalInput").ap()
    xam8 = nc.dram_tensor("xam8", [1, 8], F32, kind="ExternalInput").ap()
    o_hr = nc.dram_tensor("hrelu", [FF, S], BF16, kind="ExternalOutput").ap()
    o_ham = nc.dram_tensor("ham", [1, 1], F32, kind="ExternalOutput").ap()

    with ExitStack() as st:
        sb = lambda n, s, d=F32: st.enter_context(nc.sbuf_tensor(n, s, d))
        psum = lambda n, s, d=F32: st.enter_context(nc.psum_tensor(n, s, d))

        y_sb = sb("y3", [128, 8 * D])
        x8t = [sb(f"x8t{i}", [128, D], F8) for i in range(2)]
        x8b = [sb(f"x8b{i}", [128, D], BF16) for i in range(2)]
        x8fm = sb("x8fm", [128, 16 * S], F8)
        wst = [sb(f"wst{i}", [128, 16 * 128], F8) for i in range(2)]
        hst = [sb(f"hst{i}", [128, S], BF16) for i in range(2)]
        hamc = sb("hamc", [128, 66])
        stat = sb("stat3", [128, 64])
        xamb = sb("xamb", [128, 8])
        junk = sb("junk3", [128, D])
        ident32 = sb("ident3", [128, 128])
        identb = sb("identb3", [128, 128], BF16)

        psF = [psum("psF0", [128, 1024]), psum("psF1", [128, 1024])]
        psD = [psum("psD0", [128, 1024], BF16),
               psum("psD1", [128, 1024], BF16)]

        P = _Prog(nc, st, ("sq", "gq", "pe", "sc", "ve", "po",
                           "y0", "y1", "y2", "y3", "y4", "y5", "y6", "y7",
                           "wt0", "wt1", "st0", "st1", "sto"))
        inc, w, ss = P.inc, P.w, P.ss
        blk = st.enter_context(nc.Block())

        @blk.gpsimd
        def _(g):
            inc(g.memset(ident32[:], 0.0), "po", 1, "id0")
            w(g, "id0")
            inc(g.affine_select(out=ident32[:], in_=ident32[:], compare_op=ALU.not_equal,
                                fill=1.0, base=0, pattern=[[-1, 128]],
                                channel_multiplier=1), "po", 1, "idf")

        @blk.vector
        def _(v):
            w(v, "idf")
            inc(v.tensor_copy(identb[:], ident32[:]), "ve", 1, "identb")

        @blk.sync
        def _(sync):
            inc(sync.dma_start(out=xamb[:], in_=xam8[0:1, :].broadcast_to([128, 8])),
                "sq", 16, "xld")
            for i in range(8):
                inc(sync.dma_start(out=y_sb[:, i * D:(i + 1) * D],
                                   in_=y_in[i * 128:(i + 1) * 128, :]),
                    f"y{i}", 16, f"yld{i}")

        _sc = nc.enter_named_scope("l3_stats", False)[0]
        for i in range(8):
            @blk.scalar
            def _(s_, i=i):
                w(s_, f"yld{i}")
                if i > 0:
                    w(s_, f"yst{i - 1}")
                ss(s_, s_.activation(junk[:], y_sb[:, i * D:(i + 1) * D],
                                     AF.Copy, accum_out=stat[:, i:i + 1]), "sc")
                inc(s_.activation(junk[:], y_sb[:, i * D:(i + 1) * D],
                                  AF.Square, accum_out=stat[:, 8 + i:9 + i]),
                    "sc", 1, f"yst{i}")

        @blk.vector
        def _(v):
            w(v, "yst7")
            ss(v, v.tensor_scalar(stat[:, 16:24], stat[:, 0:8], 1.0 / D, None,
                                  ALU.mult, ALU.bypass), "ve")
            ss(v, v.tensor_scalar(stat[:, 24:32], stat[:, 8:16], 1.0 / D, None,
                                  ALU.mult, ALU.bypass), "ve")
            ss(v, v.tensor_tensor(stat[:, 48:56], stat[:, 16:24], stat[:, 16:24],
                                  ALU.mult), "ve")
            ss(v, v.tensor_tensor(stat[:, 24:32], stat[:, 24:32], stat[:, 48:56],
                                  ALU.subtract), "ve")
            inc(v.tensor_scalar(stat[:, 24:32], stat[:, 24:32], EPS, None,
                                ALU.add, ALU.bypass), "ve", 1, "var3")

        @blk.scalar
        def _(s_):
            w(s_, "var3")
            ss(s_, s_.activation(stat[:, 32:40], stat[:, 24:32], AF.Ln), "sc")
            inc(s_.activation(stat[:, 32:40], stat[:, 32:40], AF.Exp, scale=-0.5),
                "sc", 1, "rstd3")

        @blk.vector
        def _(v):
            w(v, "rstd3")
            w(v, "xld")
            ss(v, v.reduce_max(stat[:, 56:57], xamb[:, 0:8], axis=AX.X), "ve")
            ss(v, v.tensor_scalar(stat[:, 57:58], stat[:, 56:57], 1.0 / 448.0, 1e-6,
                                  ALU.mult, ALU.max), "ve")
            ss(v, v.reciprocal(stat[:, 58:59], stat[:, 57:58]), "ve")
            ss(v, v.tensor_scalar(stat[:, 58:59], stat[:, 58:59], 0.5, None,
                                  ALU.mult, ALU.bypass), "ve")
            ss(v, v.tensor_tensor(stat[:, 40:48], stat[:, 32:40],
                                  stat[:, 58:59].broadcast_to([128, 8]),
                                  ALU.mult), "ve")
            ss(v, v.tensor_tensor(stat[:, 48:56], stat[:, 16:24], stat[:, 40:48],
                                  ALU.mult), "ve")
            inc(v.tensor_scalar(stat[:, 48:56], stat[:, 48:56], -1.0, None,
                                ALU.mult, ALU.bypass), "ve", 1, "sig3")

        nc.leave_named_scope("l3_stats", _sc, False)
        _sc = nc.enter_named_scope("l3_qtr", False)[0]
        for i in range(8):
            @blk.scalar
            def _(s_, i=i):
                if i == 0:
                    w(s_, "sig3")
                if i >= 2:
                    w(s_, f"cst{i - 2}")
                inc(s_.activation(x8t[i % 2][:], y_sb[:, i * D:(i + 1) * D],
                                  AF.Identity, scale=stat[:, 40 + i:41 + i],
                                  bias=stat[:, 48 + i:49 + i]), "sc", 1, f"xq{i}")

            @blk.vector
            def _(v, i=i):
                w(v, f"xq{i}")
                if i >= 2:
                    w(v, f"trch{i - 2}")
                inc(v.tensor_copy(x8b[i % 2][:], x8t[i % 2][:]), "ve", 1, f"cst{i}")

            for k in range(16):
                n = i * 16 + k

                @blk.tensor
                def _(t, i=i, k=k, n=n):
                    if n == 0:
                        w(t, "identb")
                    if k == 0:
                        w(t, f"cst{i}")
                    if n >= 2:
                        w(t, f"tev{n - 2}")
                    inc(t.transpose(psD[n % 2][:, 0:128],
                                    x8b[i % 2][:, k * 128:(k + 1) * 128],
                                    identb[:]), "pe", 1,
                        f"trch{i}" if k == 15 else f"tr{n}")

                @blk.vector
                def _(v, i=i, k=k, n=n):
                    w(v, f"trch{i}" if k == 15 else f"tr{n}")
                    inc(v.tensor_copy(x8fm[:, k * S + i * 128:k * S + (i + 1) * 128],
                                      psD[n % 2][:, 0:128]), "ve", 1, f"tev{n}")

        P.M["x8done"] = P.M["tev127"]

        nc.leave_named_scope("l3_qtr", _sc, False)
        _sc = nc.enter_named_scope("l3_fc", False)[0]
        for ff in range(64):
            @blk.gpsimd
            def _(g, ff=ff):
                if ff >= 2:
                    w(g, f"fcm{ff - 2}")
                inc(g.dma_start(out=wst[ff % 2][:].rearrange("p (k f) -> p k f", f=128),
                                in_=w8st[ff * 2048:(ff + 1) * 2048, :].rearrange(
                                    "(p k) f -> p k f", k=16)),
                    f"wt{ff % 2}", 16, f"wtld{ff}")

            @blk.tensor
            def _(t, ff=ff):
                w(t, f"wtld{ff}")
                if ff == 0:
                    w(t, "x8done")
                if ff >= 2:
                    w(t, f"fcv{ff - 2}")
                mm = None
                for kk in range(8):
                    for hf in range(2):
                        lhs3 = wst[ff % 2][:, kk * 256:(kk + 1) * 256].rearrange(
                            "p (i m) -> p i m", i=2)
                        rhs3 = x8fm[:, 2 * kk * S:(2 * kk + 2) * S].rearrange(
                            "p (i t) -> p i t", i=2)[:, :, hf * 512:(hf + 1) * 512]
                        mm = t.matmul(psF[ff % 2][:, hf * 512:(hf + 1) * 512],
                                      lhs3, rhs3, start=(kk == 0), stop=(kk == 7),
                                      perf_mode=mybir.MatmulPerfMode.DoubleRow,
                                      skip_group_check=True)
                inc(mm, "pe", 1, f"fcm{ff}")

            @blk.vector
            def _(v, ff=ff):
                w(v, f"fcm{ff}")
                if ff >= 2:
                    w(v, f"hsw{ff - 2}")
                ss(v, v.tensor_scalar(hst[ff % 2][:], psF[ff % 2][:], 0.0, None,
                                      ALU.max, ALU.bypass), "ve")
                inc(v.tensor_reduce(hamc[:, ff:ff + 1], psF[ff % 2][:],
                                    axis=AX.X, op=ALU.max), "ve", 1, f"fcv{ff}")

            @blk.sync
            def _(sync, ff=ff):
                w(sync, f"fcv{ff}")
                inc(sync.dma_start(out=o_hr[ff * 128:(ff + 1) * 128, :],
                                   in_=hst[ff % 2][:]),
                    f"st{ff % 2}", 16, f"hsw{ff}")

        nc.leave_named_scope("l3_fc", _sc, False)
        _sc = nc.enter_named_scope("l3_ham", False)[0]
        @blk.vector
        def _(v):
            w(v, "fcv63")
            ss(v, v.reduce_max(hamc[:, 64:65], hamc[:, 0:64], axis=AX.X), "ve")
            inc(v.tensor_scalar(hamc[:, 64:65], hamc[:, 64:65], 0.0, None,
                                ALU.max, ALU.bypass), "ve", 1, "hcol")

        @blk.gpsimd
        def _(g):
            w(g, "hcol")
            inc(g.tensor_reduce(hamc[0:1, 65:66], hamc[:, 64:65], axis=AX.C,
                                op=ALU.max), "po", 1, "hram")

        @blk.sync
        def _(sync):
            w(sync, "hram")
            inc(sync.dma_start(out=o_ham[:, :], in_=hamc[0:1, 65:66]), "sto", 16)
            for ch in P.cnt:
                if P.cnt[ch]:
                    sync.wait_ge(P.sems[ch], P.cnt[ch])

        nc.leave_named_scope("l3_ham", _sc, False)

    return nc



def build_L4():
    nc = bass.Bass("TRN2", target_bir_lowering=False)
    y_in = nc.dram_tensor("y", [S, D], F32, kind="ExternalInput").ap()
    hr_in = nc.dram_tensor("hrelu", [FF, S], BF16, kind="ExternalInput").ap()
    w8pj = nc.dram_tensor("w8pj", [FF, D], F8, kind="ExternalInput").ap()
    ham8 = nc.dram_tensor("ham8", [1, 8], F32, kind="ExternalInput").ap()
    xam8 = nc.dram_tensor("xam8", [1, 8], F32, kind="ExternalInput").ap()
    wam = nc.dram_tensor("wam", [1, 16], F32, kind="ExternalInput").ap()
    o_out = nc.dram_tensor("out", [S, D], F32, kind="ExternalOutput").ap()

    with ExitStack() as st:
        sb = lambda n, s, d=F32: st.enter_context(nc.sbuf_tensor(n, s, d))
        psum = lambda n, s, d=F32: st.enter_context(nc.psum_tensor(n, s, d))

        y_sb = sb("y4", [128, 8 * D])
        h8 = sb("h8", [128, 64 * S], F8)
        hr = [sb(f"hr{i}", [128, S], BF16) for i in range(3)]
        wpj = sb("wpj", [128, 64 * S], F8)
        osb = [sb(f"osb{i}", [128, S]) for i in range(2)]
        scal = sb("scal", [128, 40])
        psP = [psum("psP0", [128, 1024]), psum("psP1", [128, 1024])]

        P = _Prog(nc, st, ("sq", "yq", "gq", "pe", "sc", "ve", "po",
                           "hq0", "hq1", "hq2", "hq3", "wq0", "wq1",
                           "st0", "st1"))
        inc, w, ss = P.inc, P.w, P.ss
        blk = st.enter_context(nc.Block())
        _preload_tables(nc, blk)

        @blk.sync
        def _(sync):
            inc(sync.dma_start(out=scal[:, 0:8],
                               in_=xam8[0:1, :].broadcast_to([128, 8])), "sq", 16)
            inc(sync.dma_start(out=scal[:, 8:16],
                               in_=ham8[0:1, :].broadcast_to([128, 8])), "sq", 16)
            inc(sync.dma_start(out=scal[:, 16:32],
                               in_=wam[0:1, :].broadcast_to([128, 16])),
                "sq", 16, "sld")
            for i in range(8):
                inc(sync.dma_start(out=y_sb[:, i * D:(i + 1) * D],
                                   in_=y_in[i * 128:(i + 1) * 128, :]),
                    "yq", 16, f"yld{i}" if i == 7 else None)

        @blk.vector
        def _(v):
            w(v, "sld")
            ss(v, v.reduce_max(scal[:, 32:33], scal[:, 0:8], axis=AX.X), "ve")
            ss(v, v.reduce_max(scal[:, 33:34], scal[:, 8:16], axis=AX.X), "ve")
            ss(v, v.reduce_max(scal[:, 34:35], scal[:, 16:24], axis=AX.X), "ve")
            ss(v, v.reduce_max(scal[:, 35:36], scal[:, 24:32], axis=AX.X), "ve")
            ss(v, v.tensor_scalar(scal[:, 32:33], scal[:, 32:33], 1.0 / 448.0, 1e-6,
                                  ALU.mult, ALU.max), "ve")
            ss(v, v.tensor_scalar(scal[:, 34:35], scal[:, 34:35], 1.0 / 448.0, 1e-6,
                                  ALU.mult, ALU.max), "ve")
            ss(v, v.tensor_scalar(scal[:, 35:36], scal[:, 35:36], 1.0 / 448.0, 1e-6,
                                  ALU.mult, ALU.max), "ve")
            ss(v, v.tensor_tensor(scal[:, 36:37], scal[:, 32:33], scal[:, 34:35],
                                  ALU.mult), "ve")
            ss(v, v.tensor_scalar(scal[:, 36:37], scal[:, 36:37], 4.0, None,
                                  ALU.mult, ALU.bypass), "ve")
            ss(v, v.tensor_tensor(scal[:, 37:38], scal[:, 36:37], scal[:, 33:34],
                                  ALU.mult), "ve")
            ss(v, v.tensor_tensor(scal[:, 37:38], scal[:, 37:38], scal[:, 37:38],
                                  ALU.mult), "ve")
            ss(v, v.tensor_scalar(scal[:, 37:38], scal[:, 37:38], 1.0 / 448.0, 1e-6,
                                  ALU.mult, ALU.max), "ve")
            ss(v, v.tensor_scalar(scal[:, 38:39], scal[:, 37:38], 2.0, None,
                                  ALU.mult, ALU.bypass), "ve")
            inc(v.reciprocal(scal[:, 38:39], scal[:, 38:39]), "ve", 1, "rech")

        @blk.scalar
        def _(s_):
            w(s_, "rech")
            ss(s_, s_.activation(scal[:, 38:39], scal[:, 38:39], AF.Ln), "sc")
            inc(s_.activation(scal[:, 38:39], scal[:, 38:39], AF.Exp, scale=0.5),
                "sc", 1, "sqh")

        @blk.vector
        def _(v):
            w(v, "sqh")
            ss(v, v.tensor_tensor(scal[:, 38:39], scal[:, 38:39], scal[:, 36:37],
                                  ALU.mult), "ve")
            ss(v, v.tensor_tensor(scal[:, 39:40], scal[:, 37:38], scal[:, 35:36],
                                  ALU.mult), "ve")
            inc(v.tensor_scalar(scal[:, 39:40], scal[:, 39:40], 4.0, None,
                                ALU.mult, ALU.bypass), "ve", 1, "deqs")

        _sc = nc.enter_named_scope("l4_h8", False)[0]
        for c in range(64):
            @blk.gpsimd
            def _(g, c=c):
                if c >= 3:
                    w(g, f"h8q{c - 3}")
                inc(g.dma_start(out=hr[c % 3][:],
                                in_=hr_in[c * 128:(c + 1) * 128, :]),
                    f"hq{c % 3}", 16, f"hld{c}")

            @blk.scalar
            def _(s_, c=c):
                w(s_, f"hld{c}")
                if c == 0:
                    w(s_, "deqs")
                inc(s_.activation(h8[:, c * S:(c + 1) * S], hr[c % 3][:],
                                  AF.Square, scale=scal[:, 38:39]),
                    "sc", 1, f"h8q{c}")

        nc.leave_named_scope("l4_h8", _sc, False)
        _sc = nc.enter_named_scope("l4_proj", False)[0]
        for dh in range(2):
            for c in range(64):
                @blk.sync
                def _(sync, dh=dh, c=c):
                    if dh == 1 and c % 16 == 0:
                        w(sync, f"pjp{c // 16 + 1}" if c < 32 else "pjm7")
                    inc(sync.dma_start(out=wpj[:, c * S:(c + 1) * S],
                                       in_=w8pj[c * 128:(c + 1) * 128,
                                                dh * 1024:(dh + 1) * 1024]),
                        f"wq{dh % 2}", 16, f"wld{dh}_{c}" if c == 63 else None)

            for tt in range(8):
                g_ = dh * 8 + tt

                @blk.tensor
                def _(t, dh=dh, tt=tt, g_=g_):
                    if tt == 0:
                        w(t, f"wld{dh}_63")
                    if g_ >= 2:
                        w(t, f"pjv{g_ - 2}")
                    mm = None
                    h8v = h8[:].rearrange("p (c t) -> p c t", t=S)
                    wpv = wpj[:].rearrange("p (c t) -> p c t", t=S)
                    for cc in range(32):
                        if g_ == 0:
                            w(t, f"h8q{2 * cc + 1}")
                        for hf in range(2):
                            mm = t.matmul(
                                psP[g_ % 2][:, hf * 512:(hf + 1) * 512],
                                h8v[:, 2 * cc:2 * cc + 2, tt * 128:(tt + 1) * 128],
                                wpv[:, 2 * cc:2 * cc + 2, hf * 512:(hf + 1) * 512],
                                start=(cc == 0), stop=(cc == 31),
                                perf_mode=mybir.MatmulPerfMode.DoubleRow,
                                skip_group_check=True)
                        if g_ == 7 and cc % 8 == 7 and cc < 31:
                            inc(mm, "pe", 1, f"pjp{cc // 8}")
                    inc(mm, "pe", 1, f"pjm{g_}")

                @blk.vector
                def _(v, dh=dh, tt=tt, g_=g_):
                    w(v, f"pjm{g_}")
                    if g_ == 0:
                        w(v, "yld7")
                    if g_ >= 2:
                        w(v, f"osw{g_ - 2}")
                    inc(v.scalar_tensor_tensor(
                        osb[g_ % 2][:], psP[g_ % 2][:], scal[:, 39:40],
                        y_sb[:, tt * D + dh * 1024:tt * D + (dh + 1) * 1024],
                        ALU.mult, ALU.add), "ve", 1, f"pjv{g_}")

                @blk.sync
                def _(sync, dh=dh, tt=tt, g_=g_):
                    w(sync, f"pjv{g_}")
                    inc(sync.dma_start(out=o_out[tt * 128:(tt + 1) * 128,
                                                 dh * 1024:(dh + 1) * 1024],
                                       in_=osb[g_ % 2][:]),
                        f"st{g_ % 2}", 16, f"osw{g_}")

        @blk.sync
        def _(sync):
            for ch in P.cnt:
                if P.cnt[ch]:
                    sync.wait_ge(P.sems[ch], P.cnt[ch])

        nc.leave_named_scope("l4_proj", _sc, False)

    return nc


def kernel(**inputs):
    x = np.ascontiguousarray(
        np.asarray(inputs["L0_attn_x"], np.float32).reshape(8 * S, D))
    Wq = np.asarray(inputs["L0_attn_Wq"], np.float32)
    Wk = np.asarray(inputs["L0_attn_Wk"], np.float32)
    Wv = np.asarray(inputs["L0_attn_Wv"], np.float32)
    Wo = np.asarray(inputs["L0_attn_Wo"], np.float32)
    wfcT = np.ascontiguousarray(np.asarray(inputs["L0_ffn_w_fc"], np.float32).T)
    wprojT = np.ascontiguousarray(np.asarray(inputs["L0_ffn_w_proj"], np.float32).T)
    wqT = np.ascontiguousarray(Wq.T)
    wkT = np.ascontiguousarray(Wk.T)
    wvT = np.ascontiguousarray(Wv.T)
    woT = np.ascontiguousarray(Wo.T)
    xs = [np.ascontiguousarray(x[c * S:(c + 1) * S]) for c in range(8)]
    wfc_sls = [np.ascontiguousarray(wfcT[:, c * S:(c + 1) * S]) for c in range(8)]
    wpj_sls = [np.ascontiguousarray(wprojT[:, c * 256:(c + 1) * 256])
               for c in range(8)]

    in1 = [{"x": xs[c], "wqT": wqT, "wkT": wkT, "wvT": wvT,
            "wfc_sl": wfc_sls[c], "wproj_sl": wpj_sls[c]} for c in range(8)]
    r1 = run_bass_kernel_spmd(build_L1(), in1, core_ids=list(range(8))).results

    wam = np.concatenate([np.asarray(r1[c]["wamax"]).reshape(2) for c in range(8)])
    wam = np.ascontiguousarray(
        np.stack([wam[0::2], wam[1::2]]).reshape(1, 16).astype(np.float32))
    in2 = []
    for c in range(8):
        grp = range(0, 4) if c < 4 else range(4, 8)
        kt = np.zeros((H, 128), np.float32)
        kb = np.zeros((H, 128), np.float32)
        cvt = np.zeros((128, D), NB)
        cvb = np.zeros((128, D), NB)
        for kk, cc in enumerate(grp):
            cand = np.asarray(r1[cc]["cand"], np.float32)
            kt[:, kk * 32:(kk + 1) * 32] = cand[:, :32]
            kb[:, kk * 32:(kk + 1) * 32] = cand[:, 32:]
            pv = np.asarray(r1[cc]["payV"], NB).reshape(2, 2, 32, 64, 16)
            for sel, dst in ((0, cvt), (1, cvb)):
                arr = pv[:, sel].transpose(1, 2, 0, 3)
                dst[kk * 32:(kk + 1) * 32, :] = arr.reshape(32, D)
        in2.append({"x": xs[c], "q": np.ascontiguousarray(r1[c]["q"]),
                    "kt": kt, "kb": kb, "cvt": cvt, "cvb": cvb, "woT": woT,
                    "wfc_sl": wfc_sls[c], "wproj_sl": wpj_sls[c], "wam": wam})

    r2 = run_bass_kernel_spmd(build_L2(), in2, core_ids=list(range(8))).results

    ys = [np.ascontiguousarray(np.asarray(r2[c]["y"], np.float32))
          for c in range(8)]
    xam8 = np.ascontiguousarray(np.concatenate(
        [np.asarray(r2[c]["xamax"], np.float32).reshape(1) for c in range(8)]
    ).reshape(1, 8))
    w8fc = np.concatenate([np.asarray(r2[c]["w8fc_sl"], NF8) for c in range(8)],
                          axis=1)
    w8pj = np.ascontiguousarray(np.concatenate(
        [np.asarray(r2[c]["w8proj_sl"], NF8) for c in range(8)], axis=1))
    w8st = np.ascontiguousarray(
        w8fc.reshape(16, 128, 64, 128).transpose(2, 1, 0, 3).reshape(
            64 * 128 * 16, 128))

    in3 = [{"y": ys[c], "w8st": w8st, "xam8": xam8} for c in range(8)]
    r3 = run_bass_kernel_spmd(build_L3(), in3, core_ids=list(range(8))).results

    ham8 = np.ascontiguousarray(np.concatenate(
        [np.asarray(r3[c]["ham"], np.float32).reshape(1) for c in range(8)]
    ).reshape(1, 8))

    in4 = [{"y": ys[c],
            "hrelu": np.ascontiguousarray(np.asarray(r3[c]["hrelu"], NB)),
            "w8pj": w8pj, "ham8": ham8, "xam8": xam8, "wam": wam}
           for c in range(8)]
    r4 = run_bass_kernel_spmd(build_L4(), in4, core_ids=list(range(8))).results

    out = np.concatenate([np.asarray(r4[c]["out"], np.float32) for c in range(8)],
                         axis=0)
    return np.ascontiguousarray(out.reshape(2, 4 * S, D))


# version 09
# speedup vs baseline: 1.5321x; 1.0091x over previous
import numpy as np
import ml_dtypes
import concourse.bass as bass
import concourse.mybir as mybir
from concourse.alu_op_type import AluOpType as ALU
from concourse.bass_utils import run_bass_kernel_spmd
from contextlib import ExitStack

F32 = mybir.dt.float32
BF16 = mybir.dt.bfloat16
F16 = mybir.dt.float16
F8 = mybir.dt.float8e4
U16 = mybir.dt.uint16
AF = mybir.ActivationFunctionType
AX = mybir.AxisListType

NB = ml_dtypes.bfloat16
NF8 = ml_dtypes.float8_e4m3

S = 1024
D = 2048
H = 128
FF = 8192
EPS = 1e-5
NEG = -1.0e30


def _preload_one(nc, s_):
    try:
        from concourse.hw_specs import get_activation_tables
        need = (AF.Exp, AF.Ln, AF.Copy, AF.Identity, AF.Square, AF.Relu)
        set_id = None
        for idx, funcs in enumerate(get_activation_tables("TRN2").values()):
            if all(f in funcs for f in need):
                set_id = idx
                break
    except Exception:
        set_id = 6
    if set_id is not None:
        s_.add_instruction(mybir.InstLoadActFuncSet(
            name=nc.get_next_instruction_name(),
            act_func_set_id=set_id, ins=[], outs=[]))


def _preload_tables(nc, blk):
    try:
        from concourse.hw_specs import get_activation_tables
        need = (AF.Exp, AF.Ln, AF.Copy, AF.Identity, AF.Square, AF.Relu)
        set_id = None
        for idx, funcs in enumerate(get_activation_tables("TRN2").values()):
            if all(f in funcs for f in need):
                set_id = idx
                break
    except Exception:
        set_id = 6
    if set_id is None:
        return

    @blk.scalar
    def _(s_):
        s_.add_instruction(mybir.InstLoadActFuncSet(
            name=nc.get_next_instruction_name(),
            act_func_set_id=set_id, ins=[], outs=[]))


class _Prog:

    def __init__(self, nc, st, chans):
        self.nc = nc
        self.sems = {k: st.enter_context(nc.semaphore(k)) for k in chans}
        self.cnt = {k: 0 for k in chans}
        self.M = {}

    def inc(self, instr, ch, n, mark=None):
        instr.then_inc(self.sems[ch], n)
        self.cnt[ch] += n
        if mark:
            self.M[mark] = (ch, self.cnt[ch])
        return self.cnt[ch]

    def w(self, eng, mark):
        ch, val = self.M[mark]
        eng.wait_ge(self.sems[ch], val)

    def ss(self, eng, instr, ch):
        instr.then_inc(self.sems[ch], 1)
        self.cnt[ch] += 1
        eng.wait_ge(self.sems[ch], self.cnt[ch])



def build_L1():
    nc = bass.Bass("TRN2", target_bir_lowering=False)
    x = nc.dram_tensor("x", [S, D], F32, kind="ExternalInput").ap()
    wqT = nc.dram_tensor("wqT", [D, H], F32, kind="ExternalInput").ap()
    wkT = nc.dram_tensor("wkT", [D, H], F32, kind="ExternalInput").ap()
    wvT = nc.dram_tensor("wvT", [D, D], F32, kind="ExternalInput").ap()
    wfc_sl = nc.dram_tensor("wfc_sl", [D, S], F32, kind="ExternalInput").ap()
    wproj_sl = nc.dram_tensor("wproj_sl", [FF, 256], F32, kind="ExternalInput").ap()

    o_q = nc.dram_tensor("q", [H, S], F32, kind="ExternalOutput").ap()
    o_cand = nc.dram_tensor("cand", [H, 64], F32, kind="ExternalOutput").ap()
    o_payV = nc.dram_tensor("payV", [128, 1024], BF16, kind="ExternalOutput").ap()
    o_wamax = nc.dram_tensor("wamax", [1, 2], F32, kind="ExternalOutput").ap()

    idx_scr = nc.dram_tensor("idx_scr", [H, 64], F32)

    with ExitStack() as st:
        sb = lambda n, s, d=F32: st.enter_context(nc.sbuf_tensor(n, s, d))
        psum = lambda n, s: st.enter_context(nc.psum_tensor(n, s, F32))

        x_ch = [sb(f"x{i}", [128, D]) for i in range(2)]
        lnx_ch = [sb(f"lnxc{i}", [128, D]) for i in range(2)]
        stg = [sb(f"stg{i}", [128, 512]) for i in range(2)]
        lnb = sb("lnb", [128, 16 * S], BF16)
        wq_t = sb("wq_t", [128, 16 * H])
        wk_t = sb("wk_t", [128, 16 * H])
        wv_ch = [sb(f"wv{i}", [128, 512], BF16) for i in range(2)]
        v_tm = sb("v_tm", [128, 8 * D], BF16)
        q_sb = sb("q_sbuf", [H, S])
        k_sb = sb("k_sb", [H, S])
        negk = sb("negk", [H, S])
        work = [sb(f"wk_w{i}", [H, S]) for i in range(2)]
        cand = sb("cand_sb", [H, 64])
        idx16 = sb("idx16", [H, 64], U16)
        idxf = sb("idxf", [H, 64])
        idx_bc = sb("idx_bc", [128, 8192], F16)
        gt = [sb(f"gt{i}", [128, 8192], BF16) for i in range(2)]
        payV = sb("payV_sb", [128, 1024], BF16)
        iota_tok = sb("iota_tok", [128, 8])
        ident32 = sb("ident32", [128, 128])
        stat = sb("stat", [128, 64])
        wsl = [sb(f"wsl{i}", [128, 2048]) for i in range(2)]
        wparts = sb("wparts", [128, 64])
        wpool = sb("wpool", [1, 64])
        wamax_sb = sb("wamax_sb", [1, 2])

        psA = psum("psA", [128, 2048])
        psB = psum("psB", [128, 1024])
        psC = psum("psC", [128, 1024])

        P = _Prog(nc, st, ("sq", "gq", "aq", "pe", "sc", "ve", "po",
                   "x0", "x1", "wv0", "wv1", "ws0", "ws1", "stq"))
        inc, w, ss = P.inc, P.w, P.ss

        blk = st.enter_context(nc.Block())

        @blk.gpsimd
        def _(g):
            inc(g.memset(ident32[:], 0.0), "po", 1, "id0")
            g.iota(iota_tok[:], [[128, 8]], base=0, channel_multiplier=1,
                   allow_small_or_imprecise_dtypes=True)
            w(g, "id0")
            inc(g.affine_select(out=ident32[:], in_=ident32[:], compare_op=ALU.not_equal,
                                fill=1.0, base=0, pattern=[[-1, 128]],
                                channel_multiplier=1), "po", 1, "const")

        @blk.vector
        def _(v):
            inc(v.memset(payV[:], 0.0), "ve", 1, "pay0")

        @blk.sync
        def _(sync):
            for c in range(16):
                inc(sync.dma_start(out=wq_t[:, c * H:(c + 1) * H],
                                   in_=wqT[c * 128:(c + 1) * 128, :]),
                    "sq", 16, "wq" if c == 15 else None)
            for c in range(16):
                inc(sync.dma_start(out=wk_t[:, c * H:(c + 1) * H],
                                   in_=wkT[c * 128:(c + 1) * 128, :]),
                    "sq", 16, "wk" if c == 15 else None)

        _sc = nc.enter_named_scope("l1_stats", False)[0]
        for i in range(8):
            @blk.sync
            def _(sync, i=i):
                if i >= 2:
                    w(sync, f"stat{i - 2}")
                inc(sync.dma_start(out=x_ch[i % 2][:], in_=x[i * 128:(i + 1) * 128, :]),
                    f"x{i % 2}", 16, f"x{i}")

            @blk.scalar
            def _(s, i=i):
                w(s, f"x{i}")
                ss(s, s.activation(lnx_ch[i % 2][:], x_ch[i % 2][:], AF.Copy,
                                   accum_out=stat[:, i:i + 1]), "sc")
                inc(s.activation(lnx_ch[i % 2][:], x_ch[i % 2][:], AF.Square,
                                 accum_out=stat[:, 8 + i:9 + i]), "sc", 1, f"stat{i}")

        @blk.vector
        def _(v):
            w(v, "stat7")
            inv = 1.0 / D
            ss(v, v.tensor_scalar(stat[:, 16:24], stat[:, 0:8], inv, None, ALU.mult, ALU.bypass), "ve")
            ss(v, v.tensor_scalar(stat[:, 24:32], stat[:, 8:16], inv, None, ALU.mult, ALU.bypass), "ve")
            ss(v, v.tensor_tensor(stat[:, 48:56], stat[:, 16:24], stat[:, 16:24], ALU.mult), "ve")
            ss(v, v.tensor_tensor(stat[:, 24:32], stat[:, 24:32], stat[:, 48:56], ALU.subtract), "ve")
            inc(v.tensor_scalar(stat[:, 24:32], stat[:, 24:32], EPS, None, ALU.add,
                                ALU.bypass), "ve", 1, "var")

        @blk.scalar
        def _(s):
            w(s, "var")
            ss(s, s.activation(stat[:, 48:56], stat[:, 24:32], AF.Ln), "sc")
            inc(s.activation(stat[:, 32:40], stat[:, 48:56], AF.Exp, scale=-0.5),
                "sc", 1, "rstd")

        @blk.vector
        def _(v):
            w(v, "rstd")
            ss(v, v.tensor_tensor(stat[:, 56:64], stat[:, 16:24], stat[:, 32:40], ALU.mult), "ve")
            inc(v.tensor_scalar(stat[:, 40:48], stat[:, 56:64], -1.0, None, ALU.mult,
                                ALU.bypass), "ve", 1, "bias")

        nc.leave_named_scope("l1_stats", _sc, False)
        _sc = nc.enter_named_scope("l1_qk", False)[0]
        slots2 = [psB[:, 0:512], psC[:, 0:512]]

        def emit_qk_mms(t, g):
            i, g2 = divmod(g, 4)
            w(t, f"evf{g}")
            first0 = (i % 4 == 0 and g2 == 0)
            mm = None
            for j in range(4):
                c = g2 * 4 + j
                first = first0 and j == 0
                t.matmul(psA[:, i * 128:(i + 1) * 128], wq_t[:, c * H:(c + 1) * H],
                         stg[g % 2][:, j * 128:(j + 1) * 128],
                         start=first, stop=False, skip_group_check=True)
                mm = t.matmul(psA[:, 1024 + i * 128:1024 + (i + 1) * 128],
                              wk_t[:, c * H:(c + 1) * H],
                              stg[g % 2][:, j * 128:(j + 1) * 128],
                              start=first, stop=False, skip_group_check=True)
            inc(mm, "pe", 1, f"qkm{g}")

        for i in range(8):
            @blk.sync
            def _(sync, i=i):
                if i == 0:
                    w(sync, "stat6")
                elif i == 1:
                    w(sync, "stat7")
                else:
                    w(sync, f"lnxn{i - 2}")
                inc(sync.dma_start(out=x_ch[i % 2][:], in_=x[i * 128:(i + 1) * 128, :]),
                    f"x{i % 2}", 16, f"x2_{i}")

            @blk.scalar
            def _(s, i=i):
                w(s, f"x2_{i}")
                if i == 0:
                    w(s, "stat7")
                    w(s, "bias")
                if i >= 2:
                    w(s, f"trg{(i - 2) * 4 + 3}")
                inc(s.activation(lnx_ch[i % 2][:], x_ch[i % 2][:], AF.Identity,
                                 scale=stat[:, 32 + i:33 + i],
                                 bias=stat[:, 40 + i:41 + i]), "sc", 1, f"lnxn{i}")

            for g2 in range(4):
                g = i * 4 + g2

                @blk.tensor
                def _(t, i=i, g2=g2, g=g):
                    if g == 0:
                        w(t, "const")
                        w(t, "wk")
                    if g2 == 0:
                        w(t, f"lnxn{i}")
                    if g >= 2:
                        w(t, f"evf{g - 2}")
                        w(t, f"evb{g - 2}")
                    mm = None
                    for j in range(4):
                        c = g2 * 4 + j
                        mm = t.transpose(slots2[g % 2][:, j * 128:(j + 1) * 128],
                                         lnx_ch[i % 2][:, c * 128:(c + 1) * 128],
                                         ident32[:])
                    inc(mm, "pe", 1, f"trg{g}")

                @blk.vector
                def _(v, i=i, g2=g2, g=g):
                    w(v, f"trg{g}")
                    if g >= 2:
                        w(v, f"qkm{g - 2}")
                    ss(v, v.tensor_copy(stg[g % 2][:], slots2[g % 2]), "ve")
                    P.M[f"evf{g}"] = ("ve", P.cnt["ve"])
                    lnb_v = lnb[:].rearrange("p (c t) -> p c t", t=S)[
                        :, g2 * 4:g2 * 4 + 4, i * 128:(i + 1) * 128]
                    ps_v = slots2[g % 2].rearrange("p (c t) -> p c t", t=128)
                    inc(v.tensor_copy(lnb_v, ps_v), "ve", 1, f"evb{g}")

                @blk.tensor
                def _(t, g=g):
                    if g >= 1:
                        emit_qk_mms(t, g - 1)

        @blk.tensor
        def _(t):
            emit_qk_mms(t, 31)

        @blk.scalar
        def _(s):
            w(s, "qkm31")
            s.activation(q_sb[:], psA[:, 0:1024], AF.Copy)
            inc(s.activation(k_sb[:], psA[:, 1024:2048], AF.Copy),
                "sc", 1, "qk_evac")

        nc.leave_named_scope("l1_qk", _sc, False)
        _sc = nc.enter_named_scope("l1_v", False)[0]
        def emit_wam(kk):
            @blk.sync
            def _(s, kk=kk):
                if kk >= 2:
                    w(s, f"wred{kk - 2}")
                if kk < 8:
                    for j_ in range(2):
                        a = kk * 2 + j_
                        inc(s.dma_start(out=wsl[kk % 2][:, j_ * 1024:(j_ + 1) * 1024],
                                        in_=wfc_sl[a * 128:(a + 1) * 128, :]),
                            f"ws{kk % 2}", 16, f"wsl{kk}" if j_ == 1 else None)
                else:
                    for j_ in range(8):
                        a = (kk - 8) * 8 + j_
                        inc(s.dma_start(out=wsl[kk % 2][:, j_ * 256:(j_ + 1) * 256],
                                        in_=wproj_sl[a * 128:(a + 1) * 128, :]),
                            f"ws{kk % 2}", 16, f"wsl{kk}" if j_ == 7 else None)

            @blk.vector
            def _(v, kk=kk):
                w(v, f"wsl{kk}")
                if kk >= 1:
                    w(v, f"wred{kk - 1}")
                base = 0 if kk < 8 else 32
                j0 = kk if kk < 8 else kk - 8
                ss(v, v.tensor_scalar(negk[:, 0:S], wsl[kk % 2][:, 0:S], 0.0,
                                      None, ALU.max, ALU.max,
                                      accum_out=wparts[:, base + j0:base + j0 + 1]), "ve")
                ss(v, v.tensor_scalar(negk[:, 0:S], wsl[kk % 2][:, S:2 * S], 0.0,
                                      None, ALU.max, ALU.max,
                                      accum_out=wparts[:, base + 8 + j0:base + 9 + j0]), "ve")
                ss(v, v.tensor_scalar(negk[:, 0:S], wsl[kk % 2][:, 0:S], -1.0,
                                      None, ALU.mult, ALU.max,
                                      accum_out=wparts[:, base + 16 + j0:base + 17 + j0]), "ve")
                inc(v.tensor_scalar(negk[:, 0:S], wsl[kk % 2][:, S:2 * S], -1.0,
                                    None, ALU.mult, ALU.max,
                                    accum_out=wparts[:, base + 24 + j0:base + 25 + j0]),
                    "ve", 1, f"wred{kk}")


        @blk.vector
        def _(v):
            w(v, "qk_evac")
            ss(v, v.tensor_scalar(negk[:], k_sb[:], -1.0, None, ALU.mult, ALU.bypass), "ve")
            for sel, srcT in ((0, k_sb), (1, negk)):
                cur = srcT
                for p in range(4):
                    col = sel * 32 + p * 8
                    ss(v, v.max(cand[:, col:col + 8], cur[:]), "ve")
                    ss(v, v.max_index(idx16[:, col:col + 8], cand[:, col:col + 8], cur[:]), "ve")
                    if p < 3:
                        nxt = work[p % 2]
                        ss(v, v.match_replace(nxt[:], in_to_replace=cand[:, col:col + 8],
                                              in_values=cur[:], imm_value=NEG), "ve")
                        cur = nxt
            inc(v.tensor_scalar(idxf[:], idx16[:], 0.0, None, ALU.add, ALU.bypass),
                "ve", 1, "sel_done")


        vtiles = [psA[:, 0:512], psA[:, 512:1024], psA[:, 1024:1536], psA[:, 1536:2048],
                  psB[:, 0:512], psB[:, 512:1024], psC[:, 0:512], psC[:, 512:1024]]
        for t_ in range(4):
            for c in range(16):
                k_ = t_ * 16 + c

                @blk.gpsimd
                def _(g, t_=t_, c=c, k_=k_):
                    if k_ >= 2:
                        w(g, f"vmm{k_ - 2}")
                    inc(g.dma_start(out=wv_ch[k_ % 2][:],
                                    in_=wvT[c * 128:(c + 1) * 128,
                                            t_ * 512:(t_ + 1) * 512]),
                        f"wv{k_ % 2}", 16, f"wv{k_}")

                @blk.tensor
                def _(t, t_=t_, c=c, k_=k_):
                    w(t, f"wv{k_}")
                    if k_ == 0:
                        w(t, "qk_evac")
                        w(t, "evb31")
                    if c == 0 and t_ > 0:
                        w(t, f"vev{t_ - 1}")
                    for i in range(7):
                        t.matmul(vtiles[i],
                                 lnb[:, c * S + i * 128:c * S + (i + 1) * 128],
                                 wv_ch[k_ % 2][:], start=(c == 0), stop=(c == 15))
                    inc(t.matmul(vtiles[7],
                                 lnb[:, c * S + 7 * 128:c * S + 8 * 128],
                                 wv_ch[k_ % 2][:], start=(c == 0), stop=(c == 15)),
                        "pe", 1, f"vmm{k_}")

            @blk.vector
            def _(v, t_=t_):
                w(v, f"vmm{t_ * 16 + 15}")
                for i in range(8):
                    v.tensor_copy(v_tm[:, i * D + t_ * 512:i * D + (t_ + 1) * 512],
                                  vtiles[i])
                inc(v.engine_nop(), "ve", 1, f"vev{t_}")

            for _wk in range(4 * t_, 4 * t_ + 4):
                emit_wam(_wk)

        nc.leave_named_scope("l1_v", _sc, False)
        _sc = nc.enter_named_scope("l1_sel", False)[0]
        nc.leave_named_scope("l1_sel", _sc, False)
        _sc = nc.enter_named_scope("l1_pay", False)[0]
        @blk.gpsimd
        def _(g):
            w(g, "sel_done")
            inc(g.dma_start(out=idx_scr[:, :], in_=idxf[:]), "gq", 16, "idxw")
            w(g, "idxw")
            flat = idx_scr.rearrange("a b -> (a b)")
            inc(g.dma_start(out=idx_bc[:, :],
                            in_=flat[None, :].broadcast_to([128, 8192])),
                "gq", 16, "idxbc")

        for i in range(8):
            @blk.vector
            def _(v, i=i):
                if i == 0:
                    w(v, "idxbc")
                if i >= 2:
                    w(v, f"cmm{i - 2}")
                inc(v.tensor_scalar(gt[i % 2][:], idx_bc[:], iota_tok[:, i:i + 1], 0.0,
                                    ALU.subtract, ALU.is_equal), "ve", 1, f"gt{i}")

            @blk.tensor
            def _(t, i=i):
                w(t, f"gt{i}")
                if i == 0:
                    w(t, "vev3")
                for pair in range(64):
                    mm = t.matmul(psA[:, pair * 32:(pair + 1) * 32],
                                  gt[i % 2][:, pair * 128:(pair + 1) * 128],
                                  v_tm[:, i * D + pair * 32:i * D + (pair + 1) * 32],
                                  start=(i == 0 and pair % 16 == 0), stop=False,
                                  skip_group_check=True)
                inc(mm, "pe", 1, f"cmm{i}")

        @blk.vector
        def _(v):
            w(v, "cmm7")
            w(v, "pay0")
            for pair in range(64):
                v.tensor_copy(payV[0:64, pair * 16:(pair + 1) * 16],
                              psA[0:64, pair * 32:pair * 32 + 16])
                v.tensor_copy(payV[64:128, pair * 16:(pair + 1) * 16],
                              psA[64:128, pair * 32 + 16:pair * 32 + 32])
            inc(v.engine_nop(), "ve", 1, "pay_done")

        nc.leave_named_scope("l1_pay", _sc, False)
        _sc = nc.enter_named_scope("l1_wam", False)[0]
        @blk.gpsimd
        def _(g):
            w(g, "wred15")
            inc(g.tensor_reduce(wpool[:, :], wparts[:], axis=AX.C, op=ALU.max),
                "po", 1, "wpool")

        @blk.vector
        def _(v):
            w(v, "wpool")
            v.reduce_max(wamax_sb[:, 0:1], wpool[:, 0:32], axis=AX.X)
            inc(v.reduce_max(wamax_sb[:, 1:2], wpool[:, 32:64], axis=AX.X),
                "ve", 1, "wamax_done")

        nc.leave_named_scope("l1_wam", _sc, False)
        _sc = nc.enter_named_scope("l1_st", False)[0]
        @blk.sync
        def _(sync):
            w(sync, "qk_evac")
            inc(sync.dma_start(out=o_q[:, :], in_=q_sb[:]), "stq", 16)
            w(sync, "sel_done")
            inc(sync.dma_start(out=o_cand[:, :], in_=cand[:]), "stq", 16)
            w(sync, "pay_done")
            inc(sync.dma_start(out=o_payV[:, :], in_=payV[:]), "stq", 16)
            w(sync, "wamax_done")
            inc(sync.dma_start(out=o_wamax[:, :], in_=wamax_sb[:]), "stq", 16)
            sync.wait_ge(P.sems["stq"], P.cnt["stq"])

        nc.leave_named_scope("l1_st", _sc, False)

    return nc




def build_L2(stage=99):
    nc = bass.Bass("TRN2", target_bir_lowering=False)
    x = nc.dram_tensor("x", [S, D], F32, kind="ExternalInput").ap()
    qin = nc.dram_tensor("q", [H, S], F32, kind="ExternalInput").ap()
    ktin = nc.dram_tensor("kt", [H, 128], F32, kind="ExternalInput").ap()
    kbin = nc.dram_tensor("kb", [H, 128], F32, kind="ExternalInput").ap()
    cvtin = nc.dram_tensor("cvt", [128, D], BF16, kind="ExternalInput").ap()
    cvbin = nc.dram_tensor("cvb", [128, D], BF16, kind="ExternalInput").ap()
    woT = nc.dram_tensor("woT", [D, D], F32, kind="ExternalInput").ap()
    wfc_sl = nc.dram_tensor("wfc_sl", [D, S], F32, kind="ExternalInput").ap()
    wproj_sl = nc.dram_tensor("wproj_sl", [FF, 256], F32, kind="ExternalInput").ap()
    wam = nc.dram_tensor("wam", [1, 16], F32, kind="ExternalInput").ap()

    o_y = nc.dram_tensor("y", [S, D], F32, kind="ExternalOutput").ap()
    o_xam = nc.dram_tensor("xamax", [1, 1], F32, kind="ExternalOutput").ap()
    o_w8fc = nc.dram_tensor("w8fc_sl", [D, S], F8, kind="ExternalOutput").ap()
    o_w8pj = nc.dram_tensor("w8proj_sl", [FF, 256], F8, kind="ExternalOutput").ap()

    idx2_scr = nc.dram_tensor("idx2_scr", [H, 64], F32)
    kmq_scr = nc.dram_tensor("kmq_scr", [4, 64], F32)

    with ExitStack() as st:
        sb = lambda n, s, d=F32: st.enter_context(nc.sbuf_tensor(n, s, d))
        psum = lambda n, s, d=F32: st.enter_context(nc.psum_tensor(n, s, d))

        q_sb = sb("q_sb2", [H, S])
        qbf = sb("qbf", [H, S], BF16)
        qpos = sb("qpos", [128, 8 * H], mybir.dt.uint8)
        kt = sb("kt_sb", [H, 128])
        kb = sb("kb_sb", [H, 128])
        cvt = sb("cvt_sb", [128, D], BF16)
        cvb = sb("cvb_sb", [128, D], BF16)
        work = [sb(f"w2_{i}", [H, 128]) for i in range(2)]
        m82 = sb("m82", [H, 64])
        idx2 = sb("idx2", [H, 64], U16)
        idx2f = sb("idx2f", [H, 64])
        m82T = sb("m82T", [32, 2 * H])
        scol = sb("scol", [128, 64])
        kmc = sb("kmc", [128, 64])
        scolM = sb("scolM", [128, 64])
        RHS = sb("RHS", [128, 64 * 68], BF16)
        selw = sb("selw", [128, 8192], BF16)
        att = sb("att", [128, 8 * D], BF16)
        e_sb = sb("e_sb", [128, 14 * S], BF16)
        attnb = sb("attnb", [128, 4 * 448], BF16)
        maskx = sb("maskx", [128, 224], mybir.dt.uint8)
        recd = sb("recd", [128, 112])
        recs = sb("recs", [128, 112])
        iota_p = sb("iota_p2", [128, 1])
        ident32 = sb("ident2", [128, 128])
        identb = sb("identb", [128, 128], BF16)
        stat = sb("stat2", [128, 128])
        wamb = sb("wamb", [128, 16])
        scl = sb("scl2", [128, 4])

        psA = psum("psA2", [128, 2048])
        psB = psum("psB2", [128, 1024])
        psC = psum("psC2", [128, 1024], BF16)
        psT = psum("psT2", [128, 1024], BF16)

        P = _Prog(nc, st, ("sq", "gq", "aq", "pe", "sc", "ve", "po",
                           "x0", "x1", "wo0", "wo1", "wo2", "ws0", "ws1", "stq",
                           "stw0", "stw1", "sy0", "sy1"))
        inc, w, ss = P.inc, P.w, P.ss
        blk = st.enter_context(nc.Block())
        _preload_tables(nc, blk)

        def _stop():
            @blk.sync
            def _(sync):
                for ch in P.cnt:
                    if P.cnt[ch]:
                        sync.wait_ge(P.sems[ch], P.cnt[ch])

        @blk.gpsimd
        def _(g):
            inc(g.memset(ident32[:], 0.0), "po", 1, "id0")
            g.iota(iota_p[:], [[0, 1]], base=0, channel_multiplier=1,
                   allow_small_or_imprecise_dtypes=True)
            w(g, "id0")
            inc(g.affine_select(out=ident32[:], in_=ident32[:], compare_op=ALU.not_equal,
                                fill=1.0, base=0, pattern=[[-1, 128]],
                                channel_multiplier=1), "po", 1, "const")
            inc(g.iota(selw[:], [[2, 64], [0, 2], [1, 2], [0, 32]], base=0,
                       channel_multiplier=0,
                       allow_small_or_imprecise_dtypes=True), "po", 1, "selw_i")

        @blk.vector
        def _(v):
            w(v, "const")
            inc(v.tensor_copy(identb[:], ident32[:]), "ve", 1, "identb")
            w(v, "selw_i")
            inc(v.tensor_scalar(selw[:], selw[:], iota_p[:], 0.0,
                                ALU.subtract, ALU.is_equal), "ve", 1, "selw")
            ss(v, v.memset(RHS[:], 0.0), "ve")
            for j_ in range(2):
                den_t = RHS[j_ * 32:(j_ + 1) * 32, :].rearrange(
                    "p (g c) -> p g c", c=68)[:, :, j_ * 34 + 32:j_ * 34 + 33]
                ss(v, v.memset(den_t, 1.0), "ve")
                den_b = RHS[64 + j_ * 32:64 + (j_ + 1) * 32, :].rearrange(
                    "p (g c) -> p g c", c=68)[:, :, j_ * 34 + 33:j_ * 34 + 34]
                ss(v, v.memset(den_b, 1.0), "ve")
            inc(v.engine_nop(), "ve", 1, "rhs_ones")

        @blk.sync
        def _(sync):
            inc(sync.dma_start(out=q_sb[:], in_=qin[:, :]), "sq", 16)
            inc(sync.dma_start(out=kt[:], in_=ktin[:, :]), "sq", 16)
            inc(sync.dma_start(out=kb[:], in_=kbin[:, :]), "sq", 16)
            inc(sync.dma_start(out=cvt[:], in_=cvtin[:, :]), "sq", 16)
            inc(sync.dma_start(out=cvb[:], in_=cvbin[:, :]), "sq", 16)
            inc(sync.dma_start(out=wamb[:], in_=wam[0:1, :].broadcast_to([128, 16])),
                "sq", 16, "loads")

        _sc = nc.enter_named_scope("l2_prep", False)[0]
        @blk.vector
        def _(v):
            w(v, "loads")
            for sel, srcT in ((0, kt), (1, kb)):
                cur = srcT
                for p in range(4):
                    col = sel * 32 + p * 8
                    ss(v, v.max(m82[:, col:col + 8], cur[:]), "ve")
                    ss(v, v.max_index(idx2[:, col:col + 8], m82[:, col:col + 8], cur[:]), "ve")
                    if p < 3:
                        nxt = work[p % 2]
                        ss(v, v.match_replace(nxt[:], in_to_replace=m82[:, col:col + 8],
                                              in_values=cur[:], imm_value=NEG), "ve")
                        cur = nxt
            inc(v.tensor_scalar(idx2f[:], idx2[:], 0.0, None, ALU.add, ALU.bypass),
                "ve", 1, "sel2")
            inc(v.tensor_copy(qbf[:], q_sb[:]), "ve", 1, "qbf")
            ss(v, v.reduce_max(scl[:, 0:1], wamb[:, 0:8], axis=AX.X), "ve")
            ss(v, v.reduce_max(scl[:, 1:2], wamb[:, 8:16], axis=AX.X), "ve")
            ss(v, v.tensor_scalar(scl[:, 0:2], scl[:, 0:2], 1.0 / 448.0, 1e-6,
                                  ALU.mult, ALU.max), "ve")
            ss(v, v.reciprocal(scl[:, 2:4], scl[:, 0:2]), "ve")
            inc(v.tensor_scalar(scl[:, 2:4], scl[:, 2:4], 0.5, None, ALU.mult,
                                ALU.bypass), "ve", 1, "scl")

        @blk.tensor
        def _(t):
            w(t, "sel2")
            w(t, "const")
            inc(t.transpose(psB[0:32, 0:128], m82[:, 0:32], ident32[:]), "pe", 1, "m8t")

        @blk.vector
        def _(v):
            w(v, "m8t")
            inc(v.tensor_copy(m82T[:, 0:128], psB[0:32, 0:128]), "ve", 1, "m8tc")

        @blk.tensor
        def _(t):
            w(t, "m8tc")
            inc(t.transpose(psB[0:32, 0:128], m82[:, 32:64], ident32[:]), "pe", 1, "m8b")

        @blk.vector
        def _(v):
            w(v, "m8b")
            ss(v, v.tensor_scalar(m82T[:, 128:256], psB[0:32, 0:128], -1.0, None,
                                  ALU.mult, ALU.bypass), "ve")
            for sel in range(2):
                for j in range(2):
                    ss(v, v.tensor_copy(
                        scol[sel * 64 + j * 32:sel * 64 + (j + 1) * 32, :],
                        m82T[0:32, sel * 128 + j:sel * 128 + 128:2]), "ve")
            inc(v.engine_nop(), "ve", 1, "scol")

        @blk.sync
        def _(sync):
            w(sync, "scol")
            with nc.allow_non_contiguous_dma(reason="tiny 64-elem shift rows"):
                inc(sync.dma_start(out=kmq_scr[0:1, :], in_=m82T[0:1, 0:128:2]), "sq", 16)
                inc(sync.dma_start(out=kmq_scr[1:2, :], in_=m82T[0:1, 1:128:2]), "sq", 16)
                inc(sync.dma_start(out=kmq_scr[2:3, :], in_=m82T[0:1, 128:256:2]), "sq", 16)
                inc(sync.dma_start(out=kmq_scr[3:4, :], in_=m82T[0:1, 129:256:2]),
                    "sq", 16, "kmw")
            w(sync, "kmw")
            for rr in range(4):
                mk = "kmr" if rr == 3 else None
                inc(sync.dma_start(out=kmc[rr * 32:(rr + 1) * 32, :],
                                   in_=kmq_scr[rr:rr + 1, :].broadcast_to([32, 64])),
                    "sq", 16, mk)

        @blk.vector
        def _(v):
            w(v, "kmr")
            inc(v.tensor_tensor(scolM[:], scol[:], kmc[:], ALU.subtract),
                "ve", 1, "scolM")

        for i in range(8):
            @blk.tensor
            def _(t, i=i):
                if i == 0:
                    w(t, "scol")
                    w(t, "loads")
                else:
                    w(t, f"qpos{i - 1}")
                inc(t.transpose(psB[:, 128:256], q_sb[:, i * 128:(i + 1) * 128],
                                ident32[:]), "pe", 1, f"qtr{i}")

            @blk.vector
            def _(v, i=i):
                w(v, f"qtr{i}")
                inc(v.tensor_scalar(qpos[:, i * H:(i + 1) * H], psB[:, 128:256],
                                    0.0, None, ALU.is_gt, ALU.bypass),
                    "ve", 1, f"qpos{i}")

        if stage <= 1:
            _stop()
            return nc

        nc.leave_named_scope("l2_prep", _sc, False)
        _sc = nc.enter_named_scope("l2_cand", False)[0]
        with ExitStack() as st2:
            idx2_bc = st2.enter_context(nc.sbuf_tensor("idx2_bc", [128, 8192], F16))
            g2t = st2.enter_context(nc.sbuf_tensor("g2t", [128, 8192], BF16))
            g2b = st2.enter_context(nc.sbuf_tensor("g2b", [128, 8192], BF16))

            @blk.vector
            def _(v):
                inc(v.memset(g2t[:], 0.0), "ve", 1, "g2t0")
                inc(v.memset(g2b[:], 0.0), "ve", 1, "g2b0")

            @blk.gpsimd
            def _(g):
                w(g, "sel2")
                inc(g.dma_start(out=idx2_scr[:, :], in_=idx2f[:]), "gq", 16, "i2w")
                w(g, "i2w")
                flat = idx2_scr.rearrange("a b -> (a b)")
                inc(g.dma_start(out=idx2_bc[:, :],
                                in_=flat[None, :].broadcast_to([128, 8192])),
                    "gq", 16, "i2bc")

            @blk.vector
            def _(v):
                w(v, "i2bc")
                w(v, "g2t0")
                w(v, "g2b0")
                g2t_out = g2t[:].rearrange("p (g s j r) -> p g s j r",
                                           s=2, j=2, r=32)[:, :, 0]
                i2t_in = idx2_bc[:].rearrange("p (h s r) -> p h s r",
                                              s=2, r=32)[:, :, 0]
                ss(v, v.tensor_scalar(g2t_out, i2t_in, iota_p[:], 0.0,
                                      ALU.subtract, ALU.is_equal), "ve")
                g2b_out = g2b[:].rearrange("p (g s j r) -> p g s j r",
                                           s=2, j=2, r=32)[:, :, 1]
                i2b_in = idx2_bc[:].rearrange("p (h s r) -> p h s r",
                                              s=2, r=32)[:, :, 1]
                inc(v.tensor_scalar(g2b_out, i2b_in, iota_p[:], 0.0,
                                    ALU.subtract, ALU.is_equal), "ve", 1, "g2")

            for wv_ in range(4):
                for gg in range(16):
                    g_ = wv_ * 16 + gg
                    scol0 = (gg % 4) * 512 + (gg // 4) * 32

                    @blk.tensor
                    def _(t, g_=g_, gg=gg, wv_=wv_, scol0=scol0):
                        if g_ == 0:
                            w(t, "g2")
                        if g_ >= 4:
                            w(t, f"rhscp{g_ - 4}")
                        first = (gg < 4)
                        t.matmul(psA[:, scol0:scol0 + 32],
                                 g2t[:, g_ * 128:(g_ + 1) * 128],
                                 cvt[:, g_ * 32:(g_ + 1) * 32],
                                 start=first, stop=False, skip_group_check=True)
                        inc(t.matmul(psA[:, scol0:scol0 + 32],
                                     g2b[:, g_ * 128:(g_ + 1) * 128],
                                     cvb[:, g_ * 32:(g_ + 1) * 32],
                                     start=False, stop=True, skip_group_check=True),
                            "pe", 1, f"cv{g_}")

                    @blk.vector
                    def _(v, g_=g_, scol0=scol0):
                        w(v, f"cv{g_}")
                        if g_ == 0:
                            w(v, "rhs_ones")
                        col = g_ * 68
                        for j in range(2):
                            v.tensor_copy(RHS[j * 32:(j + 1) * 32,
                                              col + j * 34:col + j * 34 + 16],
                                          psA[j * 32:(j + 1) * 32,
                                              scol0 + j * 16:scol0 + (j + 1) * 16])
                            v.tensor_copy(RHS[64 + j * 32:64 + (j + 1) * 32,
                                              col + j * 34 + 16:col + j * 34 + 32],
                                          psA[64 + j * 32:64 + (j + 1) * 32,
                                              scol0 + j * 16:scol0 + (j + 1) * 16])
                        inc(v.engine_nop(), "ve", 1, f"rhscp{g_}")

        if stage <= 2:
            _stop()
            return nc

        nc.leave_named_scope("l2_cand", _sc, False)
        _sc = nc.enter_named_scope("l2_attn", False)[0]
        def emit_w8_load(a):
            @blk.gpsimd
            def _(g, a=a):
                if a < 2:
                    w(g, "rhscp63")
                if a >= 2:
                    w(g, f"w8q{a - 2}")
                if a < 16:
                    inc(g.dma_start(out=wq_buf[a % 2][:],
                                    in_=wfc_sl[a * 128:(a + 1) * 128, :]),
                        f"ws{a % 2}", 16, f"wfl{a}")
                else:
                    kk = a - 16
                    for j_ in range(4):
                        r = kk * 4 + j_
                        inc(g.dma_start(out=wq_buf[a % 2][:, j_ * 256:(j_ + 1) * 256],
                                        in_=wproj_sl[r * 128:(r + 1) * 128, :]),
                            f"ws{a % 2}", 16, f"wfl{a}" if j_ == 3 else None)

        def emit_w8_quant(a):
            @blk.scalar
            def _(s_, a=a):
                w(s_, f"wfl{a}")
                if a == 0:
                    w(s_, "scl")
                if a >= 2:
                    w(s_, f"w8st{a - 2}")
                inc(s_.activation(w8_buf[a % 2][:], wq_buf[a % 2][:], AF.Copy,
                                  scale=scl[:, 2:3] if a < 16 else scl[:, 3:4]),
                    "sc", 1, f"w8q{a}")

            @blk.sync
            def _(sync, a=a):
                w(sync, f"w8q{a}")
                if a < 16:
                    inc(sync.dma_start(out=o_w8fc[a * 128:(a + 1) * 128, :],
                                       in_=w8_buf[a % 2][:]),
                        f"stw{a % 2}", 16, f"w8st{a}")
                else:
                    kk = a - 16
                    for j_ in range(4):
                        r = kk * 4 + j_
                        inc(sync.dma_start(out=o_w8pj[r * 128:(r + 1) * 128, :],
                                           in_=w8_buf[a % 2][:, j_ * 256:(j_ + 1) * 256]),
                            f"stw{a % 2}", 16, f"w8st{a}" if j_ == 3 else None)

            if a + 2 < 32:
                emit_w8_load(a + 2)

        GBLK = [list(range(a, min(a + 7, 64))) for a in range(0, 64, 7)]
        psE = [psB[:, 0:512], psB[:, 512:1024]]
        psO = [psA[:, b * 512:(b + 1) * 512] for b in range(4)]

        afm = sb("afm", [128, 16 * S], BF16)
        x_ch = [sb(f"x2c{i}", [128, D]) for i in range(2)]
        wo_cache = sb("wo_cache", [128, 16 * 512], BF16)
        y_blk = [sb(f"yblk{i}", [128, D]) for i in range(2)]
        wq_buf = [sb(f"wqb{i}", [128, 1024]) for i in range(2)]
        w8_buf = [sb(f"w8b{i}", [128, 1024], F8) for i in range(2)]
        junk2 = sb("junk2", [128, 512])
        stat3 = sb("stat3b", [128, 64])
        wof32 = [sb(f"wof32_{i}", [128, 512]) for i in range(2)]

        emit_w8_load(0)
        emit_w8_load(1)

        trslot = [psT[:, 0:512], psC[:, 0:512]]
        _ntr = [0]

        def emit_E(bi, blkg):
            base = (bi % 2) * 7 * S
            for sl_, g_ in enumerate(blkg):
                for hf in range(2):
                    k2 = 2 * g_ + hf
                    eslot = k2 % 2

                    @blk.tensor
                    def _(t, bi=bi, blkg=blkg, g_=g_, hf=hf, k2=k2, eslot=eslot):
                        if bi == 0 and g_ == blkg[0] and hf == 0:
                            w(t, "selw")
                            w(t, "qbf")
                            w(t, "rhscp63")
                            w(t, "qpos7")
                        if k2 >= 2:
                            w(t, f"exp{k2 - 2}")
                        inc(t.matmul(psE[eslot], selw[:, g_ * 128:(g_ + 1) * 128],
                                     qbf[:, hf * 512:(hf + 1) * 512],
                                     start=True, stop=True), "pe", 1, f"eb{k2}")

                    @blk.scalar
                    def _(s_, bi=bi, sl_=sl_, g_=g_, hf=hf, k2=k2, eslot=eslot,
                          base=base):
                        w(s_, f"eb{k2}")
                        if k2 == 0:
                            w(s_, "scolM")
                        if bi >= 2 and sl_ == 0 and hf == 0:
                            w(s_, f"combdone{bi - 2}")
                        inc(s_.activation(e_sb[:, base + sl_ * S + hf * 512:
                                               base + sl_ * S + (hf + 1) * 512],
                                          psE[eslot], AF.Exp,
                                          scale=scolM[:, g_:g_ + 1]),
                            "sc", 1, f"exp{k2}")

        def emit_comb(bi, blkg, nb, hx):
            base = (bi % 2) * 7 * S

            @blk.tensor
            def _(t, bi=bi, blkg=blkg, nb=nb, hx=hx, base=base):
                w(t, f"exp{2 * blkg[-1] + 1}")
                if hx == 0:
                    if bi >= 1:
                        w(t, f"nselB{bi - 1}")
                else:
                    w(t, f"nselA{bi}")
                mm = None
                for b in range(4):
                    i = hx * 4 + b
                    for sl_, g_ in enumerate(blkg):
                        mm = t.matmul(
                            psO[b][:, sl_ * 68:(sl_ + 1) * 68],
                            e_sb[:, base + sl_ * S + i * 128:
                                 base + sl_ * S + (i + 1) * 128],
                            RHS[:, g_ * 68:(g_ + 1) * 68],
                            start=(sl_ == 0), stop=(sl_ == nb - 1),
                            skip_group_check=True)
                inc(mm, "pe", 1, f"comb{bi}_{hx}")
                if hx == 1:
                    P.M[f"combdone{bi}"] = ("pe", P.cnt["pe"])

        def emit_norm(bi, blkg, nb, hx):
            g0 = blkg[0]

            @blk.vector
            def _(v, bi=bi, blkg=blkg, nb=nb, hx=hx, g0=g0):
                w(v, f"comb{bi}_{hx}")
                dens = psA[:, 0:2048].rearrange("p (b x) -> p b x", b=4)[
                    :, :, 0:nb * 68].rearrange("p b (u j c) -> p b u j c",
                                               j=2, c=34)[:, :, :, :, 32:34]
                recs_v = recs[:, 0:4 * nb * 4].rearrange(
                    "p (b u j s) -> p b u j s", b=4, j=2, s=2)
                ss(v, v.tensor_copy(recs_v, dens), "ve")
                ss(v, v.reciprocal(recd[:, 0:4 * nb * 4], recs[:, 0:4 * nb * 4]),
                   "ve")
                for b in range(4):
                    po = psO[b]
                    num_ap = po[:, 0:nb * 68].rearrange(
                        "p (u j c) -> p u j c", j=2, c=34)[:, :, :, 0:32].rearrange(
                        "p u j (s c) -> p u j s c", c=16)
                    rec_ex = recd[:, b * nb * 4:(b + 1) * nb * 4].rearrange(
                        "p (u j s) -> p u j s", j=2, s=2)[
                        :, :, :, :, None].broadcast_to([128, nb, 2, 2, 16])
                    tt_out5 = attnb[:, b * 448:b * 448 + 448].rearrange(
                        "p (s uj c) -> p uj s c", s=2, c=16).rearrange(
                        "p (u j) s c -> p u j s c", j=2)[:, 0:nb]
                    ss(v, v.tensor_tensor(tt_out5, num_ap, rec_ex, ALU.mult), "ve")
                for b in range(4):
                    i = hx * 4 + b
                    mask3 = qpos[:, i * H + 2 * g0:i * H + 2 * (g0 + nb)][
                        :, :, None].broadcast_to([128, nb * 2, 16])
                    mx3 = maskx[:, 0:nb * 32].rearrange("p (u c) -> p u c", c=16)
                    ss(v, v.tensor_copy(mx3, mask3), "ve")
                    out2 = att[:, i * D + g0 * 32:i * D + (g0 + nb) * 32]
                    ss(v, v.tensor_copy(
                        out2, attnb[:, b * 448 + 224:b * 448 + 224 + nb * 32]), "ve")
                    cp = v.copy_predicated(out2, maskx[:, 0:nb * 32],
                                           attnb[:, b * 448:b * 448 + nb * 32])
                    if b == 3:
                        inc(cp, "ve", 1, f"nsel{'AB'[hx]}{bi}")
                    else:
                        ss(v, cp, "ve")

        def emit_tr(bi, i_list, cs):
            ihalf = i_list[0] // 4
            for c in cs:
                n = _ntr[0]
                _ntr[0] += 1

                @blk.tensor
                def _(t, n=n, c=c, bi=bi, ihalf=ihalf):
                    if n == 0:
                        w(t, "identb")
                    w(t, f"nsel{'AB'[ihalf]}{bi}")
                    if n >= 2:
                        w(t, f"aev{n - 2}")
                    mm = None
                    for k in range(4):
                        i = ihalf * 4 + k
                        mm = t.transpose(
                            trslot[n % 2][:, k * 128:(k + 1) * 128],
                            att[:, i * D + c * 128:i * D + (c + 1) * 128],
                            identb[:])
                    inc(mm, "pe", 1, f"atr{n}")

                @blk.scalar
                def _(s_, n=n, c=c, ihalf=ihalf):
                    w(s_, f"atr{n}")
                    inc(s_.activation(afm[:, c * S + ihalf * 512:
                                           c * S + (ihalf + 1) * 512],
                                      trslot[n % 2], AF.Copy), "sc", 1, f"aev{n}")

        cs_done = {bi: [c for c in range(16) if (4 * c + 3) // 7 == bi]
                   for bi in range(10)}
        for bi, blkg in enumerate(GBLK):
            nb = len(blkg)
            if bi >= 1:
                pb = GBLK[bi - 1]
                emit_comb(bi - 1, pb, len(pb), 1)
                emit_norm(bi - 1, pb, len(pb), 1)
                emit_tr(bi - 1, [0, 1, 2, 3], cs_done[bi - 1])
            emit_E(bi, blkg)
            for _qa in range(bi * 4, min(bi * 4 + 4, 32)):
                emit_w8_quant(_qa)
            emit_comb(bi, blkg, nb, 0)
            emit_norm(bi, blkg, nb, 0)
            if bi >= 1:
                emit_tr(bi - 1, [4, 5, 6, 7], cs_done[bi - 1])
        emit_comb(9, GBLK[9], len(GBLK[9]), 1)
        emit_norm(9, GBLK[9], len(GBLK[9]), 1)
        emit_tr(9, [0, 1, 2, 3], cs_done[9])
        emit_tr(9, [4, 5, 6, 7], cs_done[9])
        assert _ntr[0] == 32

        if stage <= 3:
            _stop()
            return nc

        nc.leave_named_scope("l2_attn", _sc, False)
        _sc = nc.enter_named_scope("l2_wo", False)[0]
        for e in range(4):
            @blk.gpsimd
            def _(g, e=e):
                if e == 0:
                    w(g, "rhscp63")
                else:
                    w(g, f"womh{e - 1}_1")
                for c in range(8):
                    inc(g.dma_start(out=wo_cache[:, c * 512:(c + 1) * 512],
                                    in_=woT[c * 128:(c + 1) * 128,
                                            e * 512:(e + 1) * 512]),
                        "wo0", 16, f"wog{e}" if c == 7 else None)

            for c in range(8, 16):
                kc = e * 8 + (c - 8)

                @blk.sync
                def _(sync, e=e, c=c, kc=kc):
                    if kc < 2:
                        w(sync, "rhscp63")
                    else:
                        w(sync, f"wocst{kc - 2}")
                    inc(sync.dma_start(out=wof32[kc % 2][:],
                                       in_=woT[c * 128:(c + 1) * 128,
                                               e * 512:(e + 1) * 512]),
                        f"wo{1 + kc % 2}", 16, f"wol{kc}")

                @blk.vector
                def _(v, e=e, c=c, kc=kc):
                    w(v, f"wol{kc}")
                    if c == 8 and e >= 1:
                        w(v, f"womm{e - 1}_1")
                    inc(v.tensor_copy(wo_cache[:, c * 512:(c + 1) * 512],
                                      wof32[kc % 2][:]),
                        "ve", 1,
                        f"wocst{kc}" if c < 15 else f"wos{e}")
                    if c == 15:
                        P.M[f"wocst{kc}"] = P.M[f"wos{e}"]

            for hh in range(2):
                gw = e * 2 + hh

                @blk.sync
                def _(sync, e=e, hh=hh, gw=gw):
                    if gw < 2:
                        w(sync, "rhscp63")
                    else:
                        w(sync, f"yev{gw - 2}")
                    for ii in range(4):
                        i = hh * 4 + ii
                        inc(sync.dma_start(
                            out=x_ch[hh][:, ii * 512:(ii + 1) * 512],
                            in_=x[i * 128:(i + 1) * 128, e * 512:(e + 1) * 512]),
                            f"x{hh}", 16, f"xs{gw}" if ii == 3 else None)

                @blk.tensor
                def _(t, e=e, hh=hh, gw=gw):
                    w(t, f"wog{e}")
                    w(t, f"wos{e}")
                    if gw == 0:
                        w(t, "aev31")
                    if gw >= 1:
                        w(t, f"yev{gw - 1}")
                    mm = None
                    for c in range(16):
                        for ii in range(4):
                            i = hh * 4 + ii
                            mm = t.matmul(
                                psO[ii][:, 0:512],
                                afm[:, c * S + i * 128:c * S + (i + 1) * 128],
                                wo_cache[:, c * 512:(c + 1) * 512],
                                start=(c == 0), stop=(c == 15),
                                skip_group_check=True)
                        if c == 7:
                            inc(mm, "pe", 1, f"womh{e}_{hh}")
                    inc(mm, "pe", 1, f"womm{e}_{hh}")

                @blk.vector
                def _(v, e=e, hh=hh, gw=gw):
                    w(v, f"womm{e}_{hh}")
                    w(v, f"xs{gw}")
                    if gw >= 2:
                        w(v, f"ysw{gw - 2}")
                        w(v, f"yst{gw - 2}")
                        w(v, f"yrd{gw - 2}")
                    yb = y_blk[gw % 2]
                    mm = None
                    for ii in range(4):
                        ss(v, v.tensor_tensor(yb[:, ii * 512:(ii + 1) * 512],
                                              psO[ii][:, 0:512],
                                              x_ch[hh][:, ii * 512:(ii + 1) * 512],
                                              ALU.add), "ve")
                    inc(v.engine_nop(), "ve", 1, f"yev{gw}")

                @blk.vector
                def _(v, e=e, hh=hh, gw=gw):
                    yb = y_blk[gw % 2]
                    for ii in range(4):
                        ig = hh * 4 + ii
                        ss(v, v.tensor_reduce(
                            stat[:, 64 + e * 8 + ig:65 + e * 8 + ig],
                            yb[:, ii * 512:(ii + 1) * 512],
                            axis=AX.X, op=ALU.max), "ve")
                        ss(v, v.tensor_reduce(
                            stat[:, 96 + e * 8 + ig:97 + e * 8 + ig],
                            yb[:, ii * 512:(ii + 1) * 512],
                            axis=AX.X, op=ALU.min), "ve")
                    inc(v.engine_nop(), "ve", 1, f"yrd{gw}")

                @blk.scalar
                def _(s_, e=e, hh=hh, gw=gw):
                    w(s_, f"yev{gw}")
                    if gw >= 1:
                        w(s_, f"yst{gw - 1}")
                    yb = y_blk[gw % 2]
                    act = None
                    for ii in range(4):
                        ig = hh * 4 + ii
                        ss(s_, s_.activation(junk2[:],
                                             yb[:, ii * 512:(ii + 1) * 512],
                                             AF.Copy,
                                             accum_out=stat[:, e * 8 + ig:
                                                            e * 8 + ig + 1]), "sc")
                        act = s_.activation(junk2[:],
                                            yb[:, ii * 512:(ii + 1) * 512],
                                            AF.Square,
                                            accum_out=stat[:, 32 + e * 8 + ig:
                                                           33 + e * 8 + ig])
                        if ii < 3:
                            ss(s_, act, "sc")
                    inc(act, "sc", 1, f"yst{gw}")

                @blk.sync
                def _(sync, e=e, hh=hh, gw=gw):
                    w(sync, f"yev{gw}")
                    for ii in range(4):
                        i = hh * 4 + ii
                        inc(sync.dma_start(
                            out=o_y[i * 128:(i + 1) * 128, e * 512:(e + 1) * 512],
                            in_=y_blk[gw % 2][:, ii * 512:(ii + 1) * 512]),
                            f"sy{gw % 2}", 16, f"ysw{gw}" if ii == 3 else None)

        if stage <= 5:
            _stop()
            return nc

        @blk.vector
        def _(v):
            w(v, "yst7")
            w(v, "yrd7")
            for base, op, dst in ((0, ALU.add, 0), (32, ALU.add, 8),
                                  (64, ALU.max, 16), (96, ALU.min, 24)):
                ss(v, v.tensor_tensor(stat3[:, dst:dst + 8],
                                      stat[:, base:base + 8],
                                      stat[:, base + 8:base + 16], op), "ve")
                ss(v, v.tensor_tensor(stat3[:, 32 + dst:40 + dst],
                                      stat[:, base + 16:base + 24],
                                      stat[:, base + 24:base + 32], op), "ve")
                ss(v, v.tensor_tensor(stat3[:, dst:dst + 8],
                                      stat3[:, dst:dst + 8],
                                      stat3[:, 32 + dst:40 + dst], op), "ve")
            ss(v, v.tensor_scalar(stat3[:, 32:40], stat3[:, 0:8], 1.0 / D, None,
                                  ALU.mult, ALU.bypass), "ve")
            ss(v, v.tensor_scalar(stat3[:, 40:48], stat3[:, 8:16], 1.0 / D, None,
                                  ALU.mult, ALU.bypass), "ve")
            ss(v, v.tensor_tensor(stat3[:, 48:56], stat3[:, 32:40],
                                  stat3[:, 32:40], ALU.mult), "ve")
            ss(v, v.tensor_tensor(stat3[:, 40:48], stat3[:, 40:48],
                                  stat3[:, 48:56], ALU.subtract), "ve")
            inc(v.tensor_scalar(stat3[:, 40:48], stat3[:, 40:48], EPS, None,
                                ALU.add, ALU.bypass), "ve", 1, "yvar")

        @blk.scalar
        def _(s_):
            w(s_, "yvar")
            ss(s_, s_.activation(stat3[:, 48:56], stat3[:, 40:48], AF.Ln), "sc")
            inc(s_.activation(stat3[:, 48:56], stat3[:, 48:56], AF.Exp, scale=-0.5),
                "sc", 1, "yrstd")

        @blk.vector
        def _(v):
            w(v, "yrstd")
            ss(v, v.tensor_tensor(stat3[:, 56:64], stat3[:, 16:24],
                                  stat3[:, 32:40], ALU.subtract), "ve")
            ss(v, v.tensor_tensor(stat3[:, 16:24], stat3[:, 32:40],
                                  stat3[:, 24:32], ALU.subtract), "ve")
            ss(v, v.tensor_tensor(stat3[:, 56:64], stat3[:, 56:64],
                                  stat3[:, 16:24], ALU.max), "ve")
            ss(v, v.tensor_tensor(stat3[:, 56:64], stat3[:, 56:64],
                                  stat3[:, 48:56], ALU.mult), "ve")
            inc(v.reduce_max(stat3[:, 62:63], stat3[:, 56:64], axis=AX.X),
                "ve", 1, "xam_col")

        @blk.gpsimd
        def _(g):
            w(g, "xam_col")
            inc(g.tensor_reduce(stat3[0:1, 63:64], stat3[:, 62:63], axis=AX.C,
                                op=ALU.max), "po", 1, "xam")

        @blk.sync
        def _(sync):
            w(sync, "xam")
            inc(sync.dma_start(out=o_xam[0:1, :], in_=stat3[0:1, 63:64]),
                "stq", 16, "xam_out")

        nc.leave_named_scope("l2_wo", _sc, False)
        _sc = nc.enter_named_scope("l2_w8", False)[0]
        @blk.sync
        def _(sync):
            sync.wait_ge(P.sems["stq"], P.cnt["stq"])
            sync.wait_ge(P.sems["stw0"], P.cnt["stw0"])
            sync.wait_ge(P.sems["stw1"], P.cnt["stw1"])

        nc.leave_named_scope("l2_w8", _sc, False)

    return nc



def build_L3():
    nc = bass.Bass("TRN2", target_bir_lowering=False)
    y_in = nc.dram_tensor("y", [S, D], F32, kind="ExternalInput").ap()
    w8st = nc.dram_tensor("w8st", [64 * 16 * 128, 128], F8, kind="ExternalInput").ap()
    xam8 = nc.dram_tensor("xam8", [1, 8], F32, kind="ExternalInput").ap()
    o_hr = nc.dram_tensor("hrelu", [FF, S], BF16, kind="ExternalOutput").ap()
    o_ham = nc.dram_tensor("ham", [1, 1], F32, kind="ExternalOutput").ap()

    with ExitStack() as st:
        sb = lambda n, s, d=F32: st.enter_context(nc.sbuf_tensor(n, s, d))
        psum = lambda n, s, d=F32: st.enter_context(nc.psum_tensor(n, s, d))

        y_sb = sb("y3", [128, 8 * D])
        x8t = [sb(f"x8t{i}", [128, D], F8) for i in range(2)]
        x8b = [sb(f"x8b{i}", [128, D], BF16) for i in range(2)]
        x8fm = sb("x8fm", [128, 16 * S], F8)
        wst = [sb(f"wst{i}", [128, 16 * 128], F8) for i in range(2)]
        hst = [sb(f"hst{i}", [128, S], BF16) for i in range(2)]
        hamc = sb("hamc", [128, 66])
        stat = sb("stat3", [128, 64])
        xamb = sb("xamb", [128, 8])
        junk = sb("junk3", [128, D])
        ident32 = sb("ident3", [128, 128])
        identb = sb("identb3", [128, 128], BF16)

        psF = [psum("psF0", [128, 1024]), psum("psF1", [128, 1024])]
        psD = [psum("psD0", [128, 1024], BF16),
               psum("psD1", [128, 1024], BF16)]

        P = _Prog(nc, st, ("sq", "gq", "pe", "sc", "ve", "po",
                           "y0", "y1", "y2", "y3", "y4", "y5", "y6", "y7",
                           "wt0", "wt1", "st0", "st1", "sto"))
        inc, w, ss = P.inc, P.w, P.ss
        blk = st.enter_context(nc.Block())

        @blk.gpsimd
        def _(g):
            inc(g.memset(ident32[:], 0.0), "po", 1, "id0")
            w(g, "id0")
            inc(g.affine_select(out=ident32[:], in_=ident32[:], compare_op=ALU.not_equal,
                                fill=1.0, base=0, pattern=[[-1, 128]],
                                channel_multiplier=1), "po", 1, "idf")

        @blk.vector
        def _(v):
            w(v, "idf")
            inc(v.tensor_copy(identb[:], ident32[:]), "ve", 1, "identb")

        @blk.sync
        def _(sync):
            inc(sync.dma_start(out=xamb[:], in_=xam8[0:1, :].broadcast_to([128, 8])),
                "sq", 16, "xld")
            for i in range(8):
                inc(sync.dma_start(out=y_sb[:, i * D:(i + 1) * D],
                                   in_=y_in[i * 128:(i + 1) * 128, :]),
                    f"y{i}", 16, f"yld{i}")

        _sc = nc.enter_named_scope("l3_stats", False)[0]
        for i in range(8):
            @blk.scalar
            def _(s_, i=i):
                w(s_, f"yld{i}")
                if i > 0:
                    w(s_, f"yst{i - 1}")
                ss(s_, s_.activation(junk[:], y_sb[:, i * D:(i + 1) * D],
                                     AF.Copy, accum_out=stat[:, i:i + 1]), "sc")
                inc(s_.activation(junk[:], y_sb[:, i * D:(i + 1) * D],
                                  AF.Square, accum_out=stat[:, 8 + i:9 + i]),
                    "sc", 1, f"yst{i}")

        @blk.vector
        def _(v):
            w(v, "yst7")
            ss(v, v.tensor_scalar(stat[:, 16:24], stat[:, 0:8], 1.0 / D, None,
                                  ALU.mult, ALU.bypass), "ve")
            ss(v, v.tensor_scalar(stat[:, 24:32], stat[:, 8:16], 1.0 / D, None,
                                  ALU.mult, ALU.bypass), "ve")
            ss(v, v.tensor_tensor(stat[:, 48:56], stat[:, 16:24], stat[:, 16:24],
                                  ALU.mult), "ve")
            ss(v, v.tensor_tensor(stat[:, 24:32], stat[:, 24:32], stat[:, 48:56],
                                  ALU.subtract), "ve")
            inc(v.tensor_scalar(stat[:, 24:32], stat[:, 24:32], EPS, None,
                                ALU.add, ALU.bypass), "ve", 1, "var3")

        @blk.scalar
        def _(s_):
            w(s_, "var3")
            ss(s_, s_.activation(stat[:, 32:40], stat[:, 24:32], AF.Ln), "sc")
            inc(s_.activation(stat[:, 32:40], stat[:, 32:40], AF.Exp, scale=-0.5),
                "sc", 1, "rstd3")

        @blk.vector
        def _(v):
            w(v, "rstd3")
            w(v, "xld")
            ss(v, v.reduce_max(stat[:, 56:57], xamb[:, 0:8], axis=AX.X), "ve")
            ss(v, v.tensor_scalar(stat[:, 57:58], stat[:, 56:57], 1.0 / 448.0, 1e-6,
                                  ALU.mult, ALU.max), "ve")
            ss(v, v.reciprocal(stat[:, 58:59], stat[:, 57:58]), "ve")
            ss(v, v.tensor_scalar(stat[:, 58:59], stat[:, 58:59], 0.5, None,
                                  ALU.mult, ALU.bypass), "ve")
            ss(v, v.tensor_tensor(stat[:, 40:48], stat[:, 32:40],
                                  stat[:, 58:59].broadcast_to([128, 8]),
                                  ALU.mult), "ve")
            ss(v, v.tensor_tensor(stat[:, 48:56], stat[:, 16:24], stat[:, 40:48],
                                  ALU.mult), "ve")
            inc(v.tensor_scalar(stat[:, 48:56], stat[:, 48:56], -1.0, None,
                                ALU.mult, ALU.bypass), "ve", 1, "sig3")

        nc.leave_named_scope("l3_stats", _sc, False)
        _sc = nc.enter_named_scope("l3_qtr", False)[0]
        for i in range(8):
            @blk.scalar
            def _(s_, i=i):
                if i == 0:
                    w(s_, "sig3")
                if i >= 2:
                    w(s_, f"cst{i - 2}")
                inc(s_.activation(x8t[i % 2][:], y_sb[:, i * D:(i + 1) * D],
                                  AF.Identity, scale=stat[:, 40 + i:41 + i],
                                  bias=stat[:, 48 + i:49 + i]), "sc", 1, f"xq{i}")

            @blk.vector
            def _(v, i=i):
                w(v, f"xq{i}")
                if i >= 2:
                    w(v, f"trch{i - 2}")
                inc(v.tensor_copy(x8b[i % 2][:], x8t[i % 2][:]), "ve", 1, f"cst{i}")

            for k in range(16):
                n = i * 16 + k

                @blk.tensor
                def _(t, i=i, k=k, n=n):
                    if n == 0:
                        w(t, "identb")
                    if k == 0:
                        w(t, f"cst{i}")
                    if n >= 2:
                        w(t, f"tev{n - 2}")
                    inc(t.transpose(psD[n % 2][:, 0:128],
                                    x8b[i % 2][:, k * 128:(k + 1) * 128],
                                    identb[:]), "pe", 1,
                        f"trch{i}" if k == 15 else f"tr{n}")

                @blk.vector
                def _(v, i=i, k=k, n=n):
                    w(v, f"trch{i}" if k == 15 else f"tr{n}")
                    inc(v.tensor_copy(x8fm[:, k * S + i * 128:k * S + (i + 1) * 128],
                                      psD[n % 2][:, 0:128]), "ve", 1, f"tev{n}")

        P.M["x8done"] = P.M["tev127"]

        nc.leave_named_scope("l3_qtr", _sc, False)
        _sc = nc.enter_named_scope("l3_fc", False)[0]
        for ff in range(64):
            @blk.gpsimd
            def _(g, ff=ff):
                if ff >= 2:
                    w(g, f"fcm{ff - 2}")
                inc(g.dma_start(out=wst[ff % 2][:].rearrange("p (k f) -> p k f", f=128),
                                in_=w8st[ff * 2048:(ff + 1) * 2048, :].rearrange(
                                    "(p k) f -> p k f", k=16)),
                    f"wt{ff % 2}", 16, f"wtld{ff}")

            @blk.tensor
            def _(t, ff=ff):
                w(t, f"wtld{ff}")
                if ff == 0:
                    w(t, "x8done")
                if ff >= 2:
                    w(t, f"fcv{ff - 2}")
                mm = None
                for kk in range(8):
                    for hf in range(2):
                        lhs3 = wst[ff % 2][:, kk * 256:(kk + 1) * 256].rearrange(
                            "p (i m) -> p i m", i=2)
                        rhs3 = x8fm[:, 2 * kk * S:(2 * kk + 2) * S].rearrange(
                            "p (i t) -> p i t", i=2)[:, :, hf * 512:(hf + 1) * 512]
                        mm = t.matmul(psF[ff % 2][:, hf * 512:(hf + 1) * 512],
                                      lhs3, rhs3, start=(kk == 0), stop=(kk == 7),
                                      perf_mode=mybir.MatmulPerfMode.DoubleRow,
                                      skip_group_check=True)
                inc(mm, "pe", 1, f"fcm{ff}")

            @blk.vector
            def _(v, ff=ff):
                w(v, f"fcm{ff}")
                if ff >= 2:
                    w(v, f"hsw{ff - 2}")
                ss(v, v.tensor_scalar(hst[ff % 2][:], psF[ff % 2][:], 0.0, None,
                                      ALU.max, ALU.bypass), "ve")
                inc(v.tensor_reduce(hamc[:, ff:ff + 1], psF[ff % 2][:],
                                    axis=AX.X, op=ALU.max), "ve", 1, f"fcv{ff}")

            @blk.sync
            def _(sync, ff=ff):
                w(sync, f"fcv{ff}")
                inc(sync.dma_start(out=o_hr[ff * 128:(ff + 1) * 128, :],
                                   in_=hst[ff % 2][:]),
                    f"st{ff % 2}", 16, f"hsw{ff}")

        nc.leave_named_scope("l3_fc", _sc, False)
        _sc = nc.enter_named_scope("l3_ham", False)[0]
        @blk.vector
        def _(v):
            w(v, "fcv63")
            ss(v, v.reduce_max(hamc[:, 64:65], hamc[:, 0:64], axis=AX.X), "ve")
            inc(v.tensor_scalar(hamc[:, 64:65], hamc[:, 64:65], 0.0, None,
                                ALU.max, ALU.bypass), "ve", 1, "hcol")

        @blk.gpsimd
        def _(g):
            w(g, "hcol")
            inc(g.tensor_reduce(hamc[0:1, 65:66], hamc[:, 64:65], axis=AX.C,
                                op=ALU.max), "po", 1, "hram")

        @blk.sync
        def _(sync):
            w(sync, "hram")
            inc(sync.dma_start(out=o_ham[:, :], in_=hamc[0:1, 65:66]), "sto", 16)
            for ch in P.cnt:
                if P.cnt[ch]:
                    sync.wait_ge(P.sems[ch], P.cnt[ch])

        nc.leave_named_scope("l3_ham", _sc, False)

    return nc



def build_L4():
    nc = bass.Bass("TRN2", target_bir_lowering=False)
    y_in = nc.dram_tensor("y", [S, D], F32, kind="ExternalInput").ap()
    hr_in = nc.dram_tensor("hrelu", [FF, S], BF16, kind="ExternalInput").ap()
    w8pj = nc.dram_tensor("w8pj", [FF, D], F8, kind="ExternalInput").ap()
    ham8 = nc.dram_tensor("ham8", [1, 8], F32, kind="ExternalInput").ap()
    xam8 = nc.dram_tensor("xam8", [1, 8], F32, kind="ExternalInput").ap()
    wam = nc.dram_tensor("wam", [1, 16], F32, kind="ExternalInput").ap()
    o_out = nc.dram_tensor("out", [S, D], F32, kind="ExternalOutput").ap()

    with ExitStack() as st:
        sb = lambda n, s, d=F32: st.enter_context(nc.sbuf_tensor(n, s, d))
        psum = lambda n, s, d=F32: st.enter_context(nc.psum_tensor(n, s, d))

        y_sb = sb("y4", [128, 8 * D])
        h8 = sb("h8", [128, 64 * S], F8)
        hr = [sb(f"hr{i}", [128, S], BF16) for i in range(3)]
        wpj = sb("wpj", [128, 64 * S], F8)
        osb = [sb(f"osb{i}", [128, S]) for i in range(2)]
        scal = sb("scal", [128, 40])
        psP = [psum("psP0", [128, 1024]), psum("psP1", [128, 1024])]

        P = _Prog(nc, st, ("sq", "yq", "gq", "pe", "sc", "ve", "po",
                           "hq0", "hq1", "hq2", "hq3", "wq0", "wq1",
                           "st0", "st1"))
        inc, w, ss = P.inc, P.w, P.ss
        blk = st.enter_context(nc.Block())
        _preload_tables(nc, blk)

        @blk.sync
        def _(sync):
            inc(sync.dma_start(out=scal[:, 0:8],
                               in_=xam8[0:1, :].broadcast_to([128, 8])), "sq", 16)
            inc(sync.dma_start(out=scal[:, 8:16],
                               in_=ham8[0:1, :].broadcast_to([128, 8])), "sq", 16)
            inc(sync.dma_start(out=scal[:, 16:32],
                               in_=wam[0:1, :].broadcast_to([128, 16])),
                "sq", 16, "sld")
            for i in range(8):
                inc(sync.dma_start(out=y_sb[:, i * D:(i + 1) * D],
                                   in_=y_in[i * 128:(i + 1) * 128, :]),
                    "yq", 16, f"yld{i}" if i == 7 else None)

        @blk.vector
        def _(v):
            w(v, "sld")
            ss(v, v.reduce_max(scal[:, 32:33], scal[:, 0:8], axis=AX.X), "ve")
            ss(v, v.reduce_max(scal[:, 33:34], scal[:, 8:16], axis=AX.X), "ve")
            ss(v, v.reduce_max(scal[:, 34:35], scal[:, 16:24], axis=AX.X), "ve")
            ss(v, v.reduce_max(scal[:, 35:36], scal[:, 24:32], axis=AX.X), "ve")
            ss(v, v.tensor_scalar(scal[:, 32:33], scal[:, 32:33], 1.0 / 448.0, 1e-6,
                                  ALU.mult, ALU.max), "ve")
            ss(v, v.tensor_scalar(scal[:, 34:35], scal[:, 34:35], 1.0 / 448.0, 1e-6,
                                  ALU.mult, ALU.max), "ve")
            ss(v, v.tensor_scalar(scal[:, 35:36], scal[:, 35:36], 1.0 / 448.0, 1e-6,
                                  ALU.mult, ALU.max), "ve")
            ss(v, v.tensor_tensor(scal[:, 36:37], scal[:, 32:33], scal[:, 34:35],
                                  ALU.mult), "ve")
            ss(v, v.tensor_scalar(scal[:, 36:37], scal[:, 36:37], 4.0, None,
                                  ALU.mult, ALU.bypass), "ve")
            ss(v, v.tensor_tensor(scal[:, 37:38], scal[:, 36:37], scal[:, 33:34],
                                  ALU.mult), "ve")
            ss(v, v.tensor_tensor(scal[:, 37:38], scal[:, 37:38], scal[:, 37:38],
                                  ALU.mult), "ve")
            ss(v, v.tensor_scalar(scal[:, 37:38], scal[:, 37:38], 1.0 / 448.0, 1e-6,
                                  ALU.mult, ALU.max), "ve")
            ss(v, v.tensor_scalar(scal[:, 38:39], scal[:, 37:38], 2.0, None,
                                  ALU.mult, ALU.bypass), "ve")
            inc(v.reciprocal(scal[:, 38:39], scal[:, 38:39]), "ve", 1, "rech")

        @blk.scalar
        def _(s_):
            w(s_, "rech")
            ss(s_, s_.activation(scal[:, 38:39], scal[:, 38:39], AF.Ln), "sc")
            inc(s_.activation(scal[:, 38:39], scal[:, 38:39], AF.Exp, scale=0.5),
                "sc", 1, "sqh")

        @blk.vector
        def _(v):
            w(v, "sqh")
            ss(v, v.tensor_tensor(scal[:, 38:39], scal[:, 38:39], scal[:, 36:37],
                                  ALU.mult), "ve")
            ss(v, v.tensor_tensor(scal[:, 39:40], scal[:, 37:38], scal[:, 35:36],
                                  ALU.mult), "ve")
            inc(v.tensor_scalar(scal[:, 39:40], scal[:, 39:40], 4.0, None,
                                ALU.mult, ALU.bypass), "ve", 1, "deqs")

        _sc = nc.enter_named_scope("l4_h8", False)[0]
        for c in range(64):
            @blk.gpsimd
            def _(g, c=c):
                if c >= 3:
                    w(g, f"h8q{c - 3}")
                inc(g.dma_start(out=hr[c % 3][:],
                                in_=hr_in[c * 128:(c + 1) * 128, :]),
                    f"hq{c % 3}", 16, f"hld{c}")

            @blk.scalar
            def _(s_, c=c):
                w(s_, f"hld{c}")
                if c == 0:
                    w(s_, "deqs")
                inc(s_.activation(h8[:, c * S:(c + 1) * S], hr[c % 3][:],
                                  AF.Square, scale=scal[:, 38:39]),
                    "sc", 1, f"h8q{c}")

        nc.leave_named_scope("l4_h8", _sc, False)
        _sc = nc.enter_named_scope("l4_proj", False)[0]
        for dh in range(2):
            for c in range(64):
                @blk.sync
                def _(sync, dh=dh, c=c):
                    if dh == 1 and c % 16 == 0:
                        w(sync, f"pjp{c // 16 + 1}" if c < 32 else "pjm7")
                    inc(sync.dma_start(out=wpj[:, c * S:(c + 1) * S],
                                       in_=w8pj[c * 128:(c + 1) * 128,
                                                dh * 1024:(dh + 1) * 1024]),
                        f"wq{dh % 2}", 16, f"wld{dh}_{c}" if c == 63 else None)

            for tt in range(8):
                g_ = dh * 8 + tt

                @blk.tensor
                def _(t, dh=dh, tt=tt, g_=g_):
                    if tt == 0:
                        w(t, f"wld{dh}_63")
                    if g_ >= 2:
                        w(t, f"pjv{g_ - 2}")
                    mm = None
                    h8v = h8[:].rearrange("p (c t) -> p c t", t=S)
                    wpv = wpj[:].rearrange("p (c t) -> p c t", t=S)
                    for cc in range(32):
                        if g_ == 0:
                            w(t, f"h8q{2 * cc + 1}")
                        for hf in range(2):
                            mm = t.matmul(
                                psP[g_ % 2][:, hf * 512:(hf + 1) * 512],
                                h8v[:, 2 * cc:2 * cc + 2, tt * 128:(tt + 1) * 128],
                                wpv[:, 2 * cc:2 * cc + 2, hf * 512:(hf + 1) * 512],
                                start=(cc == 0), stop=(cc == 31),
                                perf_mode=mybir.MatmulPerfMode.DoubleRow,
                                skip_group_check=True)
                        if g_ == 7 and cc % 8 == 7 and cc < 31:
                            inc(mm, "pe", 1, f"pjp{cc // 8}")
                    inc(mm, "pe", 1, f"pjm{g_}")

                @blk.vector
                def _(v, dh=dh, tt=tt, g_=g_):
                    w(v, f"pjm{g_}")
                    if g_ == 0:
                        w(v, "yld7")
                    if g_ >= 2:
                        w(v, f"osw{g_ - 2}")
                    inc(v.scalar_tensor_tensor(
                        osb[g_ % 2][:], psP[g_ % 2][:], scal[:, 39:40],
                        y_sb[:, tt * D + dh * 1024:tt * D + (dh + 1) * 1024],
                        ALU.mult, ALU.add), "ve", 1, f"pjv{g_}")

                @blk.sync
                def _(sync, dh=dh, tt=tt, g_=g_):
                    w(sync, f"pjv{g_}")
                    inc(sync.dma_start(out=o_out[tt * 128:(tt + 1) * 128,
                                                 dh * 1024:(dh + 1) * 1024],
                                       in_=osb[g_ % 2][:]),
                        f"st{g_ % 2}", 16, f"osw{g_}")

        @blk.sync
        def _(sync):
            for ch in P.cnt:
                if P.cnt[ch]:
                    sync.wait_ge(P.sems[ch], P.cnt[ch])

        nc.leave_named_scope("l4_proj", _sc, False)

    return nc


def kernel(**inputs):
    x = np.ascontiguousarray(
        np.asarray(inputs["L0_attn_x"], np.float32).reshape(8 * S, D))
    Wq = np.asarray(inputs["L0_attn_Wq"], np.float32)
    Wk = np.asarray(inputs["L0_attn_Wk"], np.float32)
    Wv = np.asarray(inputs["L0_attn_Wv"], np.float32)
    Wo = np.asarray(inputs["L0_attn_Wo"], np.float32)
    wfcT = np.ascontiguousarray(np.asarray(inputs["L0_ffn_w_fc"], np.float32).T)
    wprojT = np.ascontiguousarray(np.asarray(inputs["L0_ffn_w_proj"], np.float32).T)
    wqT = np.ascontiguousarray(Wq.T)
    wkT = np.ascontiguousarray(Wk.T)
    wvT = np.ascontiguousarray(Wv.T)
    woT = np.ascontiguousarray(Wo.T)
    xs = [np.ascontiguousarray(x[c * S:(c + 1) * S]) for c in range(8)]
    wfc_sls = [np.ascontiguousarray(wfcT[:, c * S:(c + 1) * S]) for c in range(8)]
    wpj_sls = [np.ascontiguousarray(wprojT[:, c * 256:(c + 1) * 256])
               for c in range(8)]

    in1 = [{"x": xs[c], "wqT": wqT, "wkT": wkT, "wvT": wvT,
            "wfc_sl": wfc_sls[c], "wproj_sl": wpj_sls[c]} for c in range(8)]
    r1 = run_bass_kernel_spmd(build_L1(), in1, core_ids=list(range(8))).results

    wam = np.concatenate([np.asarray(r1[c]["wamax"]).reshape(2) for c in range(8)])
    wam = np.ascontiguousarray(
        np.stack([wam[0::2], wam[1::2]]).reshape(1, 16).astype(np.float32))
    in2 = []
    for c in range(8):
        grp = range(0, 4) if c < 4 else range(4, 8)
        kt = np.zeros((H, 128), np.float32)
        kb = np.zeros((H, 128), np.float32)
        cvt = np.zeros((128, D), NB)
        cvb = np.zeros((128, D), NB)
        for kk, cc in enumerate(grp):
            cand = np.asarray(r1[cc]["cand"], np.float32)
            kt[:, kk * 32:(kk + 1) * 32] = cand[:, :32]
            kb[:, kk * 32:(kk + 1) * 32] = cand[:, 32:]
            pv = np.asarray(r1[cc]["payV"], NB).reshape(2, 2, 32, 64, 16)
            for sel, dst in ((0, cvt), (1, cvb)):
                arr = pv[:, sel].transpose(1, 2, 0, 3)
                dst[kk * 32:(kk + 1) * 32, :] = arr.reshape(32, D)
        in2.append({"x": xs[c], "q": np.ascontiguousarray(r1[c]["q"]),
                    "kt": kt, "kb": kb, "cvt": cvt, "cvb": cvb, "woT": woT,
                    "wfc_sl": wfc_sls[c], "wproj_sl": wpj_sls[c], "wam": wam})

    r2 = run_bass_kernel_spmd(build_L2(), in2, core_ids=list(range(8))).results

    ys = [np.ascontiguousarray(np.asarray(r2[c]["y"], np.float32))
          for c in range(8)]
    xam8 = np.ascontiguousarray(np.concatenate(
        [np.asarray(r2[c]["xamax"], np.float32).reshape(1) for c in range(8)]
    ).reshape(1, 8))
    w8fc = np.concatenate([np.asarray(r2[c]["w8fc_sl"], NF8) for c in range(8)],
                          axis=1)
    w8pj = np.ascontiguousarray(np.concatenate(
        [np.asarray(r2[c]["w8proj_sl"], NF8) for c in range(8)], axis=1))
    w8st = np.ascontiguousarray(
        w8fc.reshape(16, 128, 64, 128).transpose(2, 1, 0, 3).reshape(
            64 * 128 * 16, 128))

    in3 = [{"y": ys[c], "w8st": w8st, "xam8": xam8} for c in range(8)]
    r3 = run_bass_kernel_spmd(build_L3(), in3, core_ids=list(range(8))).results

    ham8 = np.ascontiguousarray(np.concatenate(
        [np.asarray(r3[c]["ham"], np.float32).reshape(1) for c in range(8)]
    ).reshape(1, 8))

    in4 = [{"y": ys[c],
            "hrelu": np.ascontiguousarray(np.asarray(r3[c]["hrelu"], NB)),
            "w8pj": w8pj, "ham8": ham8, "xam8": xam8, "wam": wam}
           for c in range(8)]
    r4 = run_bass_kernel_spmd(build_L4(), in4, core_ids=list(range(8))).results

    out = np.concatenate([np.asarray(r4[c]["out"], np.float32) for c in range(8)],
                         axis=0)
    return np.ascontiguousarray(out.reshape(2, 4 * S, D))


# version 10 (final)
# speedup vs baseline: 1.5484x; 1.0106x over previous
import numpy as np
import ml_dtypes
import concourse.bass as bass
import concourse.mybir as mybir
from concourse.alu_op_type import AluOpType as ALU
from concourse.bass_utils import run_bass_kernel_spmd
from contextlib import ExitStack

F32 = mybir.dt.float32
BF16 = mybir.dt.bfloat16
F16 = mybir.dt.float16
F8 = mybir.dt.float8e4
U16 = mybir.dt.uint16
AF = mybir.ActivationFunctionType
AX = mybir.AxisListType

NB = ml_dtypes.bfloat16
NF8 = ml_dtypes.float8_e4m3

S = 1024
D = 2048
H = 128
FF = 8192
EPS = 1e-5
NEG = -1.0e30


def _preload_one(nc, s_):
    try:
        from concourse.hw_specs import get_activation_tables
        need = (AF.Exp, AF.Ln, AF.Copy, AF.Identity, AF.Square, AF.Relu)
        set_id = None
        for idx, funcs in enumerate(get_activation_tables("TRN2").values()):
            if all(f in funcs for f in need):
                set_id = idx
                break
    except Exception:
        set_id = 6
    if set_id is not None:
        s_.add_instruction(mybir.InstLoadActFuncSet(
            name=nc.get_next_instruction_name(),
            act_func_set_id=set_id, ins=[], outs=[]))


def _preload_tables(nc, blk):
    try:
        from concourse.hw_specs import get_activation_tables
        need = (AF.Exp, AF.Ln, AF.Copy, AF.Identity, AF.Square, AF.Relu)
        set_id = None
        for idx, funcs in enumerate(get_activation_tables("TRN2").values()):
            if all(f in funcs for f in need):
                set_id = idx
                break
    except Exception:
        set_id = 6
    if set_id is None:
        return

    @blk.scalar
    def _(s_):
        s_.add_instruction(mybir.InstLoadActFuncSet(
            name=nc.get_next_instruction_name(),
            act_func_set_id=set_id, ins=[], outs=[]))


class _Prog:

    def __init__(self, nc, st, chans):
        self.nc = nc
        self.sems = {k: st.enter_context(nc.semaphore(k)) for k in chans}
        self.cnt = {k: 0 for k in chans}
        self.M = {}

    def inc(self, instr, ch, n, mark=None):
        instr.then_inc(self.sems[ch], n)
        self.cnt[ch] += n
        if mark:
            self.M[mark] = (ch, self.cnt[ch])
        return self.cnt[ch]

    def w(self, eng, mark):
        ch, val = self.M[mark]
        eng.wait_ge(self.sems[ch], val)

    def ss(self, eng, instr, ch):
        instr.then_inc(self.sems[ch], 1)
        self.cnt[ch] += 1
        eng.wait_ge(self.sems[ch], self.cnt[ch])



def build_L1():
    nc = bass.Bass("TRN2", target_bir_lowering=False)
    x = nc.dram_tensor("x", [S, D], F32, kind="ExternalInput").ap()
    wqT = nc.dram_tensor("wqT", [D, H], F32, kind="ExternalInput").ap()
    wkT = nc.dram_tensor("wkT", [D, H], F32, kind="ExternalInput").ap()
    wvT = nc.dram_tensor("wvT", [D, D], F32, kind="ExternalInput").ap()
    wfc_sl = nc.dram_tensor("wfc_sl", [D, S], F32, kind="ExternalInput").ap()
    wproj_sl = nc.dram_tensor("wproj_sl", [FF, 256], F32, kind="ExternalInput").ap()

    o_q = nc.dram_tensor("q", [H, S], F32, kind="ExternalOutput").ap()
    o_cand = nc.dram_tensor("cand", [H, 64], F32, kind="ExternalOutput").ap()
    o_payV = nc.dram_tensor("payV", [128, 1024], BF16, kind="ExternalOutput").ap()
    o_wamax = nc.dram_tensor("wamax", [1, 2], F32, kind="ExternalOutput").ap()

    idx_scr = nc.dram_tensor("idx_scr", [H, 64], F32)

    with ExitStack() as st:
        sb = lambda n, s, d=F32: st.enter_context(nc.sbuf_tensor(n, s, d))
        psum = lambda n, s: st.enter_context(nc.psum_tensor(n, s, F32))

        x_ch = [sb(f"x{i}", [128, D]) for i in range(2)]
        lnx_ch = [sb(f"lnxc{i}", [128, D]) for i in range(2)]
        stg = [sb(f"stg{i}", [128, 512]) for i in range(2)]
        lnb = sb("lnb", [128, 16 * S], BF16)
        wq_t = sb("wq_t", [128, 16 * H])
        wk_t = sb("wk_t", [128, 16 * H])
        wv_ch = [sb(f"wv{i}", [128, 512], BF16) for i in range(2)]
        v_tm = sb("v_tm", [128, 8 * D], BF16)
        q_sb = sb("q_sbuf", [H, S])
        k_sb = sb("k_sb", [H, S])
        negk = sb("negk", [H, S])
        work = [sb(f"wk_w{i}", [H, S]) for i in range(2)]
        cand = sb("cand_sb", [H, 64])
        idx16 = sb("idx16", [H, 64], U16)
        idxf = sb("idxf", [H, 64])
        idx_bc = sb("idx_bc", [128, 8192], F16)
        gt = [sb(f"gt{i}", [128, 8192], BF16) for i in range(2)]
        payV = sb("payV_sb", [128, 1024], BF16)
        iota_tok = sb("iota_tok", [128, 8])
        ident32 = sb("ident32", [128, 128])
        stat = sb("stat", [128, 64])
        wsl = [sb(f"wsl{i}", [128, 2048]) for i in range(2)]
        wparts = sb("wparts", [128, 64])
        wpool = sb("wpool", [1, 64])
        wamax_sb = sb("wamax_sb", [1, 2])

        psA = psum("psA", [128, 2048])
        psB = psum("psB", [128, 1024])
        psC = psum("psC", [128, 1024])

        P = _Prog(nc, st, ("sq", "gq", "aq", "pe", "sc", "ve", "po",
                   "x0", "x1", "wv0", "wv1", "ws0", "ws1", "stq"))
        inc, w, ss = P.inc, P.w, P.ss

        blk = st.enter_context(nc.Block())

        @blk.gpsimd
        def _(g):
            inc(g.memset(ident32[:], 0.0), "po", 1, "id0")
            g.iota(iota_tok[:], [[128, 8]], base=0, channel_multiplier=1,
                   allow_small_or_imprecise_dtypes=True)
            w(g, "id0")
            inc(g.affine_select(out=ident32[:], in_=ident32[:], compare_op=ALU.not_equal,
                                fill=1.0, base=0, pattern=[[-1, 128]],
                                channel_multiplier=1), "po", 1, "const")

        @blk.vector
        def _(v):
            inc(v.memset(payV[:], 0.0), "ve", 1, "pay0")

        @blk.sync
        def _(sync):
            for c in range(16):
                inc(sync.dma_start(out=wq_t[:, c * H:(c + 1) * H],
                                   in_=wqT[c * 128:(c + 1) * 128, :]),
                    "sq", 16, "wq" if c == 15 else None)
            for c in range(16):
                inc(sync.dma_start(out=wk_t[:, c * H:(c + 1) * H],
                                   in_=wkT[c * 128:(c + 1) * 128, :]),
                    "sq", 16, "wk" if c == 15 else None)

        _sc = nc.enter_named_scope("l1_stats", False)[0]
        for i in range(8):
            @blk.sync
            def _(sync, i=i):
                if i >= 2:
                    w(sync, f"stat{i - 2}")
                inc(sync.dma_start(out=x_ch[i % 2][:], in_=x[i * 128:(i + 1) * 128, :]),
                    f"x{i % 2}", 16, f"x{i}")

            @blk.scalar
            def _(s, i=i):
                w(s, f"x{i}")
                ss(s, s.activation(lnx_ch[i % 2][:], x_ch[i % 2][:], AF.Copy,
                                   accum_out=stat[:, i:i + 1]), "sc")
                inc(s.activation(lnx_ch[i % 2][:], x_ch[i % 2][:], AF.Square,
                                 accum_out=stat[:, 8 + i:9 + i]), "sc", 1, f"stat{i}")

        @blk.vector
        def _(v):
            w(v, "stat7")
            inv = 1.0 / D
            ss(v, v.tensor_scalar(stat[:, 16:24], stat[:, 0:8], inv, None, ALU.mult, ALU.bypass), "ve")
            ss(v, v.tensor_scalar(stat[:, 24:32], stat[:, 8:16], inv, None, ALU.mult, ALU.bypass), "ve")
            ss(v, v.tensor_tensor(stat[:, 48:56], stat[:, 16:24], stat[:, 16:24], ALU.mult), "ve")
            ss(v, v.tensor_tensor(stat[:, 24:32], stat[:, 24:32], stat[:, 48:56], ALU.subtract), "ve")
            inc(v.tensor_scalar(stat[:, 24:32], stat[:, 24:32], EPS, None, ALU.add,
                                ALU.bypass), "ve", 1, "var")

        @blk.scalar
        def _(s):
            w(s, "var")
            ss(s, s.activation(stat[:, 48:56], stat[:, 24:32], AF.Ln), "sc")
            inc(s.activation(stat[:, 32:40], stat[:, 48:56], AF.Exp, scale=-0.5),
                "sc", 1, "rstd")

        @blk.vector
        def _(v):
            w(v, "rstd")
            ss(v, v.tensor_tensor(stat[:, 56:64], stat[:, 16:24], stat[:, 32:40], ALU.mult), "ve")
            inc(v.tensor_scalar(stat[:, 40:48], stat[:, 56:64], -1.0, None, ALU.mult,
                                ALU.bypass), "ve", 1, "bias")

        nc.leave_named_scope("l1_stats", _sc, False)
        _sc = nc.enter_named_scope("l1_qk", False)[0]
        slots2 = [psB[:, 0:512], psC[:, 0:512]]

        def emit_qk_mms(t, g):
            i, g2 = divmod(g, 4)
            w(t, f"evf{g}")
            first0 = (i % 4 == 0 and g2 == 0)
            mm = None
            for j in range(4):
                c = g2 * 4 + j
                first = first0 and j == 0
                t.matmul(psA[:, i * 128:(i + 1) * 128], wq_t[:, c * H:(c + 1) * H],
                         stg[g % 2][:, j * 128:(j + 1) * 128],
                         start=first, stop=False, skip_group_check=True)
                mm = t.matmul(psA[:, 1024 + i * 128:1024 + (i + 1) * 128],
                              wk_t[:, c * H:(c + 1) * H],
                              stg[g % 2][:, j * 128:(j + 1) * 128],
                              start=first, stop=False, skip_group_check=True)
            inc(mm, "pe", 1, f"qkm{g}")

        for i in range(8):
            @blk.sync
            def _(sync, i=i):
                if i == 0:
                    w(sync, "stat6")
                elif i == 1:
                    w(sync, "stat7")
                else:
                    w(sync, f"lnxn{i - 2}")
                inc(sync.dma_start(out=x_ch[i % 2][:], in_=x[i * 128:(i + 1) * 128, :]),
                    f"x{i % 2}", 16, f"x2_{i}")

            @blk.scalar
            def _(s, i=i):
                w(s, f"x2_{i}")
                if i == 0:
                    w(s, "stat7")
                    w(s, "bias")
                if i >= 2:
                    w(s, f"trg{(i - 2) * 4 + 3}")
                inc(s.activation(lnx_ch[i % 2][:], x_ch[i % 2][:], AF.Identity,
                                 scale=stat[:, 32 + i:33 + i],
                                 bias=stat[:, 40 + i:41 + i]), "sc", 1, f"lnxn{i}")

            for g2 in range(4):
                g = i * 4 + g2

                @blk.tensor
                def _(t, i=i, g2=g2, g=g):
                    if g == 0:
                        w(t, "const")
                        w(t, "wk")
                    if g2 == 0:
                        w(t, f"lnxn{i}")
                    if g >= 2:
                        w(t, f"evf{g - 2}")
                        w(t, f"evb{g - 2}")
                    mm = None
                    for j in range(4):
                        c = g2 * 4 + j
                        mm = t.transpose(slots2[g % 2][:, j * 128:(j + 1) * 128],
                                         lnx_ch[i % 2][:, c * 128:(c + 1) * 128],
                                         ident32[:])
                    inc(mm, "pe", 1, f"trg{g}")

                @blk.vector
                def _(v, i=i, g2=g2, g=g):
                    w(v, f"trg{g}")
                    if g >= 2:
                        w(v, f"qkm{g - 2}")
                    ss(v, v.tensor_copy(stg[g % 2][:], slots2[g % 2]), "ve")
                    P.M[f"evf{g}"] = ("ve", P.cnt["ve"])
                    lnb_v = lnb[:].rearrange("p (c t) -> p c t", t=S)[
                        :, g2 * 4:g2 * 4 + 4, i * 128:(i + 1) * 128]
                    ps_v = slots2[g % 2].rearrange("p (c t) -> p c t", t=128)
                    inc(v.tensor_copy(lnb_v, ps_v), "ve", 1, f"evb{g}")

                @blk.tensor
                def _(t, g=g):
                    if g >= 1:
                        emit_qk_mms(t, g - 1)

        @blk.tensor
        def _(t):
            emit_qk_mms(t, 31)

        @blk.scalar
        def _(s):
            w(s, "qkm31")
            s.activation(q_sb[:], psA[:, 0:1024], AF.Copy)
            inc(s.activation(k_sb[:], psA[:, 1024:2048], AF.Copy),
                "sc", 1, "qk_evac")

        nc.leave_named_scope("l1_qk", _sc, False)
        _sc = nc.enter_named_scope("l1_v", False)[0]
        def emit_wam(kk):
            @blk.sync
            def _(s, kk=kk):
                if kk >= 2:
                    w(s, f"wred{kk - 2}")
                if kk < 8:
                    for j_ in range(2):
                        a = kk * 2 + j_
                        inc(s.dma_start(out=wsl[kk % 2][:, j_ * 1024:(j_ + 1) * 1024],
                                        in_=wfc_sl[a * 128:(a + 1) * 128, :]),
                            f"ws{kk % 2}", 16, f"wsl{kk}" if j_ == 1 else None)
                else:
                    for j_ in range(8):
                        a = (kk - 8) * 8 + j_
                        inc(s.dma_start(out=wsl[kk % 2][:, j_ * 256:(j_ + 1) * 256],
                                        in_=wproj_sl[a * 128:(a + 1) * 128, :]),
                            f"ws{kk % 2}", 16, f"wsl{kk}" if j_ == 7 else None)

            @blk.vector
            def _(v, kk=kk):
                w(v, f"wsl{kk}")
                if kk >= 1:
                    w(v, f"wred{kk - 1}")
                base = 0 if kk < 8 else 32
                j0 = kk if kk < 8 else kk - 8
                ss(v, v.tensor_scalar(negk[:, 0:S], wsl[kk % 2][:, 0:S], 0.0,
                                      None, ALU.max, ALU.max,
                                      accum_out=wparts[:, base + j0:base + j0 + 1]), "ve")
                ss(v, v.tensor_scalar(negk[:, 0:S], wsl[kk % 2][:, S:2 * S], 0.0,
                                      None, ALU.max, ALU.max,
                                      accum_out=wparts[:, base + 8 + j0:base + 9 + j0]), "ve")
                ss(v, v.tensor_scalar(negk[:, 0:S], wsl[kk % 2][:, 0:S], -1.0,
                                      None, ALU.mult, ALU.max,
                                      accum_out=wparts[:, base + 16 + j0:base + 17 + j0]), "ve")
                inc(v.tensor_scalar(negk[:, 0:S], wsl[kk % 2][:, S:2 * S], -1.0,
                                    None, ALU.mult, ALU.max,
                                    accum_out=wparts[:, base + 24 + j0:base + 25 + j0]),
                    "ve", 1, f"wred{kk}")


        @blk.vector
        def _(v):
            w(v, "qk_evac")
            ss(v, v.tensor_scalar(negk[:], k_sb[:], -1.0, None, ALU.mult, ALU.bypass), "ve")
            for sel, srcT in ((0, k_sb), (1, negk)):
                cur = srcT
                for p in range(4):
                    col = sel * 32 + p * 8
                    ss(v, v.max(cand[:, col:col + 8], cur[:]), "ve")
                    ss(v, v.max_index(idx16[:, col:col + 8], cand[:, col:col + 8], cur[:]), "ve")
                    if p < 3:
                        nxt = work[p % 2]
                        ss(v, v.match_replace(nxt[:], in_to_replace=cand[:, col:col + 8],
                                              in_values=cur[:], imm_value=NEG), "ve")
                        cur = nxt
            inc(v.tensor_scalar(idxf[:], idx16[:], 0.0, None, ALU.add, ALU.bypass),
                "ve", 1, "sel_done")


        vtiles = [psA[:, 0:512], psA[:, 512:1024], psA[:, 1024:1536], psA[:, 1536:2048],
                  psB[:, 0:512], psB[:, 512:1024], psC[:, 0:512], psC[:, 512:1024]]
        for t_ in range(4):
            for c in range(16):
                k_ = t_ * 16 + c

                @blk.gpsimd
                def _(g, t_=t_, c=c, k_=k_):
                    if k_ >= 2:
                        w(g, f"vmm{k_ - 2}")
                    inc(g.dma_start(out=wv_ch[k_ % 2][:],
                                    in_=wvT[c * 128:(c + 1) * 128,
                                            t_ * 512:(t_ + 1) * 512]),
                        f"wv{k_ % 2}", 16, f"wv{k_}")

                @blk.tensor
                def _(t, t_=t_, c=c, k_=k_):
                    w(t, f"wv{k_}")
                    if k_ == 0:
                        w(t, "qk_evac")
                        w(t, "evb31")
                    if c == 0 and t_ > 0:
                        w(t, f"vev{t_ - 1}")
                    for i in range(7):
                        t.matmul(vtiles[i],
                                 lnb[:, c * S + i * 128:c * S + (i + 1) * 128],
                                 wv_ch[k_ % 2][:], start=(c == 0), stop=(c == 15))
                    inc(t.matmul(vtiles[7],
                                 lnb[:, c * S + 7 * 128:c * S + 8 * 128],
                                 wv_ch[k_ % 2][:], start=(c == 0), stop=(c == 15)),
                        "pe", 1, f"vmm{k_}")

            @blk.vector
            def _(v, t_=t_):
                w(v, f"vmm{t_ * 16 + 15}")
                for i in range(8):
                    v.tensor_copy(v_tm[:, i * D + t_ * 512:i * D + (t_ + 1) * 512],
                                  vtiles[i])
                inc(v.engine_nop(), "ve", 1, f"vev{t_}")

            for _wk in range(4 * t_, 4 * t_ + 4):
                emit_wam(_wk)

        nc.leave_named_scope("l1_v", _sc, False)
        _sc = nc.enter_named_scope("l1_sel", False)[0]
        nc.leave_named_scope("l1_sel", _sc, False)
        _sc = nc.enter_named_scope("l1_pay", False)[0]
        @blk.gpsimd
        def _(g):
            w(g, "sel_done")
            inc(g.dma_start(out=idx_scr[:, :], in_=idxf[:]), "gq", 16, "idxw")
            w(g, "idxw")
            flat = idx_scr.rearrange("a b -> (a b)")
            inc(g.dma_start(out=idx_bc[:, :],
                            in_=flat[None, :].broadcast_to([128, 8192])),
                "gq", 16, "idxbc")

        for i in range(8):
            @blk.vector
            def _(v, i=i):
                if i == 0:
                    w(v, "idxbc")
                if i >= 2:
                    w(v, f"cmm{i - 2}")
                inc(v.tensor_scalar(gt[i % 2][:], idx_bc[:], iota_tok[:, i:i + 1], 0.0,
                                    ALU.subtract, ALU.is_equal), "ve", 1, f"gt{i}")

            @blk.tensor
            def _(t, i=i):
                w(t, f"gt{i}")
                if i == 0:
                    w(t, "vev3")
                for pair in range(64):
                    mm = t.matmul(psA[:, pair * 32:(pair + 1) * 32],
                                  gt[i % 2][:, pair * 128:(pair + 1) * 128],
                                  v_tm[:, i * D + pair * 32:i * D + (pair + 1) * 32],
                                  start=(i == 0 and pair % 16 == 0), stop=False,
                                  skip_group_check=True)
                inc(mm, "pe", 1, f"cmm{i}")

        @blk.vector
        def _(v):
            w(v, "cmm7")
            w(v, "pay0")
            for pair in range(64):
                v.tensor_copy(payV[0:64, pair * 16:(pair + 1) * 16],
                              psA[0:64, pair * 32:pair * 32 + 16])
                v.tensor_copy(payV[64:128, pair * 16:(pair + 1) * 16],
                              psA[64:128, pair * 32 + 16:pair * 32 + 32])
            inc(v.engine_nop(), "ve", 1, "pay_done")

        nc.leave_named_scope("l1_pay", _sc, False)
        _sc = nc.enter_named_scope("l1_wam", False)[0]
        @blk.gpsimd
        def _(g):
            w(g, "wred15")
            inc(g.tensor_reduce(wpool[:, :], wparts[:], axis=AX.C, op=ALU.max),
                "po", 1, "wpool")

        @blk.vector
        def _(v):
            w(v, "wpool")
            v.reduce_max(wamax_sb[:, 0:1], wpool[:, 0:32], axis=AX.X)
            inc(v.reduce_max(wamax_sb[:, 1:2], wpool[:, 32:64], axis=AX.X),
                "ve", 1, "wamax_done")

        nc.leave_named_scope("l1_wam", _sc, False)
        _sc = nc.enter_named_scope("l1_st", False)[0]
        @blk.sync
        def _(sync):
            w(sync, "qk_evac")
            inc(sync.dma_start(out=o_q[:, :], in_=q_sb[:]), "stq", 16)
            w(sync, "sel_done")
            inc(sync.dma_start(out=o_cand[:, :], in_=cand[:]), "stq", 16)
            w(sync, "pay_done")
            inc(sync.dma_start(out=o_payV[:, :], in_=payV[:]), "stq", 16)
            w(sync, "wamax_done")
            inc(sync.dma_start(out=o_wamax[:, :], in_=wamax_sb[:]), "stq", 16)
            sync.wait_ge(P.sems["stq"], P.cnt["stq"])

        nc.leave_named_scope("l1_st", _sc, False)

    return nc




def build_L2(stage=99):
    nc = bass.Bass("TRN2", target_bir_lowering=False)
    x = nc.dram_tensor("x", [S, D], F32, kind="ExternalInput").ap()
    qin = nc.dram_tensor("q", [H, S], F32, kind="ExternalInput").ap()
    ktin = nc.dram_tensor("kt", [H, 128], F32, kind="ExternalInput").ap()
    kbin = nc.dram_tensor("kb", [H, 128], F32, kind="ExternalInput").ap()
    cvtin = nc.dram_tensor("cvt", [128, D], BF16, kind="ExternalInput").ap()
    cvbin = nc.dram_tensor("cvb", [128, D], BF16, kind="ExternalInput").ap()
    woT = nc.dram_tensor("woT", [D, D], F32, kind="ExternalInput").ap()
    wfc_sl = nc.dram_tensor("wfc_sl", [D, S], F32, kind="ExternalInput").ap()
    wproj_sl = nc.dram_tensor("wproj_sl", [FF, 256], F32, kind="ExternalInput").ap()
    wam = nc.dram_tensor("wam", [1, 16], F32, kind="ExternalInput").ap()

    o_y = nc.dram_tensor("y", [S, D], F32, kind="ExternalOutput").ap()
    o_xam = nc.dram_tensor("xamax", [1, 1], F32, kind="ExternalOutput").ap()
    o_w8fc = nc.dram_tensor("w8fc_sl", [D, S], F8, kind="ExternalOutput").ap()
    o_w8pj = nc.dram_tensor("w8proj_sl", [FF, 256], F8, kind="ExternalOutput").ap()

    idx2_scr = nc.dram_tensor("idx2_scr", [H, 64], F32)
    kmq_scr = nc.dram_tensor("kmq_scr", [4, 64], F32)

    with ExitStack() as st:
        sb = lambda n, s, d=F32: st.enter_context(nc.sbuf_tensor(n, s, d))
        psum = lambda n, s, d=F32: st.enter_context(nc.psum_tensor(n, s, d))

        q_sb = sb("q_sb2", [H, S])
        qbf = sb("qbf", [H, S], BF16)
        qpos = sb("qpos", [128, 8 * H], mybir.dt.uint8)
        kt = sb("kt_sb", [H, 128])
        kb = sb("kb_sb", [H, 128])
        cvt = sb("cvt_sb", [128, D], BF16)
        cvb = sb("cvb_sb", [128, D], BF16)
        work = [sb(f"w2_{i}", [H, 128]) for i in range(2)]
        m82 = sb("m82", [H, 64])
        idx2 = sb("idx2", [H, 64], U16)
        idx2f = sb("idx2f", [H, 64])
        m82T = sb("m82T", [32, 2 * H])
        scol = sb("scol", [128, 64])
        kmc = sb("kmc", [128, 64])
        scolM = sb("scolM", [128, 64])
        RHS = sb("RHS", [128, 64 * 68], BF16)
        selw = sb("selw", [128, 8192], BF16)
        att = sb("att", [128, 8 * D], BF16)
        e_sb = sb("e_sb", [128, 14 * S], BF16)
        attnb = sb("attnb", [128, 4 * 448], BF16)
        maskx = sb("maskx", [128, 224], mybir.dt.uint8)
        recd = sb("recd", [128, 112])
        recs = sb("recs", [128, 112])
        iota_p = sb("iota_p2", [128, 1])
        ident32 = sb("ident2", [128, 128])
        identb = sb("identb", [128, 128], BF16)
        stat = sb("stat2", [128, 128])
        wamb = sb("wamb", [128, 16])
        scl = sb("scl2", [128, 4])

        psA = psum("psA2", [128, 2048])
        psB = psum("psB2", [128, 1024])
        psC = psum("psC2", [128, 1024], BF16)
        psT = psum("psT2", [128, 1024], BF16)

        P = _Prog(nc, st, ("sq", "gq", "aq", "pe", "sc", "ve", "po",
                           "x0", "x1", "wo0", "wo1", "wo2", "ws0", "ws1", "stq",
                           "stw0", "stw1", "sy0", "sy1"))
        inc, w, ss = P.inc, P.w, P.ss
        blk = st.enter_context(nc.Block())
        _preload_tables(nc, blk)

        def _stop():
            @blk.sync
            def _(sync):
                for ch in P.cnt:
                    if P.cnt[ch]:
                        sync.wait_ge(P.sems[ch], P.cnt[ch])

        @blk.gpsimd
        def _(g):
            inc(g.memset(ident32[:], 0.0), "po", 1, "id0")
            g.iota(iota_p[:], [[0, 1]], base=0, channel_multiplier=1,
                   allow_small_or_imprecise_dtypes=True)
            w(g, "id0")
            inc(g.affine_select(out=ident32[:], in_=ident32[:], compare_op=ALU.not_equal,
                                fill=1.0, base=0, pattern=[[-1, 128]],
                                channel_multiplier=1), "po", 1, "const")
            inc(g.iota(selw[:], [[2, 64], [0, 2], [1, 2], [0, 32]], base=0,
                       channel_multiplier=0,
                       allow_small_or_imprecise_dtypes=True), "po", 1, "selw_i")

        @blk.vector
        def _(v):
            w(v, "const")
            inc(v.tensor_copy(identb[:], ident32[:]), "ve", 1, "identb")
            w(v, "selw_i")
            inc(v.tensor_scalar(selw[:], selw[:], iota_p[:], 0.0,
                                ALU.subtract, ALU.is_equal), "ve", 1, "selw")
            ss(v, v.memset(RHS[:], 0.0), "ve")
            for j_ in range(2):
                den_t = RHS[j_ * 32:(j_ + 1) * 32, :].rearrange(
                    "p (g c) -> p g c", c=68)[:, :, j_ * 34 + 32:j_ * 34 + 33]
                ss(v, v.memset(den_t, 1.0), "ve")
                den_b = RHS[64 + j_ * 32:64 + (j_ + 1) * 32, :].rearrange(
                    "p (g c) -> p g c", c=68)[:, :, j_ * 34 + 33:j_ * 34 + 34]
                ss(v, v.memset(den_b, 1.0), "ve")
            inc(v.engine_nop(), "ve", 1, "rhs_ones")

        @blk.sync
        def _(sync):
            inc(sync.dma_start(out=q_sb[:], in_=qin[:, :]), "sq", 16)
            inc(sync.dma_start(out=kt[:], in_=ktin[:, :]), "sq", 16)
            inc(sync.dma_start(out=kb[:], in_=kbin[:, :]), "sq", 16)
            inc(sync.dma_start(out=cvt[:], in_=cvtin[:, :]), "sq", 16)
            inc(sync.dma_start(out=cvb[:], in_=cvbin[:, :]), "sq", 16)
            inc(sync.dma_start(out=wamb[:], in_=wam[0:1, :].broadcast_to([128, 16])),
                "sq", 16, "loads")

        _sc = nc.enter_named_scope("l2_prep", False)[0]
        @blk.vector
        def _(v):
            w(v, "loads")
            for sel, srcT in ((0, kt), (1, kb)):
                cur = srcT
                for p in range(4):
                    col = sel * 32 + p * 8
                    ss(v, v.max(m82[:, col:col + 8], cur[:]), "ve")
                    ss(v, v.max_index(idx2[:, col:col + 8], m82[:, col:col + 8], cur[:]), "ve")
                    if p < 3:
                        nxt = work[p % 2]
                        ss(v, v.match_replace(nxt[:], in_to_replace=m82[:, col:col + 8],
                                              in_values=cur[:], imm_value=NEG), "ve")
                        cur = nxt
            inc(v.tensor_scalar(idx2f[:], idx2[:], 0.0, None, ALU.add, ALU.bypass),
                "ve", 1, "sel2")
            inc(v.tensor_copy(qbf[:], q_sb[:]), "ve", 1, "qbf")
            ss(v, v.reduce_max(scl[:, 0:1], wamb[:, 0:8], axis=AX.X), "ve")
            ss(v, v.reduce_max(scl[:, 1:2], wamb[:, 8:16], axis=AX.X), "ve")
            ss(v, v.tensor_scalar(scl[:, 0:2], scl[:, 0:2], 1.0 / 448.0, 1e-6,
                                  ALU.mult, ALU.max), "ve")
            ss(v, v.reciprocal(scl[:, 2:4], scl[:, 0:2]), "ve")
            inc(v.tensor_scalar(scl[:, 2:4], scl[:, 2:4], 0.5, None, ALU.mult,
                                ALU.bypass), "ve", 1, "scl")

        @blk.tensor
        def _(t):
            w(t, "sel2")
            w(t, "const")
            inc(t.transpose(psB[0:32, 0:128], m82[:, 0:32], ident32[:]), "pe", 1, "m8t")

        @blk.vector
        def _(v):
            w(v, "m8t")
            inc(v.tensor_copy(m82T[:, 0:128], psB[0:32, 0:128]), "ve", 1, "m8tc")

        @blk.tensor
        def _(t):
            w(t, "m8tc")
            inc(t.transpose(psB[0:32, 0:128], m82[:, 32:64], ident32[:]), "pe", 1, "m8b")

        @blk.vector
        def _(v):
            w(v, "m8b")
            ss(v, v.tensor_scalar(m82T[:, 128:256], psB[0:32, 0:128], -1.0, None,
                                  ALU.mult, ALU.bypass), "ve")
            for sel in range(2):
                for j in range(2):
                    ss(v, v.tensor_copy(
                        scol[sel * 64 + j * 32:sel * 64 + (j + 1) * 32, :],
                        m82T[0:32, sel * 128 + j:sel * 128 + 128:2]), "ve")
            inc(v.engine_nop(), "ve", 1, "scol")

        @blk.sync
        def _(sync):
            w(sync, "scol")
            with nc.allow_non_contiguous_dma(reason="tiny 64-elem shift rows"):
                inc(sync.dma_start(out=kmq_scr[0:1, :], in_=m82T[0:1, 0:128:2]), "sq", 16)
                inc(sync.dma_start(out=kmq_scr[1:2, :], in_=m82T[0:1, 1:128:2]), "sq", 16)
                inc(sync.dma_start(out=kmq_scr[2:3, :], in_=m82T[0:1, 128:256:2]), "sq", 16)
                inc(sync.dma_start(out=kmq_scr[3:4, :], in_=m82T[0:1, 129:256:2]),
                    "sq", 16, "kmw")
            w(sync, "kmw")
            for rr in range(4):
                mk = "kmr" if rr == 3 else None
                inc(sync.dma_start(out=kmc[rr * 32:(rr + 1) * 32, :],
                                   in_=kmq_scr[rr:rr + 1, :].broadcast_to([32, 64])),
                    "sq", 16, mk)

        @blk.vector
        def _(v):
            w(v, "kmr")
            inc(v.tensor_tensor(scolM[:], scol[:], kmc[:], ALU.subtract),
                "ve", 1, "scolM")

        for i in range(8):
            @blk.tensor
            def _(t, i=i):
                if i == 0:
                    w(t, "scol")
                    w(t, "loads")
                else:
                    w(t, f"qpos{i - 1}")
                inc(t.transpose(psB[:, 128:256], q_sb[:, i * 128:(i + 1) * 128],
                                ident32[:]), "pe", 1, f"qtr{i}")

            @blk.vector
            def _(v, i=i):
                w(v, f"qtr{i}")
                inc(v.tensor_scalar(qpos[:, i * H:(i + 1) * H], psB[:, 128:256],
                                    0.0, None, ALU.is_gt, ALU.bypass),
                    "ve", 1, f"qpos{i}")

        if stage <= 1:
            _stop()
            return nc

        nc.leave_named_scope("l2_prep", _sc, False)
        _sc = nc.enter_named_scope("l2_cand", False)[0]
        with ExitStack() as st2:
            idx2_bc = st2.enter_context(nc.sbuf_tensor("idx2_bc", [128, 8192], F16))
            g2t = st2.enter_context(nc.sbuf_tensor("g2t", [128, 8192], BF16))
            g2b = st2.enter_context(nc.sbuf_tensor("g2b", [128, 8192], BF16))

            @blk.vector
            def _(v):
                inc(v.memset(g2t[:], 0.0), "ve", 1, "g2t0")
                inc(v.memset(g2b[:], 0.0), "ve", 1, "g2b0")

            @blk.gpsimd
            def _(g):
                w(g, "sel2")
                inc(g.dma_start(out=idx2_scr[:, :], in_=idx2f[:]), "gq", 16, "i2w")
                w(g, "i2w")
                flat = idx2_scr.rearrange("a b -> (a b)")
                inc(g.dma_start(out=idx2_bc[:, :],
                                in_=flat[None, :].broadcast_to([128, 8192])),
                    "gq", 16, "i2bc")

            @blk.vector
            def _(v):
                w(v, "i2bc")
                w(v, "g2t0")
                w(v, "g2b0")
                g2t_out = g2t[:].rearrange("p (g s j r) -> p g s j r",
                                           s=2, j=2, r=32)[:, :, 0]
                i2t_in = idx2_bc[:].rearrange("p (h s r) -> p h s r",
                                              s=2, r=32)[:, :, 0]
                ss(v, v.tensor_scalar(g2t_out, i2t_in, iota_p[:], 0.0,
                                      ALU.subtract, ALU.is_equal), "ve")
                g2b_out = g2b[:].rearrange("p (g s j r) -> p g s j r",
                                           s=2, j=2, r=32)[:, :, 1]
                i2b_in = idx2_bc[:].rearrange("p (h s r) -> p h s r",
                                              s=2, r=32)[:, :, 1]
                inc(v.tensor_scalar(g2b_out, i2b_in, iota_p[:], 0.0,
                                    ALU.subtract, ALU.is_equal), "ve", 1, "g2")

            for wv_ in range(4):
                for gg in range(16):
                    g_ = wv_ * 16 + gg
                    scol0 = (gg % 4) * 512 + (gg // 4) * 32

                    @blk.tensor
                    def _(t, g_=g_, gg=gg, wv_=wv_, scol0=scol0):
                        if g_ == 0:
                            w(t, "g2")
                        if g_ >= 4:
                            w(t, f"rhscp{g_ - 4}")
                        first = (gg < 4)
                        t.matmul(psA[:, scol0:scol0 + 32],
                                 g2t[:, g_ * 128:(g_ + 1) * 128],
                                 cvt[:, g_ * 32:(g_ + 1) * 32],
                                 start=first, stop=False, skip_group_check=True)
                        inc(t.matmul(psA[:, scol0:scol0 + 32],
                                     g2b[:, g_ * 128:(g_ + 1) * 128],
                                     cvb[:, g_ * 32:(g_ + 1) * 32],
                                     start=False, stop=True, skip_group_check=True),
                            "pe", 1, f"cv{g_}")

                    @blk.vector
                    def _(v, g_=g_, scol0=scol0):
                        w(v, f"cv{g_}")
                        if g_ == 0:
                            w(v, "rhs_ones")
                        col = g_ * 68
                        for j in range(2):
                            v.tensor_copy(RHS[j * 32:(j + 1) * 32,
                                              col + j * 34:col + j * 34 + 16],
                                          psA[j * 32:(j + 1) * 32,
                                              scol0 + j * 16:scol0 + (j + 1) * 16])
                            v.tensor_copy(RHS[64 + j * 32:64 + (j + 1) * 32,
                                              col + j * 34 + 16:col + j * 34 + 32],
                                          psA[64 + j * 32:64 + (j + 1) * 32,
                                              scol0 + j * 16:scol0 + (j + 1) * 16])
                        inc(v.engine_nop(), "ve", 1, f"rhscp{g_}")

        if stage <= 2:
            _stop()
            return nc

        nc.leave_named_scope("l2_cand", _sc, False)
        _sc = nc.enter_named_scope("l2_attn", False)[0]
        def emit_w8_load(a):
            @blk.gpsimd
            def _(g, a=a):
                if a < 2:
                    w(g, "rhscp63")
                if a >= 2:
                    w(g, f"w8q{a - 2}")
                if a < 16:
                    inc(g.dma_start(out=wq_buf[a % 2][:],
                                    in_=wfc_sl[a * 128:(a + 1) * 128, :]),
                        f"ws{a % 2}", 16, f"wfl{a}")
                else:
                    kk = a - 16
                    for j_ in range(4):
                        r = kk * 4 + j_
                        inc(g.dma_start(out=wq_buf[a % 2][:, j_ * 256:(j_ + 1) * 256],
                                        in_=wproj_sl[r * 128:(r + 1) * 128, :]),
                            f"ws{a % 2}", 16, f"wfl{a}" if j_ == 3 else None)

        def emit_w8_quant(a):
            @blk.scalar
            def _(s_, a=a):
                w(s_, f"wfl{a}")
                if a == 0:
                    w(s_, "scl")
                if a >= 2:
                    w(s_, f"w8st{a - 2}")
                inc(s_.activation(w8_buf[a % 2][:], wq_buf[a % 2][:], AF.Copy,
                                  scale=scl[:, 2:3] if a < 16 else scl[:, 3:4]),
                    "sc", 1, f"w8q{a}")

            @blk.sync
            def _(sync, a=a):
                w(sync, f"w8q{a}")
                if a < 16:
                    inc(sync.dma_start(out=o_w8fc[a * 128:(a + 1) * 128, :],
                                       in_=w8_buf[a % 2][:]),
                        f"stw{a % 2}", 16, f"w8st{a}")
                else:
                    kk = a - 16
                    for j_ in range(4):
                        r = kk * 4 + j_
                        inc(sync.dma_start(out=o_w8pj[r * 128:(r + 1) * 128, :],
                                           in_=w8_buf[a % 2][:, j_ * 256:(j_ + 1) * 256]),
                            f"stw{a % 2}", 16, f"w8st{a}" if j_ == 3 else None)

            if a + 2 < 32:
                emit_w8_load(a + 2)

        GBLK = [list(range(a, min(a + 7, 64))) for a in range(0, 64, 7)]
        psE = [psB[:, 0:512], psB[:, 512:1024]]
        psO = [psA[:, b * 512:(b + 1) * 512] for b in range(4)]

        afm = sb("afm", [128, 16 * S], BF16)
        x_ch = [sb(f"x2c{i}", [128, D]) for i in range(2)]
        wo_cache = sb("wo_cache", [128, 16 * 512], BF16)
        y_blk = [sb(f"yblk{i}", [128, D]) for i in range(2)]
        wq_buf = [sb(f"wqb{i}", [128, 1024]) for i in range(2)]
        w8_buf = [sb(f"w8b{i}", [128, 1024], F8) for i in range(2)]
        junk2 = sb("junk2", [128, 512])
        stat3 = sb("stat3b", [128, 64])
        wof32 = [sb(f"wof32_{i}", [128, 512]) for i in range(2)]

        emit_w8_load(0)
        emit_w8_load(1)

        trslot = [psT[:, 0:512], psC[:, 0:512]]
        _ntr = [0]

        def emit_E(bi, blkg):
            base = (bi % 2) * 7 * S
            for sl_, g_ in enumerate(blkg):
                for hf in range(2):
                    k2 = 2 * g_ + hf
                    eslot = k2 % 2

                    @blk.tensor
                    def _(t, bi=bi, blkg=blkg, g_=g_, hf=hf, k2=k2, eslot=eslot):
                        if bi == 0 and g_ == blkg[0] and hf == 0:
                            w(t, "selw")
                            w(t, "qbf")
                            w(t, "rhscp63")
                            w(t, "qpos7")
                        if k2 >= 2:
                            w(t, f"exp{k2 - 2}")
                        inc(t.matmul(psE[eslot], selw[:, g_ * 128:(g_ + 1) * 128],
                                     qbf[:, hf * 512:(hf + 1) * 512],
                                     start=True, stop=True), "pe", 1, f"eb{k2}")

                    @blk.scalar
                    def _(s_, bi=bi, sl_=sl_, g_=g_, hf=hf, k2=k2, eslot=eslot,
                          base=base):
                        w(s_, f"eb{k2}")
                        if k2 == 0:
                            w(s_, "scolM")
                        if bi >= 2 and sl_ == 0 and hf == 0:
                            w(s_, f"combdone{bi - 2}")
                        inc(s_.activation(e_sb[:, base + sl_ * S + hf * 512:
                                               base + sl_ * S + (hf + 1) * 512],
                                          psE[eslot], AF.Exp,
                                          scale=scolM[:, g_:g_ + 1]),
                            "sc", 1, f"exp{k2}")

        def emit_comb(bi, blkg, nb, hx):
            base = (bi % 2) * 7 * S

            @blk.tensor
            def _(t, bi=bi, blkg=blkg, nb=nb, hx=hx, base=base):
                w(t, f"exp{2 * blkg[-1] + 1}")
                if hx == 0:
                    if bi >= 1:
                        w(t, f"nselB{bi - 1}")
                else:
                    w(t, f"nselA{bi}")
                mm = None
                for b in range(4):
                    i = hx * 4 + b
                    for sl_, g_ in enumerate(blkg):
                        mm = t.matmul(
                            psO[b][:, sl_ * 68:(sl_ + 1) * 68],
                            e_sb[:, base + sl_ * S + i * 128:
                                 base + sl_ * S + (i + 1) * 128],
                            RHS[:, g_ * 68:(g_ + 1) * 68],
                            start=(sl_ == 0), stop=(sl_ == nb - 1),
                            skip_group_check=True)
                inc(mm, "pe", 1, f"comb{bi}_{hx}")
                if hx == 1:
                    P.M[f"combdone{bi}"] = ("pe", P.cnt["pe"])

        def emit_norm(bi, blkg, nb, hx):
            g0 = blkg[0]

            @blk.vector
            def _(v, bi=bi, blkg=blkg, nb=nb, hx=hx, g0=g0):
                w(v, f"comb{bi}_{hx}")
                dens = psA[:, 0:2048].rearrange("p (b x) -> p b x", b=4)[
                    :, :, 0:nb * 68].rearrange("p b (u j c) -> p b u j c",
                                               j=2, c=34)[:, :, :, :, 32:34]
                recs_v = recs[:, 0:4 * nb * 4].rearrange(
                    "p (b u j s) -> p b u j s", b=4, j=2, s=2)
                ss(v, v.tensor_copy(recs_v, dens), "ve")
                ss(v, v.reciprocal(recd[:, 0:4 * nb * 4], recs[:, 0:4 * nb * 4]),
                   "ve")
                for b in range(4):
                    po = psO[b]
                    num_ap = po[:, 0:nb * 68].rearrange(
                        "p (u j c) -> p u j c", j=2, c=34)[:, :, :, 0:32].rearrange(
                        "p u j (s c) -> p u j s c", c=16)
                    rec_ex = recd[:, b * nb * 4:(b + 1) * nb * 4].rearrange(
                        "p (u j s) -> p u j s", j=2, s=2)[
                        :, :, :, :, None].broadcast_to([128, nb, 2, 2, 16])
                    tt_out5 = attnb[:, b * 448:b * 448 + 448].rearrange(
                        "p (s uj c) -> p uj s c", s=2, c=16).rearrange(
                        "p (u j) s c -> p u j s c", j=2)[:, 0:nb]
                    ss(v, v.tensor_tensor(tt_out5, num_ap, rec_ex, ALU.mult), "ve")
                for b in range(4):
                    i = hx * 4 + b
                    mask3 = qpos[:, i * H + 2 * g0:i * H + 2 * (g0 + nb)][
                        :, :, None].broadcast_to([128, nb * 2, 16])
                    mx3 = maskx[:, 0:nb * 32].rearrange("p (u c) -> p u c", c=16)
                    ss(v, v.tensor_copy(mx3, mask3), "ve")
                    out2 = att[:, i * D + g0 * 32:i * D + (g0 + nb) * 32]
                    ss(v, v.tensor_copy(
                        out2, attnb[:, b * 448 + 224:b * 448 + 224 + nb * 32]), "ve")
                    cp = v.copy_predicated(out2, maskx[:, 0:nb * 32],
                                           attnb[:, b * 448:b * 448 + nb * 32])
                    if b == 3:
                        inc(cp, "ve", 1, f"nsel{'AB'[hx]}{bi}")
                    else:
                        ss(v, cp, "ve")

        def emit_tr(bi, i_list, cs):
            ihalf = i_list[0] // 4
            for c in cs:
                n = _ntr[0]
                _ntr[0] += 1

                @blk.tensor
                def _(t, n=n, c=c, bi=bi, ihalf=ihalf):
                    if n == 0:
                        w(t, "identb")
                    w(t, f"nsel{'AB'[ihalf]}{bi}")
                    if n >= 2:
                        w(t, f"aev{n - 2}")
                    mm = None
                    for k in range(4):
                        i = ihalf * 4 + k
                        mm = t.transpose(
                            trslot[n % 2][:, k * 128:(k + 1) * 128],
                            att[:, i * D + c * 128:i * D + (c + 1) * 128],
                            identb[:])
                    inc(mm, "pe", 1, f"atr{n}")

                @blk.scalar
                def _(s_, n=n, c=c, ihalf=ihalf):
                    w(s_, f"atr{n}")
                    inc(s_.activation(afm[:, c * S + ihalf * 512:
                                           c * S + (ihalf + 1) * 512],
                                      trslot[n % 2], AF.Copy), "sc", 1, f"aev{n}")

        cs_done = {bi: [c for c in range(16) if (4 * c + 3) // 7 == bi]
                   for bi in range(10)}
        for bi, blkg in enumerate(GBLK):
            nb = len(blkg)
            if bi >= 1:
                pb = GBLK[bi - 1]
                emit_comb(bi - 1, pb, len(pb), 1)
                emit_norm(bi - 1, pb, len(pb), 1)
                emit_tr(bi - 1, [0, 1, 2, 3], cs_done[bi - 1])
            emit_E(bi, blkg)
            for _qa in range(bi * 4, min(bi * 4 + 4, 32)):
                emit_w8_quant(_qa)
            emit_comb(bi, blkg, nb, 0)
            emit_norm(bi, blkg, nb, 0)
            if bi >= 1:
                emit_tr(bi - 1, [4, 5, 6, 7], cs_done[bi - 1])
        emit_comb(9, GBLK[9], len(GBLK[9]), 1)
        emit_norm(9, GBLK[9], len(GBLK[9]), 1)
        emit_tr(9, [0, 1, 2, 3], cs_done[9])
        emit_tr(9, [4, 5, 6, 7], cs_done[9])
        assert _ntr[0] == 32

        if stage <= 3:
            _stop()
            return nc

        nc.leave_named_scope("l2_attn", _sc, False)
        _sc = nc.enter_named_scope("l2_wo", False)[0]
        for e in range(4):
            @blk.gpsimd
            def _(g, e=e):
                if e == 0:
                    w(g, "rhscp63")
                else:
                    w(g, f"womh{e - 1}_1")
                for c in range(8):
                    inc(g.dma_start(out=wo_cache[:, c * 512:(c + 1) * 512],
                                    in_=woT[c * 128:(c + 1) * 128,
                                            e * 512:(e + 1) * 512]),
                        "wo0", 16, f"wog{e}" if c == 7 else None)

            for c in range(8, 16):
                kc = e * 8 + (c - 8)

                @blk.sync
                def _(sync, e=e, c=c, kc=kc):
                    if kc < 2:
                        w(sync, "rhscp63")
                    else:
                        w(sync, f"wocst{kc - 2}")
                    inc(sync.dma_start(out=wof32[kc % 2][:],
                                       in_=woT[c * 128:(c + 1) * 128,
                                               e * 512:(e + 1) * 512]),
                        f"wo{1 + kc % 2}", 16, f"wol{kc}")

                @blk.vector
                def _(v, e=e, c=c, kc=kc):
                    w(v, f"wol{kc}")
                    if c == 8 and e >= 1:
                        w(v, f"womm{e - 1}_1")
                    inc(v.tensor_copy(wo_cache[:, c * 512:(c + 1) * 512],
                                      wof32[kc % 2][:]),
                        "ve", 1,
                        f"wocst{kc}" if c < 15 else f"wos{e}")
                    if c == 15:
                        P.M[f"wocst{kc}"] = P.M[f"wos{e}"]

            for hh in range(2):
                gw = e * 2 + hh

                @blk.sync
                def _(sync, e=e, hh=hh, gw=gw):
                    if gw < 2:
                        w(sync, "rhscp63")
                    else:
                        w(sync, f"yev{gw - 2}")
                    for ii in range(4):
                        i = hh * 4 + ii
                        inc(sync.dma_start(
                            out=x_ch[hh][:, ii * 512:(ii + 1) * 512],
                            in_=x[i * 128:(i + 1) * 128, e * 512:(e + 1) * 512]),
                            f"x{hh}", 16, f"xs{gw}" if ii == 3 else None)

                @blk.tensor
                def _(t, e=e, hh=hh, gw=gw):
                    w(t, f"wog{e}")
                    w(t, f"wos{e}")
                    if gw == 0:
                        w(t, "aev31")
                    if gw >= 1:
                        w(t, f"yev{gw - 1}")
                    mm = None
                    for c in range(16):
                        for ii in range(4):
                            i = hh * 4 + ii
                            mm = t.matmul(
                                psO[ii][:, 0:512],
                                afm[:, c * S + i * 128:c * S + (i + 1) * 128],
                                wo_cache[:, c * 512:(c + 1) * 512],
                                start=(c == 0), stop=(c == 15),
                                skip_group_check=True)
                        if c == 7:
                            inc(mm, "pe", 1, f"womh{e}_{hh}")
                    inc(mm, "pe", 1, f"womm{e}_{hh}")

                @blk.vector
                def _(v, e=e, hh=hh, gw=gw):
                    w(v, f"womm{e}_{hh}")
                    w(v, f"xs{gw}")
                    if gw >= 2:
                        w(v, f"ysw{gw - 2}")
                        w(v, f"yst{gw - 2}")
                        w(v, f"yrd{gw - 2}")
                    yb = y_blk[gw % 2]
                    mm = None
                    for ii in range(4):
                        ss(v, v.tensor_tensor(yb[:, ii * 512:(ii + 1) * 512],
                                              psO[ii][:, 0:512],
                                              x_ch[hh][:, ii * 512:(ii + 1) * 512],
                                              ALU.add), "ve")
                    inc(v.engine_nop(), "ve", 1, f"yev{gw}")

                @blk.vector
                def _(v, e=e, hh=hh, gw=gw):
                    yb = y_blk[gw % 2]
                    for ii in range(4):
                        ig = hh * 4 + ii
                        ss(v, v.tensor_reduce(
                            stat[:, 64 + e * 8 + ig:65 + e * 8 + ig],
                            yb[:, ii * 512:(ii + 1) * 512],
                            axis=AX.X, op=ALU.max), "ve")
                        ss(v, v.tensor_reduce(
                            stat[:, 96 + e * 8 + ig:97 + e * 8 + ig],
                            yb[:, ii * 512:(ii + 1) * 512],
                            axis=AX.X, op=ALU.min), "ve")
                    inc(v.engine_nop(), "ve", 1, f"yrd{gw}")

                @blk.scalar
                def _(s_, e=e, hh=hh, gw=gw):
                    w(s_, f"yev{gw}")
                    if gw >= 1:
                        w(s_, f"yst{gw - 1}")
                    yb = y_blk[gw % 2]
                    act = None
                    for ii in range(4):
                        ig = hh * 4 + ii
                        ss(s_, s_.activation(junk2[:],
                                             yb[:, ii * 512:(ii + 1) * 512],
                                             AF.Copy,
                                             accum_out=stat[:, e * 8 + ig:
                                                            e * 8 + ig + 1]), "sc")
                        act = s_.activation(junk2[:],
                                            yb[:, ii * 512:(ii + 1) * 512],
                                            AF.Square,
                                            accum_out=stat[:, 32 + e * 8 + ig:
                                                           33 + e * 8 + ig])
                        if ii < 3:
                            ss(s_, act, "sc")
                    inc(act, "sc", 1, f"yst{gw}")

                @blk.sync
                def _(sync, e=e, hh=hh, gw=gw):
                    w(sync, f"yev{gw}")
                    for ii in range(4):
                        i = hh * 4 + ii
                        inc(sync.dma_start(
                            out=o_y[i * 128:(i + 1) * 128, e * 512:(e + 1) * 512],
                            in_=y_blk[gw % 2][:, ii * 512:(ii + 1) * 512]),
                            f"sy{gw % 2}", 16, f"ysw{gw}" if ii == 3 else None)

        if stage <= 5:
            _stop()
            return nc

        @blk.vector
        def _(v):
            w(v, "yst7")
            w(v, "yrd7")
            for base, op, dst in ((0, ALU.add, 0), (32, ALU.add, 8),
                                  (64, ALU.max, 16), (96, ALU.min, 24)):
                ss(v, v.tensor_tensor(stat3[:, dst:dst + 8],
                                      stat[:, base:base + 8],
                                      stat[:, base + 8:base + 16], op), "ve")
                ss(v, v.tensor_tensor(stat3[:, 32 + dst:40 + dst],
                                      stat[:, base + 16:base + 24],
                                      stat[:, base + 24:base + 32], op), "ve")
                ss(v, v.tensor_tensor(stat3[:, dst:dst + 8],
                                      stat3[:, dst:dst + 8],
                                      stat3[:, 32 + dst:40 + dst], op), "ve")
            ss(v, v.tensor_scalar(stat3[:, 32:40], stat3[:, 0:8], 1.0 / D, None,
                                  ALU.mult, ALU.bypass), "ve")
            ss(v, v.tensor_scalar(stat3[:, 40:48], stat3[:, 8:16], 1.0 / D, None,
                                  ALU.mult, ALU.bypass), "ve")
            ss(v, v.tensor_tensor(stat3[:, 48:56], stat3[:, 32:40],
                                  stat3[:, 32:40], ALU.mult), "ve")
            ss(v, v.tensor_tensor(stat3[:, 40:48], stat3[:, 40:48],
                                  stat3[:, 48:56], ALU.subtract), "ve")
            inc(v.tensor_scalar(stat3[:, 40:48], stat3[:, 40:48], EPS, None,
                                ALU.add, ALU.bypass), "ve", 1, "yvar")

        @blk.scalar
        def _(s_):
            w(s_, "yvar")
            ss(s_, s_.activation(stat3[:, 48:56], stat3[:, 40:48], AF.Ln), "sc")
            inc(s_.activation(stat3[:, 48:56], stat3[:, 48:56], AF.Exp, scale=-0.5),
                "sc", 1, "yrstd")

        @blk.vector
        def _(v):
            w(v, "yrstd")
            ss(v, v.tensor_tensor(stat3[:, 56:64], stat3[:, 16:24],
                                  stat3[:, 32:40], ALU.subtract), "ve")
            ss(v, v.tensor_tensor(stat3[:, 16:24], stat3[:, 32:40],
                                  stat3[:, 24:32], ALU.subtract), "ve")
            ss(v, v.tensor_tensor(stat3[:, 56:64], stat3[:, 56:64],
                                  stat3[:, 16:24], ALU.max), "ve")
            ss(v, v.tensor_tensor(stat3[:, 56:64], stat3[:, 56:64],
                                  stat3[:, 48:56], ALU.mult), "ve")
            inc(v.reduce_max(stat3[:, 62:63], stat3[:, 56:64], axis=AX.X),
                "ve", 1, "xam_col")

        @blk.gpsimd
        def _(g):
            w(g, "xam_col")
            inc(g.tensor_reduce(stat3[0:1, 63:64], stat3[:, 62:63], axis=AX.C,
                                op=ALU.max), "po", 1, "xam")

        @blk.sync
        def _(sync):
            w(sync, "xam")
            inc(sync.dma_start(out=o_xam[0:1, :], in_=stat3[0:1, 63:64]),
                "stq", 16, "xam_out")

        nc.leave_named_scope("l2_wo", _sc, False)
        _sc = nc.enter_named_scope("l2_w8", False)[0]
        @blk.sync
        def _(sync):
            sync.wait_ge(P.sems["stq"], P.cnt["stq"])
            sync.wait_ge(P.sems["stw0"], P.cnt["stw0"])
            sync.wait_ge(P.sems["stw1"], P.cnt["stw1"])

        nc.leave_named_scope("l2_w8", _sc, False)

    return nc



def build_L3():
    nc = bass.Bass("TRN2", target_bir_lowering=False)
    y_in = nc.dram_tensor("y", [S, D], F32, kind="ExternalInput").ap()
    w8st = nc.dram_tensor("w8st", [64 * 16 * 128, 128], F8, kind="ExternalInput").ap()
    xam8 = nc.dram_tensor("xam8", [1, 8], F32, kind="ExternalInput").ap()
    o_hr = nc.dram_tensor("hrelu", [FF, S], BF16, kind="ExternalOutput").ap()
    o_ham = nc.dram_tensor("ham", [1, 1], F32, kind="ExternalOutput").ap()

    with ExitStack() as st:
        sb = lambda n, s, d=F32: st.enter_context(nc.sbuf_tensor(n, s, d))
        psum = lambda n, s, d=F32: st.enter_context(nc.psum_tensor(n, s, d))

        y_sb = sb("y3", [128, 8 * D])
        x8t = [sb(f"x8t{i}", [128, D], F8) for i in range(2)]
        x8b = [sb(f"x8b{i}", [128, D], BF16) for i in range(2)]
        x8fm = sb("x8fm", [128, 16 * S], F8)
        wst = [sb(f"wst{i}", [128, 16 * 128], F8) for i in range(3)]
        hst = [sb(f"hst{i}", [128, S], BF16) for i in range(2)]
        hamc = sb("hamc", [128, 66])
        stat = sb("stat3", [128, 64])
        xamb = sb("xamb", [128, 8])
        junk = [sb(f"junk3{i}", [128, D]) for i in range(2)]
        ident32 = sb("ident3", [128, 128])
        identb = sb("identb3", [128, 128], BF16)

        psF = [psum("psF0", [128, 1024]), psum("psF1", [128, 1024])]
        psD = [psum("psD0", [128, 1024], BF16),
               psum("psD1", [128, 1024], BF16)]

        P = _Prog(nc, st, ("sq", "gq", "pe", "sc", "ve", "po",
                           "y0", "y1", "y2", "y3", "y4", "y5", "y6", "y7",
                           "wt0", "wt1", "wt2", "st0", "st1", "sto"))
        inc, w, ss = P.inc, P.w, P.ss
        blk = st.enter_context(nc.Block())

        @blk.gpsimd
        def _(g):
            inc(g.memset(ident32[:], 0.0), "po", 1, "id0")
            w(g, "id0")
            inc(g.affine_select(out=ident32[:], in_=ident32[:], compare_op=ALU.not_equal,
                                fill=1.0, base=0, pattern=[[-1, 128]],
                                channel_multiplier=1), "po", 1, "idf")

        @blk.vector
        def _(v):
            w(v, "idf")
            inc(v.tensor_copy(identb[:], ident32[:]), "ve", 1, "identb")

        @blk.sync
        def _(sync):
            inc(sync.dma_start(out=xamb[:], in_=xam8[0:1, :].broadcast_to([128, 8])),
                "sq", 16, "xld")
            for i in range(8):
                inc(sync.dma_start(out=y_sb[:, i * D:(i + 1) * D],
                                   in_=y_in[i * 128:(i + 1) * 128, :]),
                    f"y{i}", 16, f"yld{i}")

        _sc = nc.enter_named_scope("l3_stats", False)[0]
        for i in range(8):
            @blk.scalar
            def _(s_, i=i):
                w(s_, f"yld{i}")
                if i > 1:
                    w(s_, f"yst{i - 2}")
                ss(s_, s_.activation(junk[i % 2][:], y_sb[:, i * D:(i + 1) * D],
                                     AF.Copy, accum_out=stat[:, i:i + 1]), "sc")
                inc(s_.activation(junk[i % 2][:], y_sb[:, i * D:(i + 1) * D],
                                  AF.Square, accum_out=stat[:, 8 + i:9 + i]),
                    "sc", 1, f"yst{i}")

        @blk.vector
        def _(v):
            w(v, "yst7")
            ss(v, v.tensor_scalar(stat[:, 16:24], stat[:, 0:8], 1.0 / D, None,
                                  ALU.mult, ALU.bypass), "ve")
            ss(v, v.tensor_scalar(stat[:, 24:32], stat[:, 8:16], 1.0 / D, None,
                                  ALU.mult, ALU.bypass), "ve")
            ss(v, v.tensor_tensor(stat[:, 48:56], stat[:, 16:24], stat[:, 16:24],
                                  ALU.mult), "ve")
            ss(v, v.tensor_tensor(stat[:, 24:32], stat[:, 24:32], stat[:, 48:56],
                                  ALU.subtract), "ve")
            inc(v.tensor_scalar(stat[:, 24:32], stat[:, 24:32], EPS, None,
                                ALU.add, ALU.bypass), "ve", 1, "var3")

        @blk.scalar
        def _(s_):
            w(s_, "var3")
            ss(s_, s_.activation(stat[:, 32:40], stat[:, 24:32], AF.Ln), "sc")
            inc(s_.activation(stat[:, 32:40], stat[:, 32:40], AF.Exp, scale=-0.5),
                "sc", 1, "rstd3")

        @blk.vector
        def _(v):
            w(v, "rstd3")
            w(v, "xld")
            ss(v, v.reduce_max(stat[:, 56:57], xamb[:, 0:8], axis=AX.X), "ve")
            ss(v, v.tensor_scalar(stat[:, 57:58], stat[:, 56:57], 1.0 / 448.0, 1e-6,
                                  ALU.mult, ALU.max), "ve")
            ss(v, v.reciprocal(stat[:, 58:59], stat[:, 57:58]), "ve")
            ss(v, v.tensor_scalar(stat[:, 58:59], stat[:, 58:59], 0.5, None,
                                  ALU.mult, ALU.bypass), "ve")
            ss(v, v.tensor_tensor(stat[:, 40:48], stat[:, 32:40],
                                  stat[:, 58:59].broadcast_to([128, 8]),
                                  ALU.mult), "ve")
            ss(v, v.tensor_tensor(stat[:, 48:56], stat[:, 16:24], stat[:, 40:48],
                                  ALU.mult), "ve")
            inc(v.tensor_scalar(stat[:, 48:56], stat[:, 48:56], -1.0, None,
                                ALU.mult, ALU.bypass), "ve", 1, "sig3")

        nc.leave_named_scope("l3_stats", _sc, False)
        _sc = nc.enter_named_scope("l3_qtr", False)[0]
        for i in range(8):
            @blk.scalar
            def _(s_, i=i):
                if i == 0:
                    w(s_, "sig3")
                if i >= 2:
                    w(s_, f"cst{i - 2}")
                inc(s_.activation(x8t[i % 2][:], y_sb[:, i * D:(i + 1) * D],
                                  AF.Identity, scale=stat[:, 40 + i:41 + i],
                                  bias=stat[:, 48 + i:49 + i]), "sc", 1, f"xq{i}")

            @blk.vector
            def _(v, i=i):
                w(v, f"xq{i}")
                if i >= 2:
                    w(v, f"trch{i - 2}")
                inc(v.tensor_copy(x8b[i % 2][:], x8t[i % 2][:]), "ve", 1, f"cst{i}")

            for k in range(16):
                n = i * 16 + k

                @blk.tensor
                def _(t, i=i, k=k, n=n):
                    if n == 0:
                        w(t, "identb")
                    if k == 0:
                        w(t, f"cst{i}")
                    if n >= 2:
                        w(t, f"tev{n - 2}")
                    inc(t.transpose(psD[n % 2][:, 0:128],
                                    x8b[i % 2][:, k * 128:(k + 1) * 128],
                                    identb[:]), "pe", 1,
                        f"trch{i}" if k == 15 else f"tr{n}")

                @blk.vector
                def _(v, i=i, k=k, n=n):
                    w(v, f"trch{i}" if k == 15 else f"tr{n}")
                    inc(v.tensor_copy(x8fm[:, k * S + i * 128:k * S + (i + 1) * 128],
                                      psD[n % 2][:, 0:128]), "ve", 1, f"tev{n}")

        P.M["x8done"] = P.M["tev127"]

        nc.leave_named_scope("l3_qtr", _sc, False)
        _sc = nc.enter_named_scope("l3_fc", False)[0]
        for ff in range(64):
            @blk.gpsimd
            def _(g, ff=ff):
                if ff >= 3:
                    w(g, f"fcm{ff - 3}")
                inc(g.dma_start(out=wst[ff % 3][:].rearrange("p (k f) -> p k f", f=128),
                                in_=w8st[ff * 2048:(ff + 1) * 2048, :].rearrange(
                                    "(p k) f -> p k f", k=16)),
                    f"wt{ff % 3}", 16, f"wtld{ff}")

            @blk.tensor
            def _(t, ff=ff):
                w(t, f"wtld{ff}")
                if ff == 0:
                    w(t, "x8done")
                if ff >= 2:
                    w(t, f"fcv{ff - 2}")
                mm = None
                for kk in range(8):
                    for hf in range(2):
                        lhs3 = wst[ff % 3][:, kk * 256:(kk + 1) * 256].rearrange(
                            "p (i m) -> p i m", i=2)
                        rhs3 = x8fm[:, 2 * kk * S:(2 * kk + 2) * S].rearrange(
                            "p (i t) -> p i t", i=2)[:, :, hf * 512:(hf + 1) * 512]
                        mm = t.matmul(psF[ff % 2][:, hf * 512:(hf + 1) * 512],
                                      lhs3, rhs3, start=(kk == 0), stop=(kk == 7),
                                      perf_mode=mybir.MatmulPerfMode.DoubleRow,
                                      skip_group_check=True)
                inc(mm, "pe", 1, f"fcm{ff}")

            @blk.vector
            def _(v, ff=ff):
                w(v, f"fcm{ff}")
                if ff >= 2:
                    w(v, f"hsw{ff - 2}")
                ss(v, v.tensor_scalar(hst[ff % 2][:], psF[ff % 2][:], 0.0, None,
                                      ALU.max, ALU.bypass), "ve")
                inc(v.tensor_reduce(hamc[:, ff:ff + 1], psF[ff % 2][:],
                                    axis=AX.X, op=ALU.max), "ve", 1, f"fcv{ff}")

            @blk.sync
            def _(sync, ff=ff):
                w(sync, f"fcv{ff}")
                inc(sync.dma_start(out=o_hr[ff * 128:(ff + 1) * 128, :],
                                   in_=hst[ff % 2][:]),
                    f"st{ff % 2}", 16, f"hsw{ff}")

        nc.leave_named_scope("l3_fc", _sc, False)
        _sc = nc.enter_named_scope("l3_ham", False)[0]
        @blk.vector
        def _(v):
            w(v, "fcv63")
            ss(v, v.reduce_max(hamc[:, 64:65], hamc[:, 0:64], axis=AX.X), "ve")
            inc(v.tensor_scalar(hamc[:, 64:65], hamc[:, 64:65], 0.0, None,
                                ALU.max, ALU.bypass), "ve", 1, "hcol")

        @blk.gpsimd
        def _(g):
            w(g, "hcol")
            inc(g.tensor_reduce(hamc[0:1, 65:66], hamc[:, 64:65], axis=AX.C,
                                op=ALU.max), "po", 1, "hram")

        @blk.sync
        def _(sync):
            w(sync, "hram")
            inc(sync.dma_start(out=o_ham[:, :], in_=hamc[0:1, 65:66]), "sto", 16)
            for ch in P.cnt:
                if P.cnt[ch]:
                    sync.wait_ge(P.sems[ch], P.cnt[ch])

        nc.leave_named_scope("l3_ham", _sc, False)

    return nc



def build_L4():
    nc = bass.Bass("TRN2", target_bir_lowering=False)
    y_in = nc.dram_tensor("y", [S, D], F32, kind="ExternalInput").ap()
    hr_in = nc.dram_tensor("hrelu", [FF, S], BF16, kind="ExternalInput").ap()
    w8pj = nc.dram_tensor("w8pj", [FF, D], F8, kind="ExternalInput").ap()
    ham8 = nc.dram_tensor("ham8", [1, 8], F32, kind="ExternalInput").ap()
    xam8 = nc.dram_tensor("xam8", [1, 8], F32, kind="ExternalInput").ap()
    wam = nc.dram_tensor("wam", [1, 16], F32, kind="ExternalInput").ap()
    o_out = nc.dram_tensor("out", [S, D], F32, kind="ExternalOutput").ap()

    with ExitStack() as st:
        sb = lambda n, s, d=F32: st.enter_context(nc.sbuf_tensor(n, s, d))
        psum = lambda n, s, d=F32: st.enter_context(nc.psum_tensor(n, s, d))

        y_sb = sb("y4", [128, 8 * D])
        h8 = sb("h8", [128, 64 * S], F8)
        hr = [sb(f"hr{i}", [128, S], BF16) for i in range(3)]
        wpj = sb("wpj", [128, 64 * S], F8)
        osb = [sb(f"osb{i}", [128, S]) for i in range(2)]
        scal = sb("scal", [128, 40])
        psP = [psum("psP0", [128, 1024]), psum("psP1", [128, 1024])]

        P = _Prog(nc, st, ("sq", "yq", "gq", "pe", "sc", "ve", "po",
                           "hq0", "hq1", "hq2", "hq3", "wq0", "wq1",
                           "st0", "st1"))
        inc, w, ss = P.inc, P.w, P.ss
        blk = st.enter_context(nc.Block())
        _preload_tables(nc, blk)

        @blk.sync
        def _(sync):
            inc(sync.dma_start(out=scal[:, 0:8],
                               in_=xam8[0:1, :].broadcast_to([128, 8])), "sq", 16)
            inc(sync.dma_start(out=scal[:, 8:16],
                               in_=ham8[0:1, :].broadcast_to([128, 8])), "sq", 16)
            inc(sync.dma_start(out=scal[:, 16:32],
                               in_=wam[0:1, :].broadcast_to([128, 16])),
                "sq", 16, "sld")
            for i in range(8):
                inc(sync.dma_start(out=y_sb[:, i * D:(i + 1) * D],
                                   in_=y_in[i * 128:(i + 1) * 128, :]),
                    "yq", 16, f"yld{i}" if i == 7 else None)

        @blk.vector
        def _(v):
            w(v, "sld")
            ss(v, v.reduce_max(scal[:, 32:33], scal[:, 0:8], axis=AX.X), "ve")
            ss(v, v.reduce_max(scal[:, 33:34], scal[:, 8:16], axis=AX.X), "ve")
            ss(v, v.reduce_max(scal[:, 34:35], scal[:, 16:24], axis=AX.X), "ve")
            ss(v, v.reduce_max(scal[:, 35:36], scal[:, 24:32], axis=AX.X), "ve")
            ss(v, v.tensor_scalar(scal[:, 32:33], scal[:, 32:33], 1.0 / 448.0, 1e-6,
                                  ALU.mult, ALU.max), "ve")
            ss(v, v.tensor_scalar(scal[:, 34:35], scal[:, 34:35], 1.0 / 448.0, 1e-6,
                                  ALU.mult, ALU.max), "ve")
            ss(v, v.tensor_scalar(scal[:, 35:36], scal[:, 35:36], 1.0 / 448.0, 1e-6,
                                  ALU.mult, ALU.max), "ve")
            ss(v, v.tensor_tensor(scal[:, 36:37], scal[:, 32:33], scal[:, 34:35],
                                  ALU.mult), "ve")
            ss(v, v.tensor_scalar(scal[:, 36:37], scal[:, 36:37], 4.0, None,
                                  ALU.mult, ALU.bypass), "ve")
            ss(v, v.tensor_tensor(scal[:, 37:38], scal[:, 36:37], scal[:, 33:34],
                                  ALU.mult), "ve")
            ss(v, v.tensor_tensor(scal[:, 37:38], scal[:, 37:38], scal[:, 37:38],
                                  ALU.mult), "ve")
            ss(v, v.tensor_scalar(scal[:, 37:38], scal[:, 37:38], 1.0 / 448.0, 1e-6,
                                  ALU.mult, ALU.max), "ve")
            ss(v, v.tensor_scalar(scal[:, 38:39], scal[:, 37:38], 2.0, None,
                                  ALU.mult, ALU.bypass), "ve")
            inc(v.reciprocal(scal[:, 38:39], scal[:, 38:39]), "ve", 1, "rech")

        @blk.scalar
        def _(s_):
            w(s_, "rech")
            ss(s_, s_.activation(scal[:, 38:39], scal[:, 38:39], AF.Ln), "sc")
            inc(s_.activation(scal[:, 38:39], scal[:, 38:39], AF.Exp, scale=0.5),
                "sc", 1, "sqh")

        @blk.vector
        def _(v):
            w(v, "sqh")
            ss(v, v.tensor_tensor(scal[:, 38:39], scal[:, 38:39], scal[:, 36:37],
                                  ALU.mult), "ve")
            ss(v, v.tensor_tensor(scal[:, 39:40], scal[:, 37:38], scal[:, 35:36],
                                  ALU.mult), "ve")
            inc(v.tensor_scalar(scal[:, 39:40], scal[:, 39:40], 4.0, None,
                                ALU.mult, ALU.bypass), "ve", 1, "deqs")

        _sc = nc.enter_named_scope("l4_h8", False)[0]
        for c in range(64):
            @blk.gpsimd
            def _(g, c=c):
                if c >= 3:
                    w(g, f"h8q{c - 3}")
                inc(g.dma_start(out=hr[c % 3][:],
                                in_=hr_in[c * 128:(c + 1) * 128, :]),
                    f"hq{c % 3}", 16, f"hld{c}")

            @blk.scalar
            def _(s_, c=c):
                w(s_, f"hld{c}")
                if c == 0:
                    w(s_, "deqs")
                inc(s_.activation(h8[:, c * S:(c + 1) * S], hr[c % 3][:],
                                  AF.Square, scale=scal[:, 38:39]),
                    "sc", 1, f"h8q{c}")

        nc.leave_named_scope("l4_h8", _sc, False)
        _sc = nc.enter_named_scope("l4_proj", False)[0]
        for dh in range(2):
            for c in range(64):
                @blk.sync
                def _(sync, dh=dh, c=c):
                    if dh == 1 and c % 16 == 0:
                        w(sync, f"pjp{c // 16 + 1}" if c < 32 else "pjm7")
                    inc(sync.dma_start(out=wpj[:, c * S:(c + 1) * S],
                                       in_=w8pj[c * 128:(c + 1) * 128,
                                                dh * 1024:(dh + 1) * 1024]),
                        f"wq{dh % 2}", 16, f"wld{dh}_{c}" if c == 63 else None)

            for tt in range(8):
                g_ = dh * 8 + tt

                @blk.tensor
                def _(t, dh=dh, tt=tt, g_=g_):
                    if tt == 0:
                        w(t, f"wld{dh}_63")
                    if g_ >= 2:
                        w(t, f"pjv{g_ - 2}")
                    mm = None
                    h8v = h8[:].rearrange("p (c t) -> p c t", t=S)
                    wpv = wpj[:].rearrange("p (c t) -> p c t", t=S)
                    for cc in range(32):
                        if g_ == 0:
                            w(t, f"h8q{2 * cc + 1}")
                        for hf in range(2):
                            mm = t.matmul(
                                psP[g_ % 2][:, hf * 512:(hf + 1) * 512],
                                h8v[:, 2 * cc:2 * cc + 2, tt * 128:(tt + 1) * 128],
                                wpv[:, 2 * cc:2 * cc + 2, hf * 512:(hf + 1) * 512],
                                start=(cc == 0), stop=(cc == 31),
                                perf_mode=mybir.MatmulPerfMode.DoubleRow,
                                skip_group_check=True)
                        if g_ == 7 and cc % 8 == 7 and cc < 31:
                            inc(mm, "pe", 1, f"pjp{cc // 8}")
                    inc(mm, "pe", 1, f"pjm{g_}")

                @blk.vector
                def _(v, dh=dh, tt=tt, g_=g_):
                    w(v, f"pjm{g_}")
                    if g_ == 0:
                        w(v, "yld7")
                    if g_ >= 2:
                        w(v, f"osw{g_ - 2}")
                    inc(v.scalar_tensor_tensor(
                        osb[g_ % 2][:], psP[g_ % 2][:], scal[:, 39:40],
                        y_sb[:, tt * D + dh * 1024:tt * D + (dh + 1) * 1024],
                        ALU.mult, ALU.add), "ve", 1, f"pjv{g_}")

                @blk.sync
                def _(sync, dh=dh, tt=tt, g_=g_):
                    w(sync, f"pjv{g_}")
                    inc(sync.dma_start(out=o_out[tt * 128:(tt + 1) * 128,
                                                 dh * 1024:(dh + 1) * 1024],
                                       in_=osb[g_ % 2][:]),
                        f"st{g_ % 2}", 16, f"osw{g_}")

        @blk.sync
        def _(sync):
            for ch in P.cnt:
                if P.cnt[ch]:
                    sync.wait_ge(P.sems[ch], P.cnt[ch])

        nc.leave_named_scope("l4_proj", _sc, False)

    return nc


def kernel(**inputs):
    x = np.ascontiguousarray(
        np.asarray(inputs["L0_attn_x"], np.float32).reshape(8 * S, D))
    Wq = np.asarray(inputs["L0_attn_Wq"], np.float32)
    Wk = np.asarray(inputs["L0_attn_Wk"], np.float32)
    Wv = np.asarray(inputs["L0_attn_Wv"], np.float32)
    Wo = np.asarray(inputs["L0_attn_Wo"], np.float32)
    wfcT = np.ascontiguousarray(np.asarray(inputs["L0_ffn_w_fc"], np.float32).T)
    wprojT = np.ascontiguousarray(np.asarray(inputs["L0_ffn_w_proj"], np.float32).T)
    wqT = np.ascontiguousarray(Wq.T)
    wkT = np.ascontiguousarray(Wk.T)
    wvT = np.ascontiguousarray(Wv.T)
    woT = np.ascontiguousarray(Wo.T)
    xs = [np.ascontiguousarray(x[c * S:(c + 1) * S]) for c in range(8)]
    wfc_sls = [np.ascontiguousarray(wfcT[:, c * S:(c + 1) * S]) for c in range(8)]
    wpj_sls = [np.ascontiguousarray(wprojT[:, c * 256:(c + 1) * 256])
               for c in range(8)]

    in1 = [{"x": xs[c], "wqT": wqT, "wkT": wkT, "wvT": wvT,
            "wfc_sl": wfc_sls[c], "wproj_sl": wpj_sls[c]} for c in range(8)]
    r1 = run_bass_kernel_spmd(build_L1(), in1, core_ids=list(range(8))).results

    wam = np.concatenate([np.asarray(r1[c]["wamax"]).reshape(2) for c in range(8)])
    wam = np.ascontiguousarray(
        np.stack([wam[0::2], wam[1::2]]).reshape(1, 16).astype(np.float32))
    in2 = []
    for c in range(8):
        grp = range(0, 4) if c < 4 else range(4, 8)
        kt = np.zeros((H, 128), np.float32)
        kb = np.zeros((H, 128), np.float32)
        cvt = np.zeros((128, D), NB)
        cvb = np.zeros((128, D), NB)
        for kk, cc in enumerate(grp):
            cand = np.asarray(r1[cc]["cand"], np.float32)
            kt[:, kk * 32:(kk + 1) * 32] = cand[:, :32]
            kb[:, kk * 32:(kk + 1) * 32] = cand[:, 32:]
            pv = np.asarray(r1[cc]["payV"], NB).reshape(2, 2, 32, 64, 16)
            for sel, dst in ((0, cvt), (1, cvb)):
                arr = pv[:, sel].transpose(1, 2, 0, 3)
                dst[kk * 32:(kk + 1) * 32, :] = arr.reshape(32, D)
        in2.append({"x": xs[c], "q": np.ascontiguousarray(r1[c]["q"]),
                    "kt": kt, "kb": kb, "cvt": cvt, "cvb": cvb, "woT": woT,
                    "wfc_sl": wfc_sls[c], "wproj_sl": wpj_sls[c], "wam": wam})

    r2 = run_bass_kernel_spmd(build_L2(), in2, core_ids=list(range(8))).results

    ys = [np.ascontiguousarray(np.asarray(r2[c]["y"], np.float32))
          for c in range(8)]
    xam8 = np.ascontiguousarray(np.concatenate(
        [np.asarray(r2[c]["xamax"], np.float32).reshape(1) for c in range(8)]
    ).reshape(1, 8))
    w8fc = np.concatenate([np.asarray(r2[c]["w8fc_sl"], NF8) for c in range(8)],
                          axis=1)
    w8pj = np.ascontiguousarray(np.concatenate(
        [np.asarray(r2[c]["w8proj_sl"], NF8) for c in range(8)], axis=1))
    w8st = np.ascontiguousarray(
        w8fc.reshape(16, 128, 64, 128).transpose(2, 1, 0, 3).reshape(
            64 * 128 * 16, 128))

    in3 = [{"y": ys[c], "w8st": w8st, "xam8": xam8} for c in range(8)]
    r3 = run_bass_kernel_spmd(build_L3(), in3, core_ids=list(range(8))).results

    ham8 = np.ascontiguousarray(np.concatenate(
        [np.asarray(r3[c]["ham"], np.float32).reshape(1) for c in range(8)]
    ).reshape(1, 8))

    in4 = [{"y": ys[c],
            "hrelu": np.ascontiguousarray(np.asarray(r3[c]["hrelu"], NB)),
            "w8pj": w8pj, "ham8": ham8, "xam8": xam8, "wam": wam}
           for c in range(8)]
    r4 = run_bass_kernel_spmd(build_L4(), in4, core_ids=list(range(8))).results

    out = np.concatenate([np.asarray(r4[c]["out"], np.float32) for c in range(8)],
                         axis=0)
    return np.ascontiguousarray(out.reshape(2, 4 * S, D))
